# version 20
# baseline (speedup 1.0000x reference)
import contextlib
import numpy as np
import ml_dtypes
import concourse.bass as bass
import concourse.mybir as mybir
from concourse.bass_utils import run_bass_kernel_spmd

F32 = mybir.dt.float32
BF16 = mybir.dt.bfloat16
I32 = mybir.dt.int32
AF = mybir.ActivationFunctionType
ALU = mybir.AluOpType
AXX = mybir.AxisListType.X

D = 1024
S = 2048
NBC = 2
NH = 16
HDIM = 64
FF = 4096
NCH = 8
NTC = 16
EPS_LN = 1e-5
EPS_RMS = 1e-6

FUSED = True


class Sem:
    __slots__ = ("h", "n")

    def __init__(self, stack, nc, name):
        self.h = stack.enter_context(nc.semaphore(name))
        self.n = 0

    def inc(self, instr, k=1):
        instr.then_inc(self.h, k)
        self.n += k
        return self.n


def host_constants():
    cs = np.zeros((128, 1024), np.float32)
    csb = cs.view(ml_dtypes.bfloat16)
    cs[:, 0] = 1.0
    csb[:, 2] = ml_dtypes.bfloat16(1.0)
    cs[0, 4:132] = 1.0
    cs[32, 4:132] = 1.0
    cs[0:32, 132:164] = np.eye(32, dtype=np.float32)
    for h in range(16):
        cs[(h % 2) * 64:(h % 2) * 64 + 64, 164 + h] = -1.0
    cs[:, 180:308] = 1.0
    for c in range(8):
        sel = np.zeros((16, 128), np.float32)
        sel[2 * c, 0:64] = 1.0
        sel[2 * c + 1, 64:128] = 1.0
        csb[0:16, 616 + 128 * c:616 + 128 * (c + 1)] = sel.astype(ml_dtypes.bfloat16)
    psel = np.zeros((128, 64), np.float32)
    for p in range(128):
        psel[p, p % 64] = 1.0
    csb[:, 1640:1704] = psel.astype(ml_dtypes.bfloat16)
    i = (np.arange(128) % 64) // 2
    invf = 10000.0 ** (-(np.arange(0, HDIM, 2) / HDIM))
    ang = np.outer(invf[i], np.arange(S))
    cosT = np.ascontiguousarray(np.cos(ang)).astype(ml_dtypes.bfloat16)
    sgn = np.where(np.arange(128) % 2 == 0, -1.0, 1.0)[:, None]
    sinT = np.ascontiguousarray(np.sin(ang) * sgn).astype(ml_dtypes.bfloat16)
    return cs, cosT, sinT


def build_program(layers):
    layers = tuple(layers)
    nc = bass.Bass("TRN2", target_bir_lowering=False)
    stack = contextlib.ExitStack()
    dram = {}
    input_names = []

    def din(name, shape, dtype):
        dram[name] = nc.dram_tensor(name, shape, dtype, kind="ExternalInput").ap()
        input_names.append(name)
        return dram[name]

    def dout(name, shape, dtype):
        dram[name] = nc.dram_tensor(name, shape, dtype, kind="ExternalOutput").ap()
        return dram[name]

    def dint(name, shape, dtype):
        dram[name] = nc.dram_tensor(name, shape, dtype, kind="Internal").ap()
        return dram[name]

    xin = din("x" if 0 in layers else "x1", [NBC, D, S], F32)
    xout = dout("x1" if layers == (0,) else "out", [NBC, D, S], F32)
    csmall = din("csmall", [128, 1024], F32)
    if 1 in layers:
        din("cosrope", [128, S], BF16)
        din("sinrope", [128, S], BF16)
    if 0 in layers:
        for nm in ("L0_attn_Wq", "L0_attn_Wk", "L0_attn_Wv", "L0_attn_Wo",
                   "L0_ffn_weight"):
            din(nm, [D, D], F32)
        for nm in ("L0_attn_bq", "L0_attn_bk", "L0_attn_bv", "L0_attn_bo",
                   "L0_attn_norm_q_w", "L0_attn_norm_k_w"):
            din(nm, [D], F32)
        din("L0_ffn_disha", [64, D], F32)
    if 1 in layers:
        for nm in ("L1_attn_wq", "L1_attn_wk", "L1_attn_wv"):
            din(nm, [D, D], F32)
        din("L1_ffn_ln_gamma", [D], F32)
        din("L1_ffn_ln_beta", [D], F32)
        din("L1_ffn_fc1_q", [D, FF], I32)
        din("L1_ffn_fc1_scales", [D // 128, FF], F32)
        din("L1_ffn_fc1_bias", [FF], F32)
        din("L1_ffn_fc2_q", [FF, D], I32)
        din("L1_ffn_fc2_scales", [FF // 128, D], F32)
        din("L1_ffn_fc2_bias", [D], F32)

    tw_names = []
    if 0 in layers:
        tw_names += ["L0_attn_Wq", "L0_attn_Wk", "L0_attn_Wv", "L0_attn_Wo",
                     "L0_ffn_weight"]
    if 1 in layers:
        tw_names += ["L1_attn_wq", "L1_attn_wk", "L1_attn_wv"]
    for nm in tw_names:
        dint(nm + "_bf", [D, D], BF16)
    xspill = dint("xspill", [D, S], F32)

    with stack:
        sbuf = lambda name, shape, dtype, **kw: stack.enter_context(
            nc.sbuf_tensor(name, shape, dtype, **kw))[:]
        R1 = sbuf("R1", [128, 16384], F32)
        R2 = sbuf("R2", [128, 16384], F32)
        XNT = sbuf("XNT", [128, 16384], BF16)
        WTA = sbuf("WTA", [128, 8192], BF16, is_transpose_dest=True)
        CS8 = sbuf("CS8", [128, 4096], BF16)
        CSM = sbuf("CSM", [128, 1024], F32)
        SR = sbuf("SR", [128, 4096], F32)
        VECS = sbuf("VECS", [128, 256], F32)
        VECB = sbuf("VECB", [128, 64], BF16)
        STG = SR[0:32, 0:1024]
        SRb = SR.bitcast(BF16)
        SRA = SR[0:1, 0:2048]
        SRB = SR[0:1, 2048:4096]
        OST = SRb[0:64, 0:1024]

        R1b = R1.bitcast(BF16)
        R2b = R2.bitcast(BF16)
        CSMb = CSM.bitcast(BF16)
        CS8i = CS8.bitcast(I32)
        CS8f = CS8.bitcast(F32)

        XR = R1
        VA = R1b[:, 0:16640]
        WTB = R1b[:, 0:8192]
        PT = [R1b[:, 16640:17664], R1b[:, 17664:18688],
              R1b[:, 28928:29952], R1b[:, 29952:30976]]
        DEN = R1[0:16, 9344:11392]
        RDEN = R1b[0:16, 22784:24832]
        QSQ = R1b[:, 24832:26880]
        TMPR = R1b[:, 26880:28928]

        Q16 = R2b[:, 0:16384]
        K16 = R2b[:, 16384:32768]
        XSQ32 = R2[:, 0:2048]
        TMP32 = R2[:, 2048:4096]
        Y8 = R2b
        W2X = R2b[:, 16384:32768]

        W1I = CS8i[:, 0:1024]
        SCB = CS8f[:, 1024:2048]
        SRi = SR.bitcast(I32)
        DQ_INT = [CS8i[:, 0:1024], SRi[:, 0:1024], SRi[:, 2048:3072]]
        DQ_SCL = [CS8f[:, 1024:2048], SR[:, 1024:2048], SR[:, 3072:4096]]
        COSR = CS8[:, 0:2048]
        SINR = CS8[:, 2048:4096]
        POOLPAD = CS8[:, 0:2048]
        DISHA = CS8[:, 2048:3072]

        ONE128 = CSM[:, 0:1]
        ONE128B = CSMb[:, 2:3]
        ONESROW = CSM[0:1, 4:132]
        ONESROW32 = CSM[32:33, 4:132]
        IDENT32 = CSM[0:32, 132:164]
        MASKNEG = CSM[:, 164:180]
        ONESSQ = CSM[:, 180:308]
        SEL16S = CSMb[0:16, 616:1640]
        PSEL = CSMb[:, 1640:1704]

        VC = {}
        _vc = [0]

        def vcol(name, n=1):
            VC[name] = VECS[:, _vc[0]:_vc[0] + n]
            _vc[0] += n

        for nm, n in (("bqT", 8), ("bkT", 8), ("boT", 8), ("wq0125T", 8),
                      ("wknT", 8), ("gammaT", 8), ("betaT", 8), ("hb2T", 8),
                      ("QmaxT", 8), ("KmaxT", 8), ("bo1T", 8), ("b1T", 32),
                      ("negMrhs", 16), ("negMb", 16), ("zeroT", 1), ("oneT", 1),
                      ("c0125T", 1)):
            vcol(nm, n)
        BV16 = VECB[:, 0:8]
        KMAX16 = VECB[:, 8:16]

        PS_A = stack.enter_context(nc.psum_tensor("PS_A", [128, 2048], F32))[:]
        PS_B = stack.enter_context(nc.psum_tensor("PS_B", [128, 2048], F32))[:]

        sems = {nm: Sem(stack, nc, nm)
                for nm in ("dW", "dS", "dX", "dO", "dP", "cT", "cV", "cS", "cP",
                           "dU0", "dU1", "dU2", "dDen", "dOD", "dSB")}
        dW, dS, dX, dO, dP = (sems[k] for k in ("dW", "dS", "dX", "dO", "dP"))
        cT, cV, cS, cP = (sems[k] for k in ("cT", "cV", "cS", "cP"))
        DU = [sems["dU0"], sems["dU1"], sems["dU2"]]
        dDen, dOD, dSB = sems["dDen"], sems["dOD"], sems["dSB"]

        PF = {}

        def xr_c(c):
            return XR[:, 2048 * c:2048 * (c + 1)]

        def xn_c(c):
            return XNT[:, 2048 * c:2048 * (c + 1)]

        def q_c(c):
            return Q16[:, 2048 * c:2048 * (c + 1)]

        def k_c(c):
            return K16[:, 2048 * c:2048 * (c + 1)]

        def load_wt(e, nm, buf=None, sem=None):
            buf = WTA if buf is None else buf
            sem = dS if sem is None else sem
            wbf = dram[nm + "_bf"]
            for c in range(NCH):
                sem.inc(e.dma_start(out=buf[:, 1024 * c:1024 * (c + 1)],
                                    in_=wbf[:, 128 * c:128 * (c + 1)],
                                    transpose=True), 16)
            return sem.n

        with nc.Block() as blk:
            @blk.sync
            def _(e):
                dW.inc(e.dma_start(out=CSM, in_=csmall), 16)

            @blk.gpsimd
            def _(e):
                for nm in tw_names:
                    dP.inc(e.dma_start(out=dram[nm + "_bf"], in_=dram[nm]), 16)
        cast_mark = dP.n

        vec_plan = []
        if 0 in layers:
            vec_plan += [("L0_attn_bq", "bqT", 8), ("L0_attn_bk", "bkT", 8),
                         ("L0_attn_bv", "_bv", 8), ("L0_attn_bo", "boT", 8),
                         ("L0_attn_norm_q_w", "wq0125T", 8),
                         ("L0_attn_norm_k_w", "wknT", 8)]
        if 1 in layers:
            vec_plan += [("L1_ffn_ln_gamma", "gammaT", 8),
                         ("L1_ffn_ln_beta", "betaT", 8),
                         ("L1_ffn_fc2_bias", "hb2T", 8),
                         ("L1_ffn_fc1_bias", "b1T", 32)]

        for g0 in range(0, len(vec_plan), 8):
            grp = vec_plan[g0:g0 + 8]
            with nc.Block() as blk:
                @blk.sync
                def _(e, grp=grp):
                    e.wait_ge(dW.h, 16)
                    for idx, (src, dst, rows) in enumerate(grp):
                        srcap = dram[src].rearrange("(a b) -> a b", b=128)
                        dX.inc(e.dma_start(
                            out=STG[0:rows, 128 * idx:128 * (idx + 1)],
                            in_=srcap), 16)
            ld_mark = dX.n
            with nc.Block() as blk:
                @blk.tensor
                def _(e, grp=grp):
                    e.wait_ge(dX.h, ld_mark)
                    for idx, (src, dst, rows) in enumerate(grp):
                        li = e.transpose(PS_A[:, 64 * idx:64 * idx + rows],
                                         STG[0:rows, 128 * idx:128 * (idx + 1)],
                                         IDENT32[0:rows, 0:rows])
                    cT.inc(li, 1)
            tr_mark = cT.n
            with nc.Block() as blk:
                @blk.vector
                def _(e, grp=grp):
                    e.wait_ge(cT.h, tr_mark)
                    for idx, (src, dst, rows) in enumerate(grp):
                        ps = PS_A[:, 64 * idx:64 * idx + rows]
                        if dst == "_bv":
                            e.tensor_copy(BV16, ps)
                        else:
                            e.tensor_copy(VC[dst], ps)

        with nc.Block() as blk:
            @blk.vector
            def _(e):
                if 0 in layers:
                    e.tensor_scalar_mul(VC["wq0125T"], VC["wq0125T"], 0.125)
                if 1 in layers:
                    e.tensor_scalar_mul(VC["hb2T"], VC["hb2T"], 0.5)
                e.memset(VC["zeroT"], 0.0)
                e.memset(VC["oneT"], 1.0)
                e.memset(VC["c0125T"], 0.125)
                e.memset(VC["bo1T"], 0.0)

        def phase_p1(b, layer, load_x):
            with nc.Block() as blk:
                dX0, cS0, cT0, cV0 = dX.n, cS.n, cT.n, cV.n
                if load_x:
                    @blk.sync
                    def _(e):
                        dX.inc(e.dma_start(
                            out=XR.rearrange("p (c t) -> p c t", t=S),
                            in_=xin[b].rearrange("(c p) t -> p c t", p=128)), 16)
                xload = dX.n

                XB = [R2b[:, 8192:10240], R2b[:, 10240:12288]]
                XSQ16 = [R2b[:, 12288:14336], R2b[:, 14336:16384]]

                @blk.scalar
                def _(e):
                    if load_x:
                        e.wait_ge(dX.h, xload)
                    for c in range(NCH):
                        if c >= 2:
                            e.wait_ge(cT.h, cT0 + 2 * (c - 2) + 2)
                        with nc.allow_low_precision(
                                reason="LN stats operands in bf16; fp32 "
                                       "accumulation (error ~1e-4 of sigma)"):
                            cS.inc(e.activation(XSQ16[c % 2], xr_c(c),
                                                AF.Square), 1)
                assert cS.n == cS0 + NCH

                @blk.vector
                def _(e):
                    if load_x:
                        e.wait_ge(dX.h, xload)
                    for c in range(NCH):
                        if c >= 2:
                            e.wait_ge(cT.h, cT0 + 2 * (c - 2) + 1)
                        with nc.allow_low_precision(
                                reason="LN stats operands in bf16"):
                            cV.inc(e.tensor_copy(XB[c % 2], xr_c(c)), 1)
                assert cV.n == cV0 + NCH

                @blk.tensor
                def _(e):
                    for c in range(NCH):
                        e.wait_ge(cV.h, cV0 + c + 1)
                        for j in range(4):
                            li = e.matmul(PS_A[0:1, 512 * j:512 * (j + 1)], ONE128B,
                                          XB[c % 2][:, 512 * j:512 * (j + 1)],
                                          start=(c == 0), stop=(c == NCH - 1))
                        cT.inc(li, 1)
                        e.wait_ge(cS.h, cS0 + c + 1)
                        for j in range(4):
                            li = e.matmul(PS_B[0:1, 512 * j:512 * (j + 1)], ONE128B,
                                          XSQ16[c % 2][:, 512 * j:512 * (j + 1)],
                                          start=(c == 0), stop=(c == NCH - 1))
                        cT.inc(li, 1)
                assert cT.n == cT0 + 2 * NCH

            with nc.Block() as blk:
                cV0, cS0, cT0 = cV.n, cS.n, cT.n

                @blk.vector
                def _(e):
                    T = PS_A[0:1, 0:2048]
                    e.tensor_copy(SRA, PS_A[0:1, :])
                    e.tensor_copy(SRB, PS_B[0:1, :])
                    e.tensor_scalar_mul(SRA, SRA, 1.0 / D)
                    e.tensor_scalar(SRB, SRB, 1.0 / D, EPS_LN,
                                    ALU.mult, ALU.add)
                    e.tensor_tensor(T, SRA, SRA, ALU.mult)
                    e.tensor_tensor(SRB, SRB, T, ALU.subtract)
                    cV.inc(e.reciprocal(T, SRB), 1)
                    e.wait_ge(cS.h, cS0 + 1)
                    cV.inc(e.tensor_tensor(SRA, SRA, SRB, ALU.mult), 1)
                    e.wait_ge(cT.h, cT0 + 1)
                    for c in range(NCH):
                        e.tensor_tensor(TMP32, xr_c(c), PS_A, ALU.mult)
                        li = e.tensor_tensor(xn_c(c), TMP32, PS_B, ALU.subtract)
                        if layer == 1:
                            li = e.tensor_tensor(xr_c(c), xr_c(c), xn_c(c), ALU.add)
                        if c == 1:
                            cV.inc(li, 1)
                    cV.inc(li, 1)

                @blk.scalar
                def _(e):
                    e.wait_ge(cV.h, cV0 + 1)
                    cS.inc(e.activation(SRB, PS_A[0:1, 0:2048], AF.Sqrt), 1)

                @blk.tensor
                def _(e):
                    e.wait_ge(cV.h, cV0 + 2)
                    for j in range(4):
                        e.matmul(PS_A[:, 512 * j:512 * (j + 1)], ONESROW,
                                 SRB[:, 512 * j:512 * (j + 1)],
                                 start=True, stop=True)
                    for j in range(4):
                        li = e.matmul(PS_B[:, 512 * j:512 * (j + 1)], ONESROW,
                                      SRA[:, 512 * j:512 * (j + 1)],
                                      start=True, stop=True)
                    cT.inc(li, 1)

                @blk.sync
                def _(e):
                    e.wait_ge(dP.h, cast_mark)
                    if layer == 1:
                        dS.inc(e.dma_start(out=COSR, in_=dram["cosrope"]), 16)
                        dS.inc(e.dma_start(out=SINR, in_=dram["sinrope"]), 16)
                    wq = "L0_attn_Wq" if layer == 0 else "L1_attn_wq"
                    PF["wq"] = load_wt(e, wq)
                    e.wait_ge(cV.h, cV0 + 4)
                    dX.inc(e.dma_start(
                        out=xspill.rearrange("(c p) t -> p c t", p=128),
                        in_=XR.rearrange("p (c t) -> p c t", t=S)), 16)
                    PF["spill"] = dX.n
                    PF["spill01"] = dSB.n
                    wk = "L0_attn_Wk" if layer == 0 else "L1_attn_wk"
                    e.wait_ge(dX.h, PF["spill"])
                    PF["wk"] = load_wt(e, wk, WTB, dSB)
            return PF["spill"]

        def emit_feature_proj(blk, wname, dst_c, evac_kind, evac_args,
                              preloaded=None, wbuf=None, wsem=None,
                              extra_load=None):
            wbuf = WTA if wbuf is None else wbuf
            wsem = dS if wsem is None else wsem
            with blk:
                box = {}
                if preloaded is None or extra_load is not None:
                    @blk.sync
                    def _(e):
                        if preloaded is None:
                            e.wait_ge(dP.h, cast_mark)
                            box["m"] = load_wt(e, wname, wbuf, wsem)
                        if extra_load is not None:
                            key, nm2, buf2, sem2 = extra_load
                            PF[key] = load_wt(e, nm2, buf2, sem2)
                wt_mark = box.get("m", preloaded)
                cT0, cS0 = cT.n, cS.n

                @blk.tensor
                def _(e):
                    e.wait_ge(wsem.h, wt_mark)
                    for oc in range(NCH):
                        if oc >= 2:
                            e.wait_ge(cS.h, cS0 + oc - 1)
                        ps = PS_A if oc % 2 == 0 else PS_B
                        for dc in range(NCH):
                            lhsT = wbuf[:, 1024 * dc + 128 * oc:
                                        1024 * dc + 128 * (oc + 1)]
                            for j in range(4):
                                li = e.matmul(ps[:, 512 * j:512 * (j + 1)], lhsT,
                                              xn_c(dc)[:, 512 * j:512 * (j + 1)],
                                              start=(dc == 0), stop=(dc == NCH - 1))
                        cT.inc(li, 1)
                assert cT.n == cT0 + NCH

                @blk.scalar
                def _(e):
                    for oc in range(NCH):
                        e.wait_ge(cT.h, cT0 + oc + 1)
                        ps = PS_A if oc % 2 == 0 else PS_B
                        if evac_kind == "bias":
                            cS.inc(e.activation(dst_c(oc), ps, AF.Identity,
                                                bias=evac_args[:, oc:oc + 1],
                                                scale=VC["oneT"]), 1)
                        elif evac_kind == "scale0125":
                            cS.inc(e.activation(dst_c(oc), ps, AF.Identity,
                                                bias=VC["zeroT"],
                                                scale=VC["c0125T"]), 1)
                        else:
                            cS.inc(e.activation(dst_c(oc), ps, AF.Copy), 1)
                assert cS.n == cS0 + NCH

        def phase_p2(b, layer, spill_mark):
            if layer == 0:
                wq, wk, wv = "L0_attn_Wq", "L0_attn_Wk", "L0_attn_Wv"
            else:
                wq, wk, wv = "L1_attn_wq", "L1_attn_wk", "L1_attn_wv"

            emit_feature_proj(nc.Block(), wq, q_c,
                              "bias" if layer == 0 else "scale0125",
                              VC["bqT"] if layer == 0 else None,
                              preloaded=PF.get("wq"))
            emit_feature_proj(nc.Block(), wk, k_c,
                              "bias" if layer == 0 else "copy",
                              VC["bkT"] if layer == 0 else None,
                              preloaded=PF.get("wk"), wbuf=WTB, wsem=dSB,
                              extra_load=("wv", wv, WTA, dS))

            with nc.Block() as blk:
                wt_mark = PF["wv"]
                cT0, cS0 = cT.n, cS.n

                @blk.gpsimd
                def _(e):
                    e.wait_ge(dX.h, spill_mark)
                    e.wait_ge(dSB.h, PF["spill01"])
                    cP.inc(e.memset(VA[:, 64::65], 1.0), 1)

                @blk.tensor
                def _(e):
                    e.wait_ge(dS.h, wt_mark)
                    for tc in range(NTC):
                        if tc >= 2:
                            e.wait_ge(cS.h, cS0 + tc - 1)
                        ps = PS_A if tc % 2 == 0 else PS_B
                        for dc in range(NCH):
                            lhsT = XNT[:, 2048 * dc + 128 * tc:
                                       2048 * dc + 128 * (tc + 1)]
                            for j in range(2):
                                li = e.matmul(ps[:, 512 * j:512 * (j + 1)], lhsT,
                                              WTA[:, 1024 * dc + 512 * j:
                                                  1024 * dc + 512 * (j + 1)],
                                              start=(dc == 0), stop=(dc == NCH - 1))
                        cT.inc(li, 1)

                @blk.scalar
                def _(e):
                    e.wait_ge(dX.h, spill_mark)
                    e.wait_ge(dSB.h, PF["spill01"])
                    for tc in range(NTC):
                        e.wait_ge(cT.h, cT0 + tc + 1)
                        ps = PS_A if tc % 2 == 0 else PS_B
                        va_t = bass.AP(VA.tensor, VA.offset + 1040 * tc,
                                       [list(VA.ap[0]), [65, 16], [1, 64]])
                        cS.inc(e.activation(
                            va_t, ps[:, 0:1024].rearrange("p (h j) -> p h j", j=64),
                            AF.Copy), 1)

            if layer == 0:
                with nc.Block() as blk:
                    cS0, cT0 = cS.n, cT.n

                    @blk.scalar
                    def _(e):
                        for c in range(NCH):
                            if c >= 1:
                                e.wait_ge(cT.h, cT0 + 2 * c - 1)
                            cS.inc(e.activation(QSQ, q_c(c), AF.Square), 1)
                            if c >= 1:
                                e.wait_ge(cT.h, cT0 + 2 * c)
                            cS.inc(e.activation(TMPR, k_c(c), AF.Square), 1)

                    @blk.tensor
                    def _(e):
                        for c in range(NCH):
                            e.wait_ge(cS.h, cS0 + 2 * c + 1)
                            for j in range(4):
                                li = e.matmul(PS_A[0:1, 512 * j:512 * (j + 1)],
                                              ONE128B, QSQ[:, 512 * j:512 * (j + 1)],
                                              start=(c == 0), stop=(c == NCH - 1))
                            cT.inc(li, 1)
                            e.wait_ge(cS.h, cS0 + 2 * c + 2)
                            for j in range(4):
                                li = e.matmul(PS_B[0:1, 512 * j:512 * (j + 1)],
                                              ONE128B, TMPR[:, 512 * j:512 * (j + 1)],
                                              start=(c == 0), stop=(c == NCH - 1))
                            cT.inc(li, 1)

                with nc.Block() as blk:
                    cV0, cS0, cT0 = cV.n, cS.n, cT.n

                    @blk.vector
                    def _(e):
                        T = PS_A[0:1, 0:2048]
                        T2 = PS_B[0:1, 0:2048]
                        e.tensor_copy(SRA, PS_A[0:1, :])
                        e.tensor_copy(SRB, PS_B[0:1, :])
                        e.tensor_scalar(SRA, SRA, 1.0 / D, EPS_RMS,
                                        ALU.mult, ALU.add)
                        e.tensor_scalar(SRB, SRB, 1.0 / D, EPS_RMS,
                                        ALU.mult, ALU.add)
                        e.reciprocal(T, SRA)
                        cV.inc(e.reciprocal(T2, SRB), 1)
                        e.wait_ge(cT.h, cT0 + 1)
                        for c in range(NCH):
                            e.scalar_tensor_tensor(
                                q_c(c), q_c(c), VC["wq0125T"][:, c:c + 1], PS_A,
                                ALU.mult, ALU.mult)
                            e.scalar_tensor_tensor(
                                k_c(c), k_c(c), VC["wknT"][:, c:c + 1], PS_B,
                                ALU.mult, ALU.mult)

                    @blk.scalar
                    def _(e):
                        e.wait_ge(cV.h, cV0 + 1)
                        e.activation(SRA, PS_A[0:1, 0:2048], AF.Sqrt)
                        cS.inc(e.activation(SRB, PS_B[0:1, 0:2048],
                                            AF.Sqrt), 1)

                    @blk.tensor
                    def _(e):
                        e.wait_ge(cS.h, cS0 + 1)
                        for j in range(4):
                            e.matmul(PS_A[:, 512 * j:512 * (j + 1)], ONESROW,
                                     SRA[:, 512 * j:512 * (j + 1)],
                                     start=True, stop=True)
                        for j in range(4):
                            li = e.matmul(PS_B[:, 512 * j:512 * (j + 1)], ONESROW,
                                          SRB[:, 512 * j:512 * (j + 1)],
                                          start=True, stop=True)
                        cT.inc(li, 1)
            else:
                with nc.Block() as blk:
                    @blk.vector
                    def _(e):
                        e.wait_ge(dS.h, dS.n)
                        mask = []
                        for i in range(16):
                            mask += [2 * i + 1, 2 * i]
                        for tgt in (q_c, k_c):
                            for c in range(NCH):
                                e.stream_shuffle(TMPR, tgt(c), mask)
                                e.tensor_tensor(TMPR, TMPR, SINR, ALU.mult)
                                e.tensor_tensor(tgt(c), tgt(c), COSR, ALU.mult)
                                e.tensor_tensor(tgt(c), tgt(c), TMPR, ALU.add)

            MROW = SR[0:1, 4080:4096]
            KMCOL = SR[0:16, 4079:4080]
            KMAXD = WTA[:, 0:128]
            with nc.Block() as blk:
                cV0, cT0, cS0 = cV.n, cT.n, cS.n
                dDen0 = dDen.n

                @blk.vector
                def _(e):
                    with nc.allow_low_precision(reason="softmax bound only"):
                        e.memset(KMAXD, 0.0)
                        for c in range(NCH):
                            e.reduce_max(KMAX16[:, c:c + 1], k_c(c), axis=AXX,
                                         apply_absolute_value=True)
                            h = 2 * c
                            e.tensor_copy(
                                KMAXD[0:64, 16 * c + h:16 * c + h + 1],
                                KMAX16[0:64, c:c + 1])
                            cV.inc(e.tensor_copy(
                                KMAXD[64:128, 16 * c + h + 1:16 * c + h + 2],
                                KMAX16[64:128, c:c + 1]), 1)
                        e.wait_ge(cT.h, cT0 + NCH)
                        cV.inc(e.reduce_max(KMCOL, PS_B[0:16, :], axis=AXX), 1)
                        e.wait_ge(cT.h, cT0 + NCH + 1)
                        e.tensor_scalar_mul(VC["negMb"], PS_A[:, 0:16], -1.0)

                @blk.scalar
                def _(e):
                    for c in range(NCH):
                        if c >= 2:
                            e.wait_ge(cT.h, cT0 + c - 1)
                        buf = QSQ if c % 2 == 0 else TMPR
                        cS.inc(e.activation(buf, q_c(c), AF.Abs), 1)

                @blk.gpsimd
                def _(e):
                    e.wait_ge(cV.h, cV0 + NCH + 1)
                    dDen.inc(e.dma_start(out=MROW, in_=KMCOL), 16)

                @blk.tensor
                def _(e):
                    for c in range(NCH):
                        e.wait_ge(cS.h, cS0 + c + 1)
                        e.wait_ge(cV.h, cV0 + c + 1)
                        buf = QSQ if c % 2 == 0 else TMPR
                        for j in range(4):
                            li = e.matmul(PS_B[0:16, 512 * j:512 * (j + 1)],
                                          KMAXD[:, 16 * c:16 * (c + 1)],
                                          buf[:, 512 * j:512 * (j + 1)],
                                          start=(c == 0), stop=(c == NCH - 1))
                        cT.inc(li, 1)
                    e.wait_ge(dDen.h, dDen0 + 16)
                    cT.inc(e.matmul(PS_A[:, 0:16], ONESROW, MROW,
                                    start=True, stop=True), 1)

        def phase_p25(b):
            with nc.Block() as blk:
                cS0, cT0 = cS.n, cT.n

                @blk.scalar
                def _(e):
                    for c in range(NCH):
                        if c >= 1:
                            e.wait_ge(cT.h, cT0 + 2 * c)
                        cS.inc(e.activation(QSQ, xn_c(c), AF.Square), 1)

                @blk.tensor
                def _(e):
                    for c in range(NCH):
                        for j in range(4):
                            li = e.matmul(PS_A[0:1, 512 * j:512 * (j + 1)], ONE128B,
                                          xn_c(c)[:, 512 * j:512 * (j + 1)],
                                          start=(c == 0), stop=(c == NCH - 1))
                        cT.inc(li, 1)
                        e.wait_ge(cS.h, cS0 + c + 1)
                        for j in range(4):
                            li = e.matmul(PS_B[0:1, 512 * j:512 * (j + 1)], ONE128B,
                                          QSQ[:, 512 * j:512 * (j + 1)],
                                          start=(c == 0), stop=(c == NCH - 1))
                        cT.inc(li, 1)

            with nc.Block() as blk:
                cV0, cT0, cS0 = cV.n, cT.n, cS.n

                @blk.vector
                def _(e):
                    T = PS_A[0:1, 0:2048]
                    e.tensor_copy(SRA, PS_A[0:1, :])
                    e.tensor_copy(SRB, PS_B[0:1, :])
                    e.tensor_scalar_mul(SRA, SRA, 1.0 / D)
                    e.tensor_scalar(SRB, SRB, 1.0 / D, EPS_LN,
                                    ALU.mult, ALU.add)
                    e.tensor_tensor(T, SRA, SRA, ALU.mult)
                    e.tensor_tensor(SRB, SRB, T, ALU.subtract)
                    cV.inc(e.reciprocal(T, SRB), 1)
                    e.wait_ge(cS.h, cS0 + 1)
                    cV.inc(e.tensor_tensor(SRA, SRA, SRB, ALU.mult), 1)
                    e.wait_ge(cT.h, cT0 + 1)
                    for c in range(NCH):
                        if c >= 1:
                            e.wait_ge(cS.h, cS0 + 1 + c)
                        e.tensor_tensor(TMPR, xn_c(c), PS_A, ALU.mult)
                        cV.inc(e.tensor_tensor(TMPR, TMPR, PS_B, ALU.subtract), 1)

                @blk.scalar
                def _(e):
                    e.wait_ge(cV.h, cV0 + 1)
                    cS.inc(e.activation(SRB, PS_A[0:1, 0:2048], AF.Sqrt), 1)
                    for c in range(NCH):
                        e.wait_ge(cV.h, cV0 + 3 + c)
                        cS.inc(e.activation(xn_c(c), TMPR, AF.Identity,
                                            bias=VC["betaT"][:, c:c + 1],
                                            scale=VC["gammaT"][:, c:c + 1]), 1)

                @blk.tensor
                def _(e):
                    e.wait_ge(cV.h, cV0 + 2)
                    for j in range(4):
                        e.matmul(PS_A[:, 512 * j:512 * (j + 1)], ONESROW,
                                 SRB[:, 512 * j:512 * (j + 1)],
                                 start=True, stop=True)
                    for j in range(4):
                        li = e.matmul(PS_B[:, 512 * j:512 * (j + 1)], ONESROW,
                                      SRA[:, 512 * j:512 * (j + 1)],
                                      start=True, stop=True)
                    cT.inc(li, 1)

        def phase_p3(b, layer):
            NU = NH * NTC
            STA, STB = PS_A[:, 0:1024], PS_A[:, 1024:2048]
            O_A, O_B = PS_B[0:65, 0:1024], PS_B[0:65, 1024:2048]
            PTA, PTB = [PT[0], PT[1]], [PT[2], PT[3]]

            with nc.Block() as blk:
                if layer == 0:
                    @blk.sync
                    def _(e):
                        PF["ffnw"] = load_wt(e, "L0_ffn_weight")
                else:
                    PF["du"] = {}

                    @blk.sync
                    def _(e):
                        q1 = dram["L1_ffn_fc1_q"]
                        s1 = dram["L1_ffn_fc1_scales"]
                        src = bass.AP(q1.tensor, q1.offset,
                                      [[4096, 128], [4096 * 128, 8], [1, 128]])
                        DU[0].inc(e.dma_start(out=DQ_INT[0], in_=src), 16)
                        ssrc = bass.AP(s1.tensor, s1.offset,
                                       [[0, 128], [4096, 8], [1, 128]])
                        DU[0].inc(e.dma_start(out=DQ_SCL[0], in_=ssrc), 16)
                        PF["du"][0] = DU[0].n
                        src = bass.AP(q1.tensor, q1.offset + 128,
                                      [[4096, 128], [4096 * 128, 8], [1, 128]])
                        DU[2].inc(e.dma_start(out=DQ_INT[2], in_=src), 16)
                        ssrc = bass.AP(s1.tensor, s1.offset + 128,
                                       [[0, 128], [4096, 8], [1, 128]])
                        DU[2].inc(e.dma_start(out=DQ_SCL[2], in_=ssrc), 16)
                        PF["du"][2] = DU[2].n
                cT0, cS0, cV0 = cT.n, cS.n, cV.n
                dDen0, dOD0 = dDen.n, dOD.n
                stA_mark = {}
                stB_mark = {}
                avA_mark = {}
                avB_mark = {}

                def uhc(u):
                    return 2 * (u // 32), (u // 16) % 2, u % 16

                def expA_mark(u):
                    return cS0 + 2 * u + 1

                def expB_mark(u):
                    return cS0 + 2 * u + 2

                def denA_mark(m):
                    return cV0 + 4 * m + 1

                def evacA_mark(m):
                    return cV0 + 4 * m + 2

                def denB_mark(m):
                    return cV0 + 4 * m + 3

                def evacB_mark(m):
                    return cV0 + 4 * m + 4

                def denAdma_mark(m):
                    return dDen0 + 32 * m + 16

                def denBdma_mark(m):
                    return dDen0 + 32 * m + 32

                def odma_mark(m):
                    return dOD0 + 16 * (m + 1)

                @blk.tensor
                def _(e):
                    def emit_st(u):
                        hA, H, c = uhc(u)
                        ch = 2048 * (u // 32)
                        if u >= 1:
                            e.wait_ge(cS.h, expA_mark(u - 1))
                        lhsT = K16[0:64, ch + 128 * c:ch + 128 * (c + 1)]
                        rhs = Q16[0:64, ch + 1024 * H:ch + 1024 * (H + 1)]
                        e.matmul(STA[:, 0:512], lhsT, rhs[:, 0:512],
                                 start=True, stop=True)
                        li = e.matmul(STA[:, 512:1024], lhsT, rhs[:, 512:1024],
                                      start=True, stop=True)
                        cT.inc(li, 1)
                        stA_mark[u] = cT.n
                        if u >= 1:
                            e.wait_ge(cS.h, expB_mark(u - 1))
                        lhsT = K16[64:128, ch + 128 * c:ch + 128 * (c + 1)]
                        rhs = Q16[64:128, ch + 1024 * H:ch + 1024 * (H + 1)]
                        e.matmul(STB[:, 0:512], lhsT, rhs[:, 0:512],
                                 start=True, stop=True)
                        li = e.matmul(STB[:, 512:1024], lhsT, rhs[:, 512:1024],
                                      start=True, stop=True)
                        cT.inc(li, 1)
                        stB_mark[u] = cT.n

                    def emit_av(u):
                        hA, H, c = uhc(u)
                        m = u // 16
                        e.wait_ge(cS.h, expA_mark(u))
                        if c == 0 and m >= 1:
                            e.wait_ge(cV.h, evacA_mark(m - 1))
                        lhsT = VA[:, 1040 * c + 65 * hA:1040 * c + 65 * (hA + 1)]
                        rhs = PTA[u % 2]
                        e.matmul(O_A[:, 0:512], lhsT, rhs[:, 0:512],
                                 start=(c == 0), stop=(c == 15))
                        li = e.matmul(O_A[:, 512:1024], lhsT, rhs[:, 512:1024],
                                      start=(c == 0), stop=(c == 15))
                        cT.inc(li, 1)
                        avA_mark[u] = cT.n
                        e.wait_ge(cS.h, expB_mark(u))
                        if c == 0 and m >= 1:
                            e.wait_ge(cV.h, evacB_mark(m - 1))
                        lhsT = VA[:, 1040 * c + 65 * (hA + 1):1040 * c + 65 * (hA + 2)]
                        rhs = PTB[u % 2]
                        e.matmul(O_B[:, 0:512], lhsT, rhs[:, 0:512],
                                 start=(c == 0), stop=(c == 15))
                        li = e.matmul(O_B[:, 512:1024], lhsT, rhs[:, 512:1024],
                                      start=(c == 0), stop=(c == 15))
                        cT.inc(li, 1)
                        avB_mark[u] = cT.n

                    emit_st(0)
                    for u in range(NU):
                        if u + 1 < NU:
                            emit_st(u + 1)
                        emit_av(u)
                assert cT.n == cT0 + 4 * NU

                @blk.scalar
                def _(e):
                    for u in range(NU):
                        hA, H, c = uhc(u)
                        e.wait_ge(cT.h, stA_mark[u])
                        if u >= 2:
                            e.wait_ge(cT.h, avA_mark[u - 2])
                        cS.inc(e.activation(PTA[u % 2], STA, AF.Exp,
                                            bias=VC["negMb"][:, hA:hA + 1],
                                            scale=1.0), 1)
                        e.wait_ge(cT.h, stB_mark[u])
                        if u >= 2:
                            e.wait_ge(cT.h, avB_mark[u - 2])
                        cS.inc(e.activation(PTB[u % 2], STB, AF.Exp,
                                            bias=VC["negMb"][:, hA + 1:hA + 2],
                                            scale=1.0), 1)
                assert cS.n == cS0 + 2 * NU

                @blk.vector
                def _(e):
                    prev_m = None
                    for m in range(NU // 16):
                        u = m * 16 + 15
                        hA, H, c = uhc(u)
                        ch = 2048 * (u // 32)
                        e.wait_ge(cT.h, avA_mark[u])
                        if m >= 1:
                            e.wait_ge(dDen.h, denAdma_mark(m - 1))
                        cV.inc(e.tensor_copy(SR[64:65, 0:1024],
                                             O_A[64:65, :]), 1)
                        assert cV.n == denA_mark(m)
                        dstA = Q16[0:64, ch + 1024 * H:ch + 1024 * (H + 1)]
                        cV.inc(e.tensor_copy(dstA, O_A[0:64, :]), 1)
                        assert cV.n == evacA_mark(m)
                        e.wait_ge(cT.h, avB_mark[u])
                        if m >= 1:
                            e.wait_ge(dDen.h, denBdma_mark(m - 1))
                        cV.inc(e.tensor_copy(SR[64:65, 1024:2048],
                                             O_B[64:65, :]), 1)
                        assert cV.n == denB_mark(m)
                        if prev_m is not None:
                            e.wait_ge(dOD.h, odma_mark(prev_m))
                        prev_m = m
                        cV.inc(e.tensor_copy(OST, O_B[0:64, :]), 1)
                        assert cV.n == evacB_mark(m)
                assert cV.n == cV0 + 4 * (NU // 16)

                @blk.gpsimd
                def _(e):
                    for m in range(NU // 16):
                        u = m * 16 + 15
                        hA = 2 * (u // 32)
                        H = (u // 16) % 2
                        ch = 2048 * (u // 32)
                        e.wait_ge(cV.h, denA_mark(m))
                        dDen.inc(e.dma_start(
                            out=DEN[hA:hA + 1, 1024 * H:1024 * (H + 1)],
                            in_=SR[64:65, 0:1024]), 16)
                        assert dDen.n == denAdma_mark(m)
                        e.wait_ge(cV.h, denB_mark(m))
                        dDen.inc(e.dma_start(
                            out=DEN[hA + 1:hA + 2, 1024 * H:1024 * (H + 1)],
                            in_=SR[64:65, 1024:2048]), 16)
                        assert dDen.n == denBdma_mark(m)
                        e.wait_ge(cV.h, evacB_mark(m))
                        dOD.inc(e.dma_start(
                            out=Q16[64:128, ch + 1024 * H:ch + 1024 * (H + 1)],
                            in_=OST), 16)
                        assert dOD.n == odma_mark(m)

        def phase_p35(b, layer):
            with nc.Block() as blk:
                cV0, cT0 = cV.n, cT.n
                reload_mark = dX.n + 16

                @blk.vector
                def _(e):
                    e.wait_ge(dDen.h, dDen.n)
                    e.wait_ge(dOD.h, dOD.n)
                    with nc.allow_low_precision(
                            reason="bf16 softmax denominators; matmul operands "
                                   "are bf16 throughout"):
                        cV.inc(e.reciprocal(RDEN, DEN), 1)
                    for c in range(NCH):
                        e.wait_ge(cT.h, cT0 + c + 1)
                        ps = PS_A if c % 2 == 0 else PS_B
                        cV.inc(e.tensor_tensor(q_c(c), q_c(c), ps, ALU.mult), 1)
                    if layer == 1:
                        e.wait_ge(dX.h, reload_mark)
                        for c in range(NCH):
                            e.scalar_tensor_tensor(
                                xr_c(c), q_c(c), VC["hb2T"][:, c:c + 1], xr_c(c),
                                ALU.add, ALU.add)

                @blk.tensor
                def _(e):
                    e.wait_ge(cV.h, cV0 + 1)
                    for c in range(NCH):
                        if c >= 2:
                            e.wait_ge(cV.h, cV0 + c)
                        ps = PS_A if c % 2 == 0 else PS_B
                        for j in range(4):
                            li = e.matmul(ps[:, 512 * j:512 * (j + 1)],
                                          SEL16S[:, 128 * c:128 * (c + 1)],
                                          RDEN[:, 512 * j:512 * (j + 1)],
                                          start=True, stop=True)
                        cT.inc(li, 1)

                @blk.sync
                def _(e):
                    e.wait_ge(cV.h, cV0 + 1 + NCH)
                    dX.inc(e.dma_start(
                        out=XR.rearrange("p (c t) -> p c t", t=S),
                        in_=xspill.rearrange("(c p) t -> p c t", p=128)), 16)
                    assert dX.n == reload_mark

                @blk.gpsimd
                def _(e):
                    e.wait_ge(dX.h, dX.n)

        def phase_p4_l0(b):
            with nc.Block() as blk:
                cT0, cP0, dP0 = cT.n, cP.n, dP.n

                @blk.gpsimd
                def _(e):
                    cP.inc(e.memset(POOLPAD, 0.0), 1)
                    cP.inc(e.memset(DISHA, 0.0), 1)
                    dP.inc(e.dma_start(out=DISHA[0:64, :],
                                       in_=dram["L0_ffn_disha"]), 16)

                @blk.tensor
                def _(e):
                    for dc in range(NCH):
                        for j in range(4):
                            li = e.matmul(PS_A[0:64, 512 * j:512 * (j + 1)], PSEL,
                                          xn_c(dc)[:, 512 * j:512 * (j + 1)],
                                          start=(dc == 0), stop=(dc == NCH - 1))
                    cT.inc(li, 1)

                @blk.scalar
                def _(e):
                    e.wait_ge(cT.h, cT0 + 1)
                    e.wait_ge(cP.h, cP0 + 1)
                    cS.inc(e.activation(POOLPAD[0:64, :], PS_A[0:64, :],
                                        AF.Copy), 1)

            with nc.Block() as blk:
                cT0, cV0 = cT.n, cV.n

                @blk.tensor
                def _(e):
                    e.wait_ge(dS.h, dS.n)
                    e.wait_ge(dP.h, dP.n)
                    for oc in range(NCH):
                        if oc >= 2:
                            e.wait_ge(cV.h, cV0 + oc - 1)
                        ps = PS_A if oc % 2 == 0 else PS_B
                        for dc in range(NCH):
                            lhsT = WTA[:, 1024 * dc + 128 * oc:
                                       1024 * dc + 128 * (oc + 1)]
                            for j in range(4):
                                e.matmul(ps[:, 512 * j:512 * (j + 1)], lhsT,
                                         xn_c(dc)[:, 512 * j:512 * (j + 1)],
                                         start=(dc == 0), stop=False)
                        lhsT = DISHA[:, 128 * oc:128 * (oc + 1)]
                        for j in range(4):
                            li = e.matmul(ps[:, 512 * j:512 * (j + 1)], lhsT,
                                          POOLPAD[:, 512 * j:512 * (j + 1)],
                                          start=False, stop=True)
                        cT.inc(li, 1)

                @blk.vector
                def _(e):
                    for oc in range(NCH):
                        e.wait_ge(cT.h, cT0 + oc + 1)
                        ps = PS_A if oc % 2 == 0 else PS_B
                        cV.inc(e.tensor_tensor(xr_c(oc), xr_c(oc), ps,
                                               ALU.add), 1)

            with nc.Block() as blk:
                @blk.sync
                def _(e):
                    load_wt(e, "L0_attn_Wo")
                wt_mark = dS.n
                cT0 = cT.n

                @blk.tensor
                def _(e):
                    e.wait_ge(dS.h, wt_mark)
                    if b == 0:
                        for oc in range(NCH):
                            for dc in range(NCH):
                                lhsT = WTA[:, 1024 * dc + 128 * oc:
                                           1024 * dc + 128 * (oc + 1)]
                                li = e.matmul(PS_A[:, 64 * oc:64 * oc + 1], lhsT,
                                              BV16[:, dc:dc + 1],
                                              start=(dc == 0), stop=(dc == NCH - 1))
                        cT.inc(li, 1)

                @blk.vector
                def _(e):
                    if b == 0:
                        e.wait_ge(cT.h, cT0 + 1)
                        ps = bass.AP(PS_A.tensor, PS_A.offset,
                                     [list(PS_A.ap[0]), [64, 8]])
                        e.tensor_tensor(VC["bo1T"], VC["boT"], ps, ALU.add)

            with nc.Block() as blk:
                cT0, cV0 = cT.n, cV.n

                @blk.tensor
                def _(e):
                    for oc in range(NCH):
                        if oc >= 2:
                            e.wait_ge(cV.h, cV0 + oc - 1)
                        ps = PS_A if oc % 2 == 0 else PS_B
                        for dc in range(NCH):
                            lhsT = WTA[:, 1024 * dc + 128 * oc:
                                       1024 * dc + 128 * (oc + 1)]
                            for j in range(4):
                                li = e.matmul(ps[:, 512 * j:512 * (j + 1)], lhsT,
                                              Q16[:, 2048 * dc + 512 * j:
                                                  2048 * dc + 512 * (j + 1)],
                                              start=(dc == 0), stop=(dc == NCH - 1))
                        cT.inc(li, 1)

                @blk.vector
                def _(e):
                    for oc in range(NCH):
                        e.wait_ge(cT.h, cT0 + oc + 1)
                        ps = PS_A if oc % 2 == 0 else PS_B
                        cV.inc(e.scalar_tensor_tensor(
                            xr_c(oc), ps, VC["bo1T"][:, oc:oc + 1], xr_c(oc),
                            ALU.add, ALU.add), 1)

        def phase_p4_l1(b):
            q1 = dram["L1_ffn_fc1_q"]
            s1 = dram["L1_ffn_fc1_scales"]
            q2 = dram["L1_ffn_fc2_q"]
            s2 = dram["L1_ffn_fc2_scales"]
            with nc.Block() as blk:
                cV0, cT0, cS0 = cV.n, cT.n, cS.n
                du_mark = dict(PF.get("du", {}))

                def _vpos_stt(f):
                    sbk, fi = f // 8, f % 8
                    if sbk == 0:
                        return 2 * fi
                    base = {1: 16, 2: 48, 3: 80}[sbk]
                    return base + 2 * fi + (16 if fi >= 2 else 0)

                def stt1_mark(f):
                    return cV0 + _vpos_stt(f) + 1

                def stt2_mark(f):
                    return cV0 + _vpos_stt(f) + 2

                def zadd_mark(sbk, oc, th):
                    zbase = {0: 20, 1: 52, 2: 84, 3: 112}[sbk]
                    return cV0 + zbase + 2 * oc + th + 1

                def ymm_mark(f, th):
                    sbk = f // 8
                    return cT0 + 32 * sbk + 2 * (f % 8) + th + 1

                def zmm_mark(sbk, oc, th):
                    return cT0 + 32 * sbk + 16 + 2 * oc + th + 1

                def silu_mark(f, th):
                    sbk = f // 8
                    return cS0 + 16 * sbk + 2 * (f % 8) + th + 1

                def stt_i_mark(i):
                    return stt1_mark(i // 2) if i % 2 == 0 else stt2_mark(i // 2)

                @blk.sync
                def _(e):
                    for i in [1] + list(range(3, 64)):
                        f, w2flag = i // 2, i % 2
                        if i >= 3:
                            e.wait_ge(cV.h, stt_i_mark(i - 3))
                        if not w2flag:
                            src = bass.AP(q1.tensor, q1.offset + 128 * f,
                                          [[4096, 128], [4096 * 128, 8], [1, 128]])
                            DU[i % 3].inc(e.dma_start(out=DQ_INT[i % 3], in_=src), 16)
                            ssrc = bass.AP(s1.tensor, s1.offset + 128 * f,
                                           [[0, 128], [4096, 8], [1, 128]])
                            DU[i % 3].inc(e.dma_start(out=DQ_SCL[i % 3], in_=ssrc), 16)
                        else:
                            DU[i % 3].inc(e.dma_start(
                                out=DQ_INT[i % 3], in_=q2[128 * f:128 * (f + 1), :]), 16)
                            ssrc2 = bass.AP(s2.tensor, s2.offset + 1024 * f,
                                            [[0, 128], [1, 1024]])
                            DU[i % 3].inc(e.dma_start(out=DQ_SCL[i % 3], in_=ssrc2), 16)
                        du_mark[i] = DU[i % 3].n

                @blk.vector
                def _(e):
                    def emit_stt_pair(f):
                        sbk = f // 8
                        i1, i2 = 2 * f, 2 * f + 1
                        e.wait_ge(DU[i1 % 3].h, du_mark[i1])
                        if f >= 2:
                            e.wait_ge(cT.h, ymm_mark(f - 2, 1))
                        w1f = WTA[:, 4096 * (f % 2):4096 * (f % 2) + 1024]
                        cV.inc(e.scalar_tensor_tensor(
                            w1f, DQ_INT[i1 % 3], -128.0, DQ_SCL[i1 % 3],
                            ALU.add, ALU.mult), 1)
                        e.wait_ge(DU[i2 % 3].h, du_mark[i2])
                        if sbk >= 2:
                            e.wait_ge(cT.h, zmm_mark(sbk - 2, 7, 1))
                        slot = (sbk % 2) * 8 + (f % 8)
                        w2sb = W2X[:, 1024 * slot:1024 * (slot + 1)]
                        cV.inc(e.scalar_tensor_tensor(
                            w2sb, DQ_INT[i2 % 3], -128.0, DQ_SCL[i2 % 3],
                            ALU.add, ALU.mult), 1)

                    def emit_zadds(sbk):
                        for oc in range(NCH):
                            for th in range(2):
                                e.wait_ge(cT.h, zmm_mark(sbk, oc, th))
                                dst = XR[:, 2048 * oc + 1024 * th:
                                         2048 * oc + 1024 * (th + 1)]
                                cV.inc(e.scalar_tensor_tensor(
                                    dst, PS_B[:, 1024 * th:1024 * (th + 1)],
                                    0.5, dst, ALU.mult, ALU.add), 1)

                    for sbk in range(4):
                        emit_stt_pair(8 * sbk)
                        emit_stt_pair(8 * sbk + 1)
                        if sbk >= 1:
                            emit_zadds(sbk - 1)
                        for fi in range(2, 8):
                            emit_stt_pair(8 * sbk + fi)
                    emit_zadds(3)
                assert cV.n == cV0 + 128

                @blk.tensor
                def _(e):
                    for sbk in range(4):
                        for fi in range(8):
                            f = sbk * 8 + fi
                            e.wait_ge(cV.h, stt1_mark(f))
                            w1f = WTA[:, 4096 * (f % 2):4096 * (f % 2) + 1024]
                            for th in range(2):
                                yi = 2 * f + th
                                if yi >= 2:
                                    e.wait_ge(cS.h, silu_mark((yi - 2) // 2,
                                                              (yi - 2) % 2))
                                ps = PS_A[:, 1024 * (yi % 2):1024 * (yi % 2 + 1)]
                                for hc in range(NCH):
                                    lhsT = w1f[:, 128 * hc:128 * (hc + 1)]
                                    e.matmul(ps[:, 0:512], lhsT,
                                             xn_c(hc)[:, 1024 * th:1024 * th + 512],
                                             start=(hc == 0), stop=(hc == NCH - 1))
                                    li = e.matmul(ps[:, 512:1024], lhsT,
                                                  xn_c(hc)[:, 1024 * th + 512:
                                                           1024 * (th + 1)],
                                                  start=(hc == 0), stop=(hc == NCH - 1))
                                cT.inc(li, 1)
                        for oc in range(NCH):
                            for th in range(2):
                                if oc >= 1:
                                    e.wait_ge(cV.h, zadd_mark(sbk, oc - 1, th))
                                elif sbk >= 1:
                                    e.wait_ge(cV.h, zadd_mark(sbk - 1, 7, th))
                                zs = PS_B[:, 1024 * th:1024 * (th + 1)]
                                for fi in range(8):
                                    f = sbk * 8 + fi
                                    e.wait_ge(cV.h, stt2_mark(f))
                                    e.wait_ge(cS.h, silu_mark(f, 1))
                                    slot = (sbk % 2) * 8 + fi
                                    lhsT = W2X[:, 1024 * slot + 128 * oc:
                                               1024 * slot + 128 * (oc + 1)]
                                    yv = Y8[:, 2048 * fi + 1024 * th:
                                            2048 * fi + 1024 * (th + 1)]
                                    for j in range(2):
                                        li = e.matmul(
                                            zs[:, 512 * j:512 * (j + 1)], lhsT,
                                            yv[:, 512 * j:512 * (j + 1)],
                                            start=(fi == 0), stop=(fi == 7))
                                cT.inc(li, 1)
                assert cT.n == cT0 + 4 * 32

                @blk.scalar
                def _(e):
                    for sbk in range(4):
                        for fi in range(8):
                            f = sbk * 8 + fi
                            for th in range(2):
                                yi = 2 * f + th
                                e.wait_ge(cT.h, ymm_mark(f, th))
                                if sbk >= 1 and th == 0:
                                    e.wait_ge(cT.h, zmm_mark(sbk - 1, 7, 1))
                                ps = PS_A[:, 1024 * (yi % 2):1024 * (yi % 2 + 1)]
                                ydst = Y8[:, 2048 * fi + 1024 * th:
                                          2048 * fi + 1024 * (th + 1)]
                                cS.inc(e.activation(ydst, ps, AF.Silu,
                                                    bias=VC["b1T"][:, f:f + 1],
                                                    scale=1.0), 1)
                assert cS.n == cS0 + 64

            with nc.Block() as blk:
                @blk.sync
                def _(e):
                    dO.inc(e.dma_start(
                        out=xout[b].rearrange("(c p) t -> p c t", p=128),
                        in_=XR.rearrange("p (c t) -> p c t", t=S)), 16)

        def phase_out_l0(b):
            with nc.Block() as blk:
                @blk.sync
                def _(e):
                    dO.inc(e.dma_start(
                        out=xout[b].rearrange("(c p) t -> p c t", p=128),
                        in_=XR.rearrange("p (c t) -> p c t", t=S)), 16)

        for b in range(NBC):
            for li, layer in enumerate(layers):
                spill_mark = phase_p1(b, layer, load_x=(li == 0))
                phase_p2(b, layer, spill_mark)
                if layer == 1:
                    phase_p25(b)
                phase_p3(b, layer)
                phase_p35(b, layer)
                if layer == 0:
                    phase_p4_l0(b)
                    if layers == (0,):
                        phase_out_l0(b)
                else:
                    phase_p4_l1(b)

        with nc.Block() as blk:
            @blk.sync
            def _(e):
                e.wait_ge(dO.h, dO.n)

    nc._megakernel_inputs = list(input_names)
    return nc


_PROG_CACHE = {}
_CONSTS = None


def _get_prog(layers):
    key = tuple(layers)
    if key not in _PROG_CACHE:
        _PROG_CACHE[key] = build_program(key)
    return _PROG_CACHE[key]


def _consts():
    global _CONSTS
    if _CONSTS is None:
        _CONSTS = host_constants()
    return _CONSTS


def _core_maps(prog, inputs, extra_per_core):
    cs, cosT, sinT = _consts()
    names = prog._megakernel_inputs
    base = {}
    for nm in names:
        if nm in inputs:
            base[nm] = np.ascontiguousarray(np.asarray(inputs[nm]))
    base["csmall"] = cs
    if "cosrope" in names:
        base["cosrope"] = cosT
        base["sinrope"] = sinT
    maps = []
    for i in range(8):
        m = dict(base)
        m.update(extra_per_core[i])
        maps.append(m)
    return maps


def kernel(**inputs):
    x = np.asarray(inputs["L0_attn_x"])
    if FUSED:
        prog = _get_prog((0, 1))
        extra = [{"x": np.ascontiguousarray(x[2 * i:2 * i + 2].transpose(0, 2, 1))}
                 for i in range(8)]
        res = run_bass_kernel_spmd(prog, _core_maps(prog, inputs, extra),
                                   core_ids=list(range(8)))
        outs = [res.results[i]["out"].transpose(0, 2, 1) for i in range(8)]
        return np.ascontiguousarray(np.concatenate(outs, axis=0),
                                    dtype=np.float32)
    prog0 = _get_prog((0,))
    extra = [{"x": np.ascontiguousarray(x[2 * i:2 * i + 2].transpose(0, 2, 1))}
             for i in range(8)]
    res0 = run_bass_kernel_spmd(prog0, _core_maps(prog0, inputs, extra),
                                core_ids=list(range(8)))
    prog1 = _get_prog((1,))
    extra1 = [{"x1": np.ascontiguousarray(res0.results[i]["x1"])}
              for i in range(8)]
    res1 = run_bass_kernel_spmd(prog1, _core_maps(prog1, inputs, extra1),
                                core_ids=list(range(8)))
    outs = [res1.results[i]["out"].transpose(0, 2, 1) for i in range(8)]
    return np.ascontiguousarray(np.concatenate(outs, axis=0), dtype=np.float32)


# version 21
# speedup vs baseline: 1.1211x; 1.1211x over previous
import contextlib
import numpy as np
import ml_dtypes
import concourse.bass as bass
import concourse.mybir as mybir
from concourse.bass_utils import run_bass_kernel_spmd

F32 = mybir.dt.float32
BF16 = mybir.dt.bfloat16
I32 = mybir.dt.int32
AF = mybir.ActivationFunctionType
ALU = mybir.AluOpType
AXX = mybir.AxisListType.X

D = 1024
S = 2048
NBC = 2
NH = 16
HDIM = 64
FF = 4096
NCH = 8
NTC = 16
EPS_LN = 1e-5
EPS_RMS = 1e-6

FUSED = True


class Sem:
    __slots__ = ("h", "n")

    def __init__(self, stack, nc, name):
        self.h = stack.enter_context(nc.semaphore(name))
        self.n = 0

    def inc(self, instr, k=1):
        instr.then_inc(self.h, k)
        self.n += k
        return self.n


def host_constants():
    cs = np.zeros((128, 1024), np.float32)
    csb = cs.view(ml_dtypes.bfloat16)
    cs[:, 0] = 1.0
    csb[:, 2] = ml_dtypes.bfloat16(1.0)
    cs[0, 4:132] = 1.0
    cs[32, 4:132] = 1.0
    cs[0:32, 132:164] = np.eye(32, dtype=np.float32)
    for h in range(16):
        cs[(h % 2) * 64:(h % 2) * 64 + 64, 164 + h] = -1.0
    cs[:, 180:308] = 1.0
    for c in range(8):
        sel = np.zeros((16, 128), np.float32)
        sel[2 * c, 0:64] = 1.0
        sel[2 * c + 1, 64:128] = 1.0
        csb[0:16, 616 + 128 * c:616 + 128 * (c + 1)] = sel.astype(ml_dtypes.bfloat16)
    psel = np.zeros((128, 64), np.float32)
    for p in range(128):
        psel[p, p % 64] = 1.0
    csb[:, 1640:1704] = psel.astype(ml_dtypes.bfloat16)
    i = (np.arange(128) % 64) // 2
    invf = 10000.0 ** (-(np.arange(0, HDIM, 2) / HDIM))
    ang = np.outer(invf[i], np.arange(S))
    cosT = np.ascontiguousarray(np.cos(ang)).astype(ml_dtypes.bfloat16)
    sgn = np.where(np.arange(128) % 2 == 0, -1.0, 1.0)[:, None]
    sinT = np.ascontiguousarray(np.sin(ang) * sgn).astype(ml_dtypes.bfloat16)
    return cs, cosT, sinT


def build_program(layers):
    layers = tuple(layers)
    nc = bass.Bass("TRN2", target_bir_lowering=False)
    stack = contextlib.ExitStack()
    dram = {}
    input_names = []

    def din(name, shape, dtype):
        dram[name] = nc.dram_tensor(name, shape, dtype, kind="ExternalInput").ap()
        input_names.append(name)
        return dram[name]

    def dout(name, shape, dtype):
        dram[name] = nc.dram_tensor(name, shape, dtype, kind="ExternalOutput").ap()
        return dram[name]

    def dint(name, shape, dtype):
        dram[name] = nc.dram_tensor(name, shape, dtype, kind="Internal").ap()
        return dram[name]

    xin = din("x" if 0 in layers else "x1", [NBC, D, S], F32)
    xout = dout("x1" if layers == (0,) else "out", [NBC, D, S], F32)
    csmall = din("csmall", [128, 1024], F32)
    if 1 in layers:
        din("cosrope", [128, S], BF16)
        din("sinrope", [128, S], BF16)
    if 0 in layers:
        for nm in ("L0_attn_Wq", "L0_attn_Wk", "L0_attn_Wv", "L0_attn_Wo",
                   "L0_ffn_weight"):
            din(nm, [D, D], F32)
        for nm in ("L0_attn_bq", "L0_attn_bk", "L0_attn_bv", "L0_attn_bo",
                   "L0_attn_norm_q_w", "L0_attn_norm_k_w"):
            din(nm, [D], F32)
        din("L0_ffn_disha", [64, D], F32)
    if 1 in layers:
        for nm in ("L1_attn_wq", "L1_attn_wk", "L1_attn_wv"):
            din(nm, [D, D], F32)
        din("L1_ffn_ln_gamma", [D], F32)
        din("L1_ffn_ln_beta", [D], F32)
        din("L1_ffn_fc1_q", [D, FF], I32)
        din("L1_ffn_fc1_scales", [D // 128, FF], F32)
        din("L1_ffn_fc1_bias", [FF], F32)
        din("L1_ffn_fc2_q", [FF, D], I32)
        din("L1_ffn_fc2_scales", [FF // 128, D], F32)
        din("L1_ffn_fc2_bias", [D], F32)

    tw_names = []
    if 0 in layers:
        tw_names += ["L0_attn_Wq", "L0_attn_Wk", "L0_attn_Wv", "L0_attn_Wo",
                     "L0_ffn_weight"]
    if 1 in layers:
        tw_names += ["L1_attn_wq", "L1_attn_wk", "L1_attn_wv"]
    for nm in tw_names:
        dint(nm + "_bf", [D, D], BF16)
    xspill = dint("xspill", [D, S], F32)

    with stack:
        sbuf = lambda name, shape, dtype, **kw: stack.enter_context(
            nc.sbuf_tensor(name, shape, dtype, **kw))[:]
        R1 = sbuf("R1", [128, 16384], F32)
        R2 = sbuf("R2", [128, 16384], F32)
        XNT = sbuf("XNT", [128, 16384], BF16)
        WTA = sbuf("WTA", [128, 8192], BF16, is_transpose_dest=True)
        CS8 = sbuf("CS8", [128, 4096], BF16)
        CSM = sbuf("CSM", [128, 1024], F32)
        SR = sbuf("SR", [128, 4096], F32)
        VECS = sbuf("VECS", [128, 256], F32)
        VECB = sbuf("VECB", [128, 64], BF16)
        STG = SR[0:32, 0:1024]
        SRb = SR.bitcast(BF16)
        SRA = SR[0:1, 0:2048]
        SRB = SR[0:1, 2048:4096]
        OST = SRb[0:64, 0:1024]

        R1b = R1.bitcast(BF16)
        R2b = R2.bitcast(BF16)
        CSMb = CSM.bitcast(BF16)
        CS8i = CS8.bitcast(I32)
        CS8f = CS8.bitcast(F32)

        XR = R1
        VA = R1b[:, 0:16640]
        WTB = R1b[:, 0:8192]
        PT = [R1b[:, 16640:17664], R1b[:, 17664:18688],
              R1b[:, 28928:29952], R1b[:, 29952:30976]]
        DEN = R1[0:16, 9344:11392]
        RDEN = R1b[0:16, 22784:24832]
        QSQ = R1b[:, 24832:26880]
        TMPR = R1b[:, 26880:28928]

        Q16 = R2b[:, 0:16384]
        K16 = R2b[:, 16384:32768]
        XSQ32 = R2[:, 0:2048]
        TMP32 = R2[:, 2048:4096]
        Y8 = R2b
        W2X = R2b[:, 16384:32768]

        W1I = CS8i[:, 0:1024]
        SCB = CS8f[:, 1024:2048]
        SRi = SR.bitcast(I32)
        DQ_INT = [CS8i[:, 0:1024], SRi[:, 0:1024], SRi[:, 2048:3072]]
        DQ_SCL = [CS8f[:, 1024:2048], SR[:, 1024:2048], SR[:, 3072:4096]]
        COSR = CS8[:, 0:2048]
        SINR = CS8[:, 2048:4096]
        POOLPAD = CS8[:, 0:2048]
        DISHA = CS8[:, 2048:3072]

        ONE128 = CSM[:, 0:1]
        ONE128B = CSMb[:, 2:3]
        ONESROW = CSM[0:1, 4:132]
        ONESROW32 = CSM[32:33, 4:132]
        IDENT32 = CSM[0:32, 132:164]
        MASKNEG = CSM[:, 164:180]
        ONESSQ = CSM[:, 180:308]
        SEL16S = CSMb[0:16, 616:1640]
        PSEL = CSMb[:, 1640:1704]

        VC = {}
        _vc = [0]

        def vcol(name, n=1):
            VC[name] = VECS[:, _vc[0]:_vc[0] + n]
            _vc[0] += n

        for nm, n in (("bqT", 8), ("bkT", 8), ("boT", 8), ("wq0125T", 8),
                      ("wknT", 8), ("gammaT", 8), ("betaT", 8), ("hb2T", 8),
                      ("QmaxT", 8), ("KmaxT", 8), ("bo1T", 8), ("b1T", 32),
                      ("negMrhs", 16), ("negMb", 16), ("zeroT", 1), ("oneT", 1),
                      ("c0125T", 1)):
            vcol(nm, n)
        BV16 = VECB[:, 0:8]
        KMAX16 = VECB[:, 8:16]

        PS_A = stack.enter_context(nc.psum_tensor("PS_A", [128, 2048], F32))[:]
        PS_B = stack.enter_context(nc.psum_tensor("PS_B", [128, 2048], F32))[:]

        sems = {nm: Sem(stack, nc, nm)
                for nm in ("dW", "dS", "dX", "dO", "dP", "cT", "cV", "cS", "cP",
                           "dU0", "dU1", "dU2", "dDen", "dOD", "dSB")}
        dW, dS, dX, dO, dP = (sems[k] for k in ("dW", "dS", "dX", "dO", "dP"))
        cT, cV, cS, cP = (sems[k] for k in ("cT", "cV", "cS", "cP"))
        DU = [sems["dU0"], sems["dU1"], sems["dU2"]]
        dDen, dOD, dSB = sems["dDen"], sems["dOD"], sems["dSB"]

        PF = {}

        def xr_c(c):
            return XR[:, 2048 * c:2048 * (c + 1)]

        def xn_c(c):
            return XNT[:, 2048 * c:2048 * (c + 1)]

        def q_c(c):
            return Q16[:, 2048 * c:2048 * (c + 1)]

        def k_c(c):
            return K16[:, 2048 * c:2048 * (c + 1)]

        def load_wt(e, nm, buf=None, sem=None):
            buf = WTA if buf is None else buf
            sem = dS if sem is None else sem
            wbf = dram[nm + "_bf"]
            for c in range(NCH):
                sem.inc(e.dma_start(out=buf[:, 1024 * c:1024 * (c + 1)],
                                    in_=wbf[:, 128 * c:128 * (c + 1)],
                                    transpose=True), 16)
            return sem.n

        with nc.Block() as blk:
            @blk.sync
            def _(e):
                dW.inc(e.dma_start(out=CSM, in_=csmall), 16)

            @blk.gpsimd
            def _(e):
                for nm in tw_names:
                    dP.inc(e.dma_start(out=dram[nm + "_bf"], in_=dram[nm]), 16)
        cast_mark = dP.n

        vec_plan = []
        if 0 in layers:
            vec_plan += [("L0_attn_bq", "bqT", 8), ("L0_attn_bk", "bkT", 8),
                         ("L0_attn_bv", "_bv", 8), ("L0_attn_bo", "boT", 8),
                         ("L0_attn_norm_q_w", "wq0125T", 8),
                         ("L0_attn_norm_k_w", "wknT", 8)]
        if 1 in layers:
            vec_plan += [("L1_ffn_ln_gamma", "gammaT", 8),
                         ("L1_ffn_ln_beta", "betaT", 8),
                         ("L1_ffn_fc2_bias", "hb2T", 8),
                         ("L1_ffn_fc1_bias", "b1T", 32)]

        for g0 in range(0, len(vec_plan), 8):
            grp = vec_plan[g0:g0 + 8]
            with nc.Block() as blk:
                @blk.sync
                def _(e, grp=grp):
                    e.wait_ge(dW.h, 16)
                    for idx, (src, dst, rows) in enumerate(grp):
                        srcap = dram[src].rearrange("(a b) -> a b", b=128)
                        dX.inc(e.dma_start(
                            out=STG[0:rows, 128 * idx:128 * (idx + 1)],
                            in_=srcap), 16)
            ld_mark = dX.n
            with nc.Block() as blk:
                @blk.tensor
                def _(e, grp=grp):
                    e.wait_ge(dX.h, ld_mark)
                    for idx, (src, dst, rows) in enumerate(grp):
                        li = e.transpose(PS_A[:, 64 * idx:64 * idx + rows],
                                         STG[0:rows, 128 * idx:128 * (idx + 1)],
                                         IDENT32[0:rows, 0:rows])
                    cT.inc(li, 1)
            tr_mark = cT.n
            with nc.Block() as blk:
                @blk.vector
                def _(e, grp=grp):
                    e.wait_ge(cT.h, tr_mark)
                    for idx, (src, dst, rows) in enumerate(grp):
                        ps = PS_A[:, 64 * idx:64 * idx + rows]
                        if dst == "_bv":
                            e.tensor_copy(BV16, ps)
                        else:
                            e.tensor_copy(VC[dst], ps)

        with nc.Block() as blk:
            @blk.vector
            def _(e):
                if 0 in layers:
                    e.tensor_scalar_mul(VC["wq0125T"], VC["wq0125T"], 0.125)
                if 1 in layers:
                    e.tensor_scalar_mul(VC["hb2T"], VC["hb2T"], 0.5)
                e.memset(VC["zeroT"], 0.0)
                e.memset(VC["oneT"], 1.0)
                e.memset(VC["c0125T"], 0.125)
                e.memset(VC["bo1T"], 0.0)

        def phase_p1(b, layer, load_x):
            with nc.Block() as blk:
                dX0, cS0, cT0, cV0 = dX.n, cS.n, cT.n, cV.n
                if load_x:
                    @blk.sync
                    def _(e):
                        dX.inc(e.dma_start(
                            out=XR.rearrange("p (c t) -> p c t", t=S),
                            in_=xin[b].rearrange("(c p) t -> p c t", p=128)), 16)
                xload = dX.n

                XB = [R2b[:, 8192:10240], R2b[:, 10240:12288]]
                XSQ16 = [R2b[:, 12288:14336], R2b[:, 14336:16384]]

                @blk.scalar
                def _(e):
                    if load_x:
                        e.wait_ge(dX.h, xload)
                    for c in range(NCH):
                        if c >= 2:
                            e.wait_ge(cT.h, cT0 + 2 * (c - 2) + 2)
                        with nc.allow_low_precision(
                                reason="LN stats operands in bf16; fp32 "
                                       "accumulation (error ~1e-4 of sigma)"):
                            cS.inc(e.activation(XSQ16[c % 2], xr_c(c),
                                                AF.Square), 1)
                assert cS.n == cS0 + NCH

                @blk.vector
                def _(e):
                    if load_x:
                        e.wait_ge(dX.h, xload)
                    for c in range(NCH):
                        if c >= 2:
                            e.wait_ge(cT.h, cT0 + 2 * (c - 2) + 1)
                        with nc.allow_low_precision(
                                reason="LN stats operands in bf16"):
                            cV.inc(e.tensor_copy(XB[c % 2], xr_c(c)), 1)
                assert cV.n == cV0 + NCH

                @blk.tensor
                def _(e):
                    for c in range(NCH):
                        e.wait_ge(cV.h, cV0 + c + 1)
                        for j in range(4):
                            li = e.matmul(PS_A[0:1, 512 * j:512 * (j + 1)], ONE128B,
                                          XB[c % 2][:, 512 * j:512 * (j + 1)],
                                          start=(c == 0), stop=(c == NCH - 1))
                        cT.inc(li, 1)
                        e.wait_ge(cS.h, cS0 + c + 1)
                        for j in range(4):
                            li = e.matmul(PS_B[0:1, 512 * j:512 * (j + 1)], ONE128B,
                                          XSQ16[c % 2][:, 512 * j:512 * (j + 1)],
                                          start=(c == 0), stop=(c == NCH - 1))
                        cT.inc(li, 1)
                assert cT.n == cT0 + 2 * NCH

            with nc.Block() as blk:
                cV0, cS0, cT0 = cV.n, cS.n, cT.n

                @blk.vector
                def _(e):
                    T = PS_A[0:1, 0:2048]
                    e.tensor_copy(SRA, PS_A[0:1, :])
                    e.tensor_copy(SRB, PS_B[0:1, :])
                    e.tensor_scalar_mul(SRA, SRA, 1.0 / D)
                    e.tensor_scalar(SRB, SRB, 1.0 / D, EPS_LN,
                                    ALU.mult, ALU.add)
                    e.tensor_tensor(T, SRA, SRA, ALU.mult)
                    e.tensor_tensor(SRB, SRB, T, ALU.subtract)
                    cV.inc(e.reciprocal(T, SRB), 1)
                    e.wait_ge(cS.h, cS0 + 1)
                    cV.inc(e.tensor_tensor(SRA, SRA, SRB, ALU.mult), 1)
                    e.wait_ge(cT.h, cT0 + 1)
                    for c in range(NCH):
                        e.tensor_tensor(TMP32, xr_c(c), PS_A, ALU.mult)
                        li = e.tensor_tensor(xn_c(c), TMP32, PS_B, ALU.subtract)
                        if layer == 1:
                            li = e.tensor_tensor(xr_c(c), xr_c(c), xn_c(c), ALU.add)
                        if c == 1:
                            cV.inc(li, 1)
                    cV.inc(li, 1)

                @blk.scalar
                def _(e):
                    e.wait_ge(cV.h, cV0 + 1)
                    cS.inc(e.activation(SRB, PS_A[0:1, 0:2048], AF.Sqrt), 1)

                @blk.tensor
                def _(e):
                    e.wait_ge(cV.h, cV0 + 2)
                    for j in range(4):
                        e.matmul(PS_A[:, 512 * j:512 * (j + 1)], ONESROW,
                                 SRB[:, 512 * j:512 * (j + 1)],
                                 start=True, stop=True)
                    for j in range(4):
                        li = e.matmul(PS_B[:, 512 * j:512 * (j + 1)], ONESROW,
                                      SRA[:, 512 * j:512 * (j + 1)],
                                      start=True, stop=True)
                    cT.inc(li, 1)

                @blk.sync
                def _(e):
                    e.wait_ge(dP.h, cast_mark)
                    if layer == 1:
                        dS.inc(e.dma_start(out=COSR, in_=dram["cosrope"]), 16)
                        dS.inc(e.dma_start(out=SINR, in_=dram["sinrope"]), 16)
                    wq = "L0_attn_Wq" if layer == 0 else "L1_attn_wq"
                    PF["wq"] = load_wt(e, wq)
                    e.wait_ge(cV.h, cV0 + 4)
                    dX.inc(e.dma_start(
                        out=xspill[0:256].rearrange("(c p) t -> p c t", p=128),
                        in_=XR[:, 0:2 * S].rearrange("p (c t) -> p c t", t=S)),
                        16)
                    dX.inc(e.dma_start(
                        out=xspill[256:1024].rearrange("(c p) t -> p c t",
                                                       p=128),
                        in_=XR[:, 2 * S:8 * S].rearrange("p (c t) -> p c t",
                                                         t=S)), 16)
                    PF["spill"] = dX.n
                    PF["spill01"] = dSB.n
                    wk = "L0_attn_Wk" if layer == 0 else "L1_attn_wk"
                    e.wait_ge(dX.h, PF["spill"])
                    PF["wk"] = load_wt(e, wk, WTB, dSB)
            return PF["spill"]

        def emit_feature_proj(blk, wname, dst_c, evac_kind, evac_args,
                              preloaded=None, wbuf=None, wsem=None,
                              extra_load=None):
            wbuf = WTA if wbuf is None else wbuf
            wsem = dS if wsem is None else wsem
            with blk:
                box = {}
                if preloaded is None or extra_load is not None:
                    @blk.sync
                    def _(e):
                        if preloaded is None:
                            e.wait_ge(dP.h, cast_mark)
                            box["m"] = load_wt(e, wname, wbuf, wsem)
                        if extra_load is not None:
                            key, nm2, buf2, sem2 = extra_load
                            PF[key] = load_wt(e, nm2, buf2, sem2)
                wt_mark = box.get("m", preloaded)
                cT0, cS0 = cT.n, cS.n

                @blk.tensor
                def _(e):
                    e.wait_ge(wsem.h, wt_mark)
                    for oc in range(NCH):
                        if oc >= 2:
                            e.wait_ge(cS.h, cS0 + oc - 1)
                        ps = PS_A if oc % 2 == 0 else PS_B
                        for dc in range(NCH):
                            lhsT = wbuf[:, 1024 * dc + 128 * oc:
                                        1024 * dc + 128 * (oc + 1)]
                            for j in range(4):
                                li = e.matmul(ps[:, 512 * j:512 * (j + 1)], lhsT,
                                              xn_c(dc)[:, 512 * j:512 * (j + 1)],
                                              start=(dc == 0), stop=(dc == NCH - 1))
                        cT.inc(li, 1)
                assert cT.n == cT0 + NCH

                @blk.scalar
                def _(e):
                    for oc in range(NCH):
                        e.wait_ge(cT.h, cT0 + oc + 1)
                        ps = PS_A if oc % 2 == 0 else PS_B
                        if evac_kind == "bias":
                            cS.inc(e.activation(dst_c(oc), ps, AF.Identity,
                                                bias=evac_args[:, oc:oc + 1],
                                                scale=VC["oneT"]), 1)
                        elif evac_kind == "scale0125":
                            cS.inc(e.activation(dst_c(oc), ps, AF.Identity,
                                                bias=VC["zeroT"],
                                                scale=VC["c0125T"]), 1)
                        else:
                            cS.inc(e.activation(dst_c(oc), ps, AF.Copy), 1)
                assert cS.n == cS0 + NCH

        def phase_p2(b, layer, spill_mark):
            if layer == 0:
                wq, wk, wv = "L0_attn_Wq", "L0_attn_Wk", "L0_attn_Wv"
            else:
                wq, wk, wv = "L1_attn_wq", "L1_attn_wk", "L1_attn_wv"

            emit_feature_proj(nc.Block(), wq, q_c,
                              "bias" if layer == 0 else "scale0125",
                              VC["bqT"] if layer == 0 else None,
                              preloaded=PF.get("wq"))
            emit_feature_proj(nc.Block(), wk, k_c,
                              "bias" if layer == 0 else "copy",
                              VC["bkT"] if layer == 0 else None,
                              preloaded=PF.get("wk"), wbuf=WTB, wsem=dSB,
                              extra_load=("wv", wv, WTA, dS))

            with nc.Block() as blk:
                wt_mark = PF["wv"]
                cT0, cS0 = cT.n, cS.n

                @blk.gpsimd
                def _(e):
                    e.wait_ge(dX.h, spill_mark)
                    e.wait_ge(dSB.h, PF["spill01"])
                    cP.inc(e.memset(VA[:, 64::65], 1.0), 1)

                @blk.tensor
                def _(e):
                    e.wait_ge(dS.h, wt_mark)
                    for tc in range(NTC):
                        if tc >= 2:
                            e.wait_ge(cS.h, cS0 + tc - 1)
                        ps = PS_A if tc % 2 == 0 else PS_B
                        for dc in range(NCH):
                            lhsT = XNT[:, 2048 * dc + 128 * tc:
                                       2048 * dc + 128 * (tc + 1)]
                            for j in range(2):
                                li = e.matmul(ps[:, 512 * j:512 * (j + 1)], lhsT,
                                              WTA[:, 1024 * dc + 512 * j:
                                                  1024 * dc + 512 * (j + 1)],
                                              start=(dc == 0), stop=(dc == NCH - 1))
                        cT.inc(li, 1)

                @blk.scalar
                def _(e):
                    e.wait_ge(dX.h, spill_mark)
                    e.wait_ge(dSB.h, PF["spill01"])
                    for tc in range(NTC):
                        e.wait_ge(cT.h, cT0 + tc + 1)
                        ps = PS_A if tc % 2 == 0 else PS_B
                        va_t = bass.AP(VA.tensor, VA.offset + 1040 * tc,
                                       [list(VA.ap[0]), [65, 16], [1, 64]])
                        cS.inc(e.activation(
                            va_t, ps[:, 0:1024].rearrange("p (h j) -> p h j", j=64),
                            AF.Copy), 1)

            if layer == 0:
                with nc.Block() as blk:
                    cS0, cT0 = cS.n, cT.n

                    @blk.scalar
                    def _(e):
                        for c in range(NCH):
                            if c >= 1:
                                e.wait_ge(cT.h, cT0 + 2 * c - 1)
                            cS.inc(e.activation(QSQ, q_c(c), AF.Square), 1)
                            if c >= 1:
                                e.wait_ge(cT.h, cT0 + 2 * c)
                            cS.inc(e.activation(TMPR, k_c(c), AF.Square), 1)

                    @blk.tensor
                    def _(e):
                        for c in range(NCH):
                            e.wait_ge(cS.h, cS0 + 2 * c + 1)
                            for j in range(4):
                                li = e.matmul(PS_A[0:1, 512 * j:512 * (j + 1)],
                                              ONE128B, QSQ[:, 512 * j:512 * (j + 1)],
                                              start=(c == 0), stop=(c == NCH - 1))
                            cT.inc(li, 1)
                            e.wait_ge(cS.h, cS0 + 2 * c + 2)
                            for j in range(4):
                                li = e.matmul(PS_B[0:1, 512 * j:512 * (j + 1)],
                                              ONE128B, TMPR[:, 512 * j:512 * (j + 1)],
                                              start=(c == 0), stop=(c == NCH - 1))
                            cT.inc(li, 1)

                with nc.Block() as blk:
                    cV0, cS0, cT0 = cV.n, cS.n, cT.n

                    @blk.vector
                    def _(e):
                        T = PS_A[0:1, 0:2048]
                        T2 = PS_B[0:1, 0:2048]
                        e.tensor_copy(SRA, PS_A[0:1, :])
                        e.tensor_copy(SRB, PS_B[0:1, :])
                        e.tensor_scalar(SRA, SRA, 1.0 / D, EPS_RMS,
                                        ALU.mult, ALU.add)
                        e.tensor_scalar(SRB, SRB, 1.0 / D, EPS_RMS,
                                        ALU.mult, ALU.add)
                        e.reciprocal(T, SRA)
                        cV.inc(e.reciprocal(T2, SRB), 1)
                        e.wait_ge(cT.h, cT0 + 1)
                        for c in range(NCH):
                            e.scalar_tensor_tensor(
                                q_c(c), q_c(c), VC["wq0125T"][:, c:c + 1], PS_A,
                                ALU.mult, ALU.mult)
                            e.scalar_tensor_tensor(
                                k_c(c), k_c(c), VC["wknT"][:, c:c + 1], PS_B,
                                ALU.mult, ALU.mult)

                    @blk.scalar
                    def _(e):
                        e.wait_ge(cV.h, cV0 + 1)
                        e.activation(SRA, PS_A[0:1, 0:2048], AF.Sqrt)
                        cS.inc(e.activation(SRB, PS_B[0:1, 0:2048],
                                            AF.Sqrt), 1)

                    @blk.tensor
                    def _(e):
                        e.wait_ge(cS.h, cS0 + 1)
                        for j in range(4):
                            e.matmul(PS_A[:, 512 * j:512 * (j + 1)], ONESROW,
                                     SRA[:, 512 * j:512 * (j + 1)],
                                     start=True, stop=True)
                        for j in range(4):
                            li = e.matmul(PS_B[:, 512 * j:512 * (j + 1)], ONESROW,
                                          SRB[:, 512 * j:512 * (j + 1)],
                                          start=True, stop=True)
                        cT.inc(li, 1)
            else:
                with nc.Block() as blk:
                    @blk.vector
                    def _(e):
                        e.wait_ge(dS.h, dS.n)
                        mask = []
                        for i in range(16):
                            mask += [2 * i + 1, 2 * i]
                        for tgt in (q_c, k_c):
                            for c in range(NCH):
                                e.stream_shuffle(TMPR, tgt(c), mask)
                                e.tensor_tensor(TMPR, TMPR, SINR, ALU.mult)
                                e.tensor_tensor(tgt(c), tgt(c), COSR, ALU.mult)
                                e.tensor_tensor(tgt(c), tgt(c), TMPR, ALU.add)

            MROW = SR[0:1, 4080:4096]
            KMCOL = SR[0:16, 4079:4080]
            KMAXD = WTA[:, 0:128]
            with nc.Block() as blk:
                cV0, cT0, cS0 = cV.n, cT.n, cS.n
                dDen0 = dDen.n

                @blk.vector
                def _(e):
                    with nc.allow_low_precision(reason="softmax bound only"):
                        e.memset(KMAXD, 0.0)
                        for c in range(NCH):
                            e.reduce_max(KMAX16[:, c:c + 1], k_c(c), axis=AXX,
                                         apply_absolute_value=True)
                            h = 2 * c
                            e.tensor_copy(
                                KMAXD[0:64, 16 * c + h:16 * c + h + 1],
                                KMAX16[0:64, c:c + 1])
                            cV.inc(e.tensor_copy(
                                KMAXD[64:128, 16 * c + h + 1:16 * c + h + 2],
                                KMAX16[64:128, c:c + 1]), 1)
                        e.wait_ge(cT.h, cT0 + NCH)
                        cV.inc(e.reduce_max(KMCOL, PS_B[0:16, :], axis=AXX), 1)
                        e.wait_ge(cT.h, cT0 + NCH + 1)
                        e.tensor_scalar_mul(VC["negMb"], PS_A[:, 0:16], -1.0)

                @blk.scalar
                def _(e):
                    for c in range(NCH):
                        if c >= 2:
                            e.wait_ge(cT.h, cT0 + c - 1)
                        buf = QSQ if c % 2 == 0 else TMPR
                        cS.inc(e.activation(buf, q_c(c), AF.Abs), 1)

                @blk.gpsimd
                def _(e):
                    e.wait_ge(cV.h, cV0 + NCH + 1)
                    dDen.inc(e.dma_start(out=MROW, in_=KMCOL), 16)

                @blk.tensor
                def _(e):
                    for c in range(NCH):
                        e.wait_ge(cS.h, cS0 + c + 1)
                        e.wait_ge(cV.h, cV0 + c + 1)
                        buf = QSQ if c % 2 == 0 else TMPR
                        for j in range(4):
                            li = e.matmul(PS_B[0:16, 512 * j:512 * (j + 1)],
                                          KMAXD[:, 16 * c:16 * (c + 1)],
                                          buf[:, 512 * j:512 * (j + 1)],
                                          start=(c == 0), stop=(c == NCH - 1))
                        cT.inc(li, 1)
                    e.wait_ge(dDen.h, dDen0 + 16)
                    cT.inc(e.matmul(PS_A[:, 0:16], ONESROW, MROW,
                                    start=True, stop=True), 1)

        def phase_p25(b):
            with nc.Block() as blk:
                cS0, cT0 = cS.n, cT.n

                @blk.scalar
                def _(e):
                    for c in range(NCH):
                        if c >= 1:
                            e.wait_ge(cT.h, cT0 + 2 * c)
                        cS.inc(e.activation(QSQ, xn_c(c), AF.Square), 1)

                @blk.tensor
                def _(e):
                    for c in range(NCH):
                        for j in range(4):
                            li = e.matmul(PS_A[0:1, 512 * j:512 * (j + 1)], ONE128B,
                                          xn_c(c)[:, 512 * j:512 * (j + 1)],
                                          start=(c == 0), stop=(c == NCH - 1))
                        cT.inc(li, 1)
                        e.wait_ge(cS.h, cS0 + c + 1)
                        for j in range(4):
                            li = e.matmul(PS_B[0:1, 512 * j:512 * (j + 1)], ONE128B,
                                          QSQ[:, 512 * j:512 * (j + 1)],
                                          start=(c == 0), stop=(c == NCH - 1))
                        cT.inc(li, 1)

            with nc.Block() as blk:
                cV0, cT0, cS0 = cV.n, cT.n, cS.n

                @blk.vector
                def _(e):
                    T = PS_A[0:1, 0:2048]
                    e.tensor_copy(SRA, PS_A[0:1, :])
                    e.tensor_copy(SRB, PS_B[0:1, :])
                    e.tensor_scalar_mul(SRA, SRA, 1.0 / D)
                    e.tensor_scalar(SRB, SRB, 1.0 / D, EPS_LN,
                                    ALU.mult, ALU.add)
                    e.tensor_tensor(T, SRA, SRA, ALU.mult)
                    e.tensor_tensor(SRB, SRB, T, ALU.subtract)
                    cV.inc(e.reciprocal(T, SRB), 1)
                    e.wait_ge(cS.h, cS0 + 1)
                    cV.inc(e.tensor_tensor(SRA, SRA, SRB, ALU.mult), 1)
                    e.wait_ge(cT.h, cT0 + 1)
                    for c in range(NCH):
                        if c >= 1:
                            e.wait_ge(cS.h, cS0 + 1 + c)
                        e.tensor_tensor(TMPR, xn_c(c), PS_A, ALU.mult)
                        cV.inc(e.tensor_tensor(TMPR, TMPR, PS_B, ALU.subtract), 1)

                @blk.scalar
                def _(e):
                    e.wait_ge(cV.h, cV0 + 1)
                    cS.inc(e.activation(SRB, PS_A[0:1, 0:2048], AF.Sqrt), 1)
                    for c in range(NCH):
                        e.wait_ge(cV.h, cV0 + 3 + c)
                        cS.inc(e.activation(xn_c(c), TMPR, AF.Identity,
                                            bias=VC["betaT"][:, c:c + 1],
                                            scale=VC["gammaT"][:, c:c + 1]), 1)

                @blk.tensor
                def _(e):
                    e.wait_ge(cV.h, cV0 + 2)
                    for j in range(4):
                        e.matmul(PS_A[:, 512 * j:512 * (j + 1)], ONESROW,
                                 SRB[:, 512 * j:512 * (j + 1)],
                                 start=True, stop=True)
                    for j in range(4):
                        li = e.matmul(PS_B[:, 512 * j:512 * (j + 1)], ONESROW,
                                      SRA[:, 512 * j:512 * (j + 1)],
                                      start=True, stop=True)
                    cT.inc(li, 1)

        def phase_p3(b, layer):
            NU = NH * NTC
            STA, STB = PS_A[:, 0:1024], PS_A[:, 1024:2048]
            O_A, O_B = PS_B[0:65, 0:1024], PS_B[0:65, 1024:2048]
            PTA, PTB = [PT[0], PT[1]], [PT[2], PT[3]]

            with nc.Block() as blk:
                if layer == 0:
                    @blk.sync
                    def _(e):
                        PF["ffnw"] = load_wt(e, "L0_ffn_weight")
                else:
                    PF["du"] = {}

                    @blk.sync
                    def _(e):
                        q1 = dram["L1_ffn_fc1_q"]
                        s1 = dram["L1_ffn_fc1_scales"]
                        src = bass.AP(q1.tensor, q1.offset,
                                      [[4096, 128], [4096 * 128, 8], [1, 128]])
                        DU[0].inc(e.dma_start(out=DQ_INT[0], in_=src), 16)
                        ssrc = bass.AP(s1.tensor, s1.offset,
                                       [[0, 128], [4096, 8], [1, 128]])
                        DU[0].inc(e.dma_start(out=DQ_SCL[0], in_=ssrc), 16)
                        PF["du"][0] = DU[0].n
                        src = bass.AP(q1.tensor, q1.offset + 128,
                                      [[4096, 128], [4096 * 128, 8], [1, 128]])
                        DU[2].inc(e.dma_start(out=DQ_INT[2], in_=src), 16)
                        ssrc = bass.AP(s1.tensor, s1.offset + 128,
                                       [[0, 128], [4096, 8], [1, 128]])
                        DU[2].inc(e.dma_start(out=DQ_SCL[2], in_=ssrc), 16)
                        PF["du"][2] = DU[2].n
                cT0, cS0, cV0 = cT.n, cS.n, cV.n
                dDen0, dOD0 = dDen.n, dOD.n
                stA_mark = {}
                stB_mark = {}
                avA_mark = {}
                avB_mark = {}

                def uhc(u):
                    return 2 * (u // 32), (u // 16) % 2, u % 16

                def expA_mark(u):
                    return cS0 + 2 * u + 1

                def expB_mark(u):
                    return cS0 + 2 * u + 2

                def denA_mark(m):
                    return cV0 + 4 * m + 1

                def evacA_mark(m):
                    return cV0 + 4 * m + 2

                def denB_mark(m):
                    return cV0 + 4 * m + 3

                def evacB_mark(m):
                    return cV0 + 4 * m + 4

                def denAdma_mark(m):
                    return dDen0 + 32 * m + 16

                def denBdma_mark(m):
                    return dDen0 + 32 * m + 32

                def odma_mark(m):
                    return dOD0 + 16 * (m + 1)

                @blk.tensor
                def _(e):
                    def emit_st(u):
                        hA, H, c = uhc(u)
                        ch = 2048 * (u // 32)
                        if u >= 1:
                            e.wait_ge(cS.h, expA_mark(u - 1))
                        lhsT = K16[0:64, ch + 128 * c:ch + 128 * (c + 1)]
                        rhs = Q16[0:64, ch + 1024 * H:ch + 1024 * (H + 1)]
                        e.matmul(STA[:, 0:512], lhsT, rhs[:, 0:512],
                                 start=True, stop=True)
                        li = e.matmul(STA[:, 512:1024], lhsT, rhs[:, 512:1024],
                                      start=True, stop=True)
                        cT.inc(li, 1)
                        stA_mark[u] = cT.n
                        if u >= 1:
                            e.wait_ge(cS.h, expB_mark(u - 1))
                        lhsT = K16[64:128, ch + 128 * c:ch + 128 * (c + 1)]
                        rhs = Q16[64:128, ch + 1024 * H:ch + 1024 * (H + 1)]
                        e.matmul(STB[:, 0:512], lhsT, rhs[:, 0:512],
                                 start=True, stop=True)
                        li = e.matmul(STB[:, 512:1024], lhsT, rhs[:, 512:1024],
                                      start=True, stop=True)
                        cT.inc(li, 1)
                        stB_mark[u] = cT.n

                    def emit_av(u):
                        hA, H, c = uhc(u)
                        m = u // 16
                        e.wait_ge(cS.h, expA_mark(u))
                        if c == 0 and m >= 1:
                            e.wait_ge(cV.h, evacA_mark(m - 1))
                        lhsT = VA[:, 1040 * c + 65 * hA:1040 * c + 65 * (hA + 1)]
                        rhs = PTA[u % 2]
                        e.matmul(O_A[:, 0:512], lhsT, rhs[:, 0:512],
                                 start=(c == 0), stop=(c == 15))
                        li = e.matmul(O_A[:, 512:1024], lhsT, rhs[:, 512:1024],
                                      start=(c == 0), stop=(c == 15))
                        cT.inc(li, 1)
                        avA_mark[u] = cT.n
                        e.wait_ge(cS.h, expB_mark(u))
                        if c == 0 and m >= 1:
                            e.wait_ge(cV.h, evacB_mark(m - 1))
                        lhsT = VA[:, 1040 * c + 65 * (hA + 1):1040 * c + 65 * (hA + 2)]
                        rhs = PTB[u % 2]
                        e.matmul(O_B[:, 0:512], lhsT, rhs[:, 0:512],
                                 start=(c == 0), stop=(c == 15))
                        li = e.matmul(O_B[:, 512:1024], lhsT, rhs[:, 512:1024],
                                      start=(c == 0), stop=(c == 15))
                        cT.inc(li, 1)
                        avB_mark[u] = cT.n

                    emit_st(0)
                    for u in range(NU):
                        if u + 1 < NU:
                            emit_st(u + 1)
                        emit_av(u)
                assert cT.n == cT0 + 4 * NU

                @blk.scalar
                def _(e):
                    for u in range(NU):
                        hA, H, c = uhc(u)
                        e.wait_ge(cT.h, stA_mark[u])
                        if u >= 2:
                            e.wait_ge(cT.h, avA_mark[u - 2])
                        cS.inc(e.activation(PTA[u % 2], STA, AF.Exp,
                                            bias=VC["negMb"][:, hA:hA + 1],
                                            scale=1.0), 1)
                        e.wait_ge(cT.h, stB_mark[u])
                        if u >= 2:
                            e.wait_ge(cT.h, avB_mark[u - 2])
                        cS.inc(e.activation(PTB[u % 2], STB, AF.Exp,
                                            bias=VC["negMb"][:, hA + 1:hA + 2],
                                            scale=1.0), 1)
                assert cS.n == cS0 + 2 * NU

                @blk.vector
                def _(e):
                    prev_m = None
                    for m in range(NU // 16):
                        u = m * 16 + 15
                        hA, H, c = uhc(u)
                        ch = 2048 * (u // 32)
                        e.wait_ge(cT.h, avA_mark[u])
                        if m >= 1:
                            e.wait_ge(dDen.h, denAdma_mark(m - 1))
                        cV.inc(e.tensor_copy(SR[64:65, 0:1024],
                                             O_A[64:65, :]), 1)
                        assert cV.n == denA_mark(m)
                        dstA = Q16[0:64, ch + 1024 * H:ch + 1024 * (H + 1)]
                        cV.inc(e.tensor_copy(dstA, O_A[0:64, :]), 1)
                        assert cV.n == evacA_mark(m)
                        e.wait_ge(cT.h, avB_mark[u])
                        if m >= 1:
                            e.wait_ge(dDen.h, denBdma_mark(m - 1))
                        cV.inc(e.tensor_copy(SR[64:65, 1024:2048],
                                             O_B[64:65, :]), 1)
                        assert cV.n == denB_mark(m)
                        if prev_m is not None:
                            e.wait_ge(dOD.h, odma_mark(prev_m))
                        prev_m = m
                        cV.inc(e.tensor_copy(OST, O_B[0:64, :]), 1)
                        assert cV.n == evacB_mark(m)
                assert cV.n == cV0 + 4 * (NU // 16)

                @blk.gpsimd
                def _(e):
                    for m in range(NU // 16):
                        u = m * 16 + 15
                        hA = 2 * (u // 32)
                        H = (u // 16) % 2
                        ch = 2048 * (u // 32)
                        e.wait_ge(cV.h, denA_mark(m))
                        dDen.inc(e.dma_start(
                            out=DEN[hA:hA + 1, 1024 * H:1024 * (H + 1)],
                            in_=SR[64:65, 0:1024]), 16)
                        assert dDen.n == denAdma_mark(m)
                        e.wait_ge(cV.h, denB_mark(m))
                        dDen.inc(e.dma_start(
                            out=DEN[hA + 1:hA + 2, 1024 * H:1024 * (H + 1)],
                            in_=SR[64:65, 1024:2048]), 16)
                        assert dDen.n == denBdma_mark(m)
                        e.wait_ge(cV.h, evacB_mark(m))
                        dOD.inc(e.dma_start(
                            out=Q16[64:128, ch + 1024 * H:ch + 1024 * (H + 1)],
                            in_=OST), 16)
                        assert dOD.n == odma_mark(m)

        def phase_p35(b, layer):
            with nc.Block() as blk:
                cV0, cT0 = cV.n, cT.n
                reload_mark = dX.n + 16

                @blk.vector
                def _(e):
                    e.wait_ge(dDen.h, dDen.n)
                    e.wait_ge(dOD.h, dOD.n)
                    with nc.allow_low_precision(
                            reason="bf16 softmax denominators; matmul operands "
                                   "are bf16 throughout"):
                        cV.inc(e.reciprocal(RDEN, DEN), 1)
                    for c in range(NCH):
                        e.wait_ge(cT.h, cT0 + c + 1)
                        ps = PS_A if c % 2 == 0 else PS_B
                        cV.inc(e.tensor_tensor(q_c(c), q_c(c), ps, ALU.mult), 1)
                    if layer == 1:
                        e.wait_ge(dX.h, reload_mark)
                        for c in range(NCH):
                            e.scalar_tensor_tensor(
                                xr_c(c), q_c(c), VC["hb2T"][:, c:c + 1], xr_c(c),
                                ALU.add, ALU.add)

                @blk.tensor
                def _(e):
                    e.wait_ge(cV.h, cV0 + 1)
                    for c in range(NCH):
                        if c >= 2:
                            e.wait_ge(cV.h, cV0 + c)
                        ps = PS_A if c % 2 == 0 else PS_B
                        for j in range(4):
                            li = e.matmul(ps[:, 512 * j:512 * (j + 1)],
                                          SEL16S[:, 128 * c:128 * (c + 1)],
                                          RDEN[:, 512 * j:512 * (j + 1)],
                                          start=True, stop=True)
                        cT.inc(li, 1)

                @blk.sync
                def _(e):
                    e.wait_ge(cV.h, cV0 + 1 + NCH)
                    dX.inc(e.dma_start(
                        out=XR.rearrange("p (c t) -> p c t", t=S),
                        in_=xspill.rearrange("(c p) t -> p c t", p=128)), 16)
                    assert dX.n == reload_mark

                @blk.gpsimd
                def _(e):
                    e.wait_ge(dX.h, dX.n)

        def phase_p4_l0(b):
            with nc.Block() as blk:
                cT0, cP0, dP0 = cT.n, cP.n, dP.n

                @blk.gpsimd
                def _(e):
                    cP.inc(e.memset(POOLPAD, 0.0), 1)
                    cP.inc(e.memset(DISHA, 0.0), 1)
                    dP.inc(e.dma_start(out=DISHA[0:64, :],
                                       in_=dram["L0_ffn_disha"]), 16)

                @blk.tensor
                def _(e):
                    for dc in range(NCH):
                        for j in range(4):
                            li = e.matmul(PS_A[0:64, 512 * j:512 * (j + 1)], PSEL,
                                          xn_c(dc)[:, 512 * j:512 * (j + 1)],
                                          start=(dc == 0), stop=(dc == NCH - 1))
                    cT.inc(li, 1)

                @blk.scalar
                def _(e):
                    e.wait_ge(cT.h, cT0 + 1)
                    e.wait_ge(cP.h, cP0 + 1)
                    cS.inc(e.activation(POOLPAD[0:64, :], PS_A[0:64, :],
                                        AF.Copy), 1)

            with nc.Block() as blk:
                cT0, cV0 = cT.n, cV.n

                @blk.tensor
                def _(e):
                    e.wait_ge(dS.h, dS.n)
                    e.wait_ge(dP.h, dP.n)
                    for oc in range(NCH):
                        if oc >= 2:
                            e.wait_ge(cV.h, cV0 + oc - 1)
                        ps = PS_A if oc % 2 == 0 else PS_B
                        for dc in range(NCH):
                            lhsT = WTA[:, 1024 * dc + 128 * oc:
                                       1024 * dc + 128 * (oc + 1)]
                            for j in range(4):
                                e.matmul(ps[:, 512 * j:512 * (j + 1)], lhsT,
                                         xn_c(dc)[:, 512 * j:512 * (j + 1)],
                                         start=(dc == 0), stop=False)
                        lhsT = DISHA[:, 128 * oc:128 * (oc + 1)]
                        for j in range(4):
                            li = e.matmul(ps[:, 512 * j:512 * (j + 1)], lhsT,
                                          POOLPAD[:, 512 * j:512 * (j + 1)],
                                          start=False, stop=True)
                        cT.inc(li, 1)

                @blk.vector
                def _(e):
                    for oc in range(NCH):
                        e.wait_ge(cT.h, cT0 + oc + 1)
                        ps = PS_A if oc % 2 == 0 else PS_B
                        cV.inc(e.tensor_tensor(xr_c(oc), xr_c(oc), ps,
                                               ALU.add), 1)

            with nc.Block() as blk:
                @blk.sync
                def _(e):
                    load_wt(e, "L0_attn_Wo")
                wt_mark = dS.n
                cT0 = cT.n

                @blk.tensor
                def _(e):
                    e.wait_ge(dS.h, wt_mark)
                    if b == 0:
                        for oc in range(NCH):
                            for dc in range(NCH):
                                lhsT = WTA[:, 1024 * dc + 128 * oc:
                                           1024 * dc + 128 * (oc + 1)]
                                li = e.matmul(PS_A[:, 64 * oc:64 * oc + 1], lhsT,
                                              BV16[:, dc:dc + 1],
                                              start=(dc == 0), stop=(dc == NCH - 1))
                        cT.inc(li, 1)

                @blk.vector
                def _(e):
                    if b == 0:
                        e.wait_ge(cT.h, cT0 + 1)
                        ps = bass.AP(PS_A.tensor, PS_A.offset,
                                     [list(PS_A.ap[0]), [64, 8]])
                        e.tensor_tensor(VC["bo1T"], VC["boT"], ps, ALU.add)

            with nc.Block() as blk:
                cT0, cV0 = cT.n, cV.n

                @blk.tensor
                def _(e):
                    for oc in range(NCH):
                        if oc >= 2:
                            e.wait_ge(cV.h, cV0 + oc - 1)
                        ps = PS_A if oc % 2 == 0 else PS_B
                        for dc in range(NCH):
                            lhsT = WTA[:, 1024 * dc + 128 * oc:
                                       1024 * dc + 128 * (oc + 1)]
                            for j in range(4):
                                li = e.matmul(ps[:, 512 * j:512 * (j + 1)], lhsT,
                                              Q16[:, 2048 * dc + 512 * j:
                                                  2048 * dc + 512 * (j + 1)],
                                              start=(dc == 0), stop=(dc == NCH - 1))
                        cT.inc(li, 1)

                @blk.vector
                def _(e):
                    for oc in range(NCH):
                        e.wait_ge(cT.h, cT0 + oc + 1)
                        ps = PS_A if oc % 2 == 0 else PS_B
                        cV.inc(e.scalar_tensor_tensor(
                            xr_c(oc), ps, VC["bo1T"][:, oc:oc + 1], xr_c(oc),
                            ALU.add, ALU.add), 1)

        def phase_p4_l1(b):
            q1 = dram["L1_ffn_fc1_q"]
            s1 = dram["L1_ffn_fc1_scales"]
            q2 = dram["L1_ffn_fc2_q"]
            s2 = dram["L1_ffn_fc2_scales"]
            with nc.Block() as blk:
                cV0, cT0, cS0 = cV.n, cT.n, cS.n
                du_mark = dict(PF.get("du", {}))

                def _vpos_stt(f):
                    sbk, fi = f // 8, f % 8
                    if sbk == 0:
                        return 2 * fi
                    base = {1: 16, 2: 48, 3: 80}[sbk]
                    return base + 2 * fi + (16 if fi >= 2 else 0)

                def stt1_mark(f):
                    return cV0 + _vpos_stt(f) + 1

                def stt2_mark(f):
                    return cV0 + _vpos_stt(f) + 2

                def zadd_mark(sbk, oc, th):
                    zbase = {0: 20, 1: 52, 2: 84, 3: 112}[sbk]
                    return cV0 + zbase + 2 * oc + th + 1

                def ymm_mark(f, th):
                    sbk = f // 8
                    return cT0 + 32 * sbk + 2 * (f % 8) + th + 1

                def zmm_mark(sbk, oc, th):
                    return cT0 + 32 * sbk + 16 + 2 * oc + th + 1

                def silu_mark(f, th):
                    sbk = f // 8
                    return cS0 + 16 * sbk + 2 * (f % 8) + th + 1

                def stt_i_mark(i):
                    return stt1_mark(i // 2) if i % 2 == 0 else stt2_mark(i // 2)

                @blk.sync
                def _(e):
                    for i in [1] + list(range(3, 64)):
                        f, w2flag = i // 2, i % 2
                        if i >= 3:
                            e.wait_ge(cV.h, stt_i_mark(i - 3))
                        if not w2flag:
                            src = bass.AP(q1.tensor, q1.offset + 128 * f,
                                          [[4096, 128], [4096 * 128, 8], [1, 128]])
                            DU[i % 3].inc(e.dma_start(out=DQ_INT[i % 3], in_=src), 16)
                            ssrc = bass.AP(s1.tensor, s1.offset + 128 * f,
                                           [[0, 128], [4096, 8], [1, 128]])
                            DU[i % 3].inc(e.dma_start(out=DQ_SCL[i % 3], in_=ssrc), 16)
                        else:
                            DU[i % 3].inc(e.dma_start(
                                out=DQ_INT[i % 3], in_=q2[128 * f:128 * (f + 1), :]), 16)
                            ssrc2 = bass.AP(s2.tensor, s2.offset + 1024 * f,
                                            [[0, 128], [1, 1024]])
                            DU[i % 3].inc(e.dma_start(out=DQ_SCL[i % 3], in_=ssrc2), 16)
                        du_mark[i] = DU[i % 3].n

                @blk.vector
                def _(e):
                    def emit_stt_pair(f):
                        sbk = f // 8
                        i1, i2 = 2 * f, 2 * f + 1
                        e.wait_ge(DU[i1 % 3].h, du_mark[i1])
                        if f >= 2:
                            e.wait_ge(cT.h, ymm_mark(f - 2, 1))
                        w1f = WTA[:, 4096 * (f % 2):4096 * (f % 2) + 1024]
                        cV.inc(e.scalar_tensor_tensor(
                            w1f, DQ_INT[i1 % 3], -128.0, DQ_SCL[i1 % 3],
                            ALU.add, ALU.mult), 1)
                        e.wait_ge(DU[i2 % 3].h, du_mark[i2])
                        if sbk >= 2:
                            e.wait_ge(cT.h, zmm_mark(sbk - 2, 7, 1))
                        slot = (sbk % 2) * 8 + (f % 8)
                        w2sb = W2X[:, 1024 * slot:1024 * (slot + 1)]
                        cV.inc(e.scalar_tensor_tensor(
                            w2sb, DQ_INT[i2 % 3], -128.0, DQ_SCL[i2 % 3],
                            ALU.add, ALU.mult), 1)

                    def emit_zadds(sbk):
                        for oc in range(NCH):
                            for th in range(2):
                                e.wait_ge(cT.h, zmm_mark(sbk, oc, th))
                                dst = XR[:, 2048 * oc + 1024 * th:
                                         2048 * oc + 1024 * (th + 1)]
                                cV.inc(e.scalar_tensor_tensor(
                                    dst, PS_B[:, 1024 * th:1024 * (th + 1)],
                                    0.5, dst, ALU.mult, ALU.add), 1)

                    for sbk in range(4):
                        emit_stt_pair(8 * sbk)
                        emit_stt_pair(8 * sbk + 1)
                        if sbk >= 1:
                            emit_zadds(sbk - 1)
                        for fi in range(2, 8):
                            emit_stt_pair(8 * sbk + fi)
                    emit_zadds(3)
                assert cV.n == cV0 + 128

                @blk.tensor
                def _(e):
                    for sbk in range(4):
                        for fi in range(8):
                            f = sbk * 8 + fi
                            e.wait_ge(cV.h, stt1_mark(f))
                            w1f = WTA[:, 4096 * (f % 2):4096 * (f % 2) + 1024]
                            for th in range(2):
                                yi = 2 * f + th
                                if yi >= 2:
                                    e.wait_ge(cS.h, silu_mark((yi - 2) // 2,
                                                              (yi - 2) % 2))
                                ps = PS_A[:, 1024 * (yi % 2):1024 * (yi % 2 + 1)]
                                for hc in range(NCH):
                                    lhsT = w1f[:, 128 * hc:128 * (hc + 1)]
                                    e.matmul(ps[:, 0:512], lhsT,
                                             xn_c(hc)[:, 1024 * th:1024 * th + 512],
                                             start=(hc == 0), stop=(hc == NCH - 1))
                                    li = e.matmul(ps[:, 512:1024], lhsT,
                                                  xn_c(hc)[:, 1024 * th + 512:
                                                           1024 * (th + 1)],
                                                  start=(hc == 0), stop=(hc == NCH - 1))
                                cT.inc(li, 1)
                        for oc in range(NCH):
                            for th in range(2):
                                if oc >= 1:
                                    e.wait_ge(cV.h, zadd_mark(sbk, oc - 1, th))
                                elif sbk >= 1:
                                    e.wait_ge(cV.h, zadd_mark(sbk - 1, 7, th))
                                zs = PS_B[:, 1024 * th:1024 * (th + 1)]
                                for fi in range(8):
                                    f = sbk * 8 + fi
                                    e.wait_ge(cV.h, stt2_mark(f))
                                    e.wait_ge(cS.h, silu_mark(f, 1))
                                    slot = (sbk % 2) * 8 + fi
                                    lhsT = W2X[:, 1024 * slot + 128 * oc:
                                               1024 * slot + 128 * (oc + 1)]
                                    yv = Y8[:, 2048 * fi + 1024 * th:
                                            2048 * fi + 1024 * (th + 1)]
                                    for j in range(2):
                                        li = e.matmul(
                                            zs[:, 512 * j:512 * (j + 1)], lhsT,
                                            yv[:, 512 * j:512 * (j + 1)],
                                            start=(fi == 0), stop=(fi == 7))
                                cT.inc(li, 1)
                assert cT.n == cT0 + 4 * 32

                @blk.scalar
                def _(e):
                    for sbk in range(4):
                        for fi in range(8):
                            f = sbk * 8 + fi
                            for th in range(2):
                                yi = 2 * f + th
                                e.wait_ge(cT.h, ymm_mark(f, th))
                                if sbk >= 1 and th == 0:
                                    e.wait_ge(cT.h, zmm_mark(sbk - 1, 7, 1))
                                ps = PS_A[:, 1024 * (yi % 2):1024 * (yi % 2 + 1)]
                                ydst = Y8[:, 2048 * fi + 1024 * th:
                                          2048 * fi + 1024 * (th + 1)]
                                cS.inc(e.activation(ydst, ps, AF.Silu,
                                                    bias=VC["b1T"][:, f:f + 1],
                                                    scale=1.0), 1)
                assert cS.n == cS0 + 64

            with nc.Block() as blk:
                @blk.sync
                def _(e):
                    dO.inc(e.dma_start(
                        out=xout[b].rearrange("(c p) t -> p c t", p=128),
                        in_=XR.rearrange("p (c t) -> p c t", t=S)), 16)

        def phase_out_l0(b):
            with nc.Block() as blk:
                @blk.sync
                def _(e):
                    dO.inc(e.dma_start(
                        out=xout[b].rearrange("(c p) t -> p c t", p=128),
                        in_=XR.rearrange("p (c t) -> p c t", t=S)), 16)

        for b in range(NBC):
            for li, layer in enumerate(layers):
                spill_mark = phase_p1(b, layer, load_x=(li == 0))
                phase_p2(b, layer, spill_mark)
                if layer == 1:
                    phase_p25(b)
                phase_p3(b, layer)
                phase_p35(b, layer)
                if layer == 0:
                    phase_p4_l0(b)
                    if layers == (0,):
                        phase_out_l0(b)
                else:
                    phase_p4_l1(b)

        with nc.Block() as blk:
            @blk.sync
            def _(e):
                e.wait_ge(dO.h, dO.n)

    nc._megakernel_inputs = list(input_names)
    return nc


_PROG_CACHE = {}
_CONSTS = None


def _get_prog(layers):
    key = tuple(layers)
    if key not in _PROG_CACHE:
        _PROG_CACHE[key] = build_program(key)
    return _PROG_CACHE[key]


def _consts():
    global _CONSTS
    if _CONSTS is None:
        _CONSTS = host_constants()
    return _CONSTS


def _core_maps(prog, inputs, extra_per_core):
    cs, cosT, sinT = _consts()
    names = prog._megakernel_inputs
    base = {}
    for nm in names:
        if nm in inputs:
            base[nm] = np.ascontiguousarray(np.asarray(inputs[nm]))
    base["csmall"] = cs
    if "cosrope" in names:
        base["cosrope"] = cosT
        base["sinrope"] = sinT
    maps = []
    for i in range(8):
        m = dict(base)
        m.update(extra_per_core[i])
        maps.append(m)
    return maps


def kernel(**inputs):
    x = np.asarray(inputs["L0_attn_x"])
    if FUSED:
        prog = _get_prog((0, 1))
        extra = [{"x": np.ascontiguousarray(x[2 * i:2 * i + 2].transpose(0, 2, 1))}
                 for i in range(8)]
        res = run_bass_kernel_spmd(prog, _core_maps(prog, inputs, extra),
                                   core_ids=list(range(8)))
        outs = [res.results[i]["out"].transpose(0, 2, 1) for i in range(8)]
        return np.ascontiguousarray(np.concatenate(outs, axis=0),
                                    dtype=np.float32)
    prog0 = _get_prog((0,))
    extra = [{"x": np.ascontiguousarray(x[2 * i:2 * i + 2].transpose(0, 2, 1))}
             for i in range(8)]
    res0 = run_bass_kernel_spmd(prog0, _core_maps(prog0, inputs, extra),
                                core_ids=list(range(8)))
    prog1 = _get_prog((1,))
    extra1 = [{"x1": np.ascontiguousarray(res0.results[i]["x1"])}
              for i in range(8)]
    res1 = run_bass_kernel_spmd(prog1, _core_maps(prog1, inputs, extra1),
                                core_ids=list(range(8)))
    outs = [res1.results[i]["out"].transpose(0, 2, 1) for i in range(8)]
    return np.ascontiguousarray(np.concatenate(outs, axis=0), dtype=np.float32)


# version 31
# speedup vs baseline: 1.1743x; 1.0475x over previous
import contextlib
import numpy as np
import ml_dtypes
import concourse.bass as bass
import concourse.mybir as mybir
from concourse.bass_utils import run_bass_kernel_spmd

F32 = mybir.dt.float32
BF16 = mybir.dt.bfloat16
I32 = mybir.dt.int32
AF = mybir.ActivationFunctionType
ALU = mybir.AluOpType
AXX = mybir.AxisListType.X

D = 1024
S = 2048
NBC = 2
NH = 16
HDIM = 64
FF = 4096
NCH = 8
NTC = 16
EPS_LN = 1e-5
EPS_RMS = 1e-6

FUSED = True


class Sem:
    __slots__ = ("h", "n")

    def __init__(self, stack, nc, name):
        self.h = stack.enter_context(nc.semaphore(name))
        self.n = 0

    def inc(self, instr, k=1):
        instr.then_inc(self.h, k)
        self.n += k
        return self.n


def host_constants():
    cs = np.zeros((128, 1024), np.float32)
    csb = cs.view(ml_dtypes.bfloat16)
    cs[:, 0] = 1.0
    csb[:, 2] = ml_dtypes.bfloat16(1.0)
    cs[0, 4:132] = 1.0
    cs[32, 4:132] = 1.0
    cs[0:32, 132:164] = np.eye(32, dtype=np.float32)
    for h in range(16):
        cs[(h % 2) * 64:(h % 2) * 64 + 64, 164 + h] = -1.0
    cs[:, 180:308] = 1.0
    for c in range(8):
        sel = np.zeros((16, 128), np.float32)
        sel[2 * c, 0:64] = 1.0
        sel[2 * c + 1, 64:128] = 1.0
        csb[0:16, 616 + 128 * c:616 + 128 * (c + 1)] = sel.astype(ml_dtypes.bfloat16)
    psel = np.zeros((128, 64), np.float32)
    for p in range(128):
        psel[p, p % 64] = 1.0
    csb[:, 1640:1704] = psel.astype(ml_dtypes.bfloat16)
    i = (np.arange(128) % 64) // 2
    invf = 10000.0 ** (-(np.arange(0, HDIM, 2) / HDIM))
    ang = np.outer(invf[i], np.arange(S))
    cosT = np.ascontiguousarray(np.cos(ang)).astype(ml_dtypes.bfloat16)
    sgn = np.where(np.arange(128) % 2 == 0, -1.0, 1.0)[:, None]
    sinT = np.ascontiguousarray(np.sin(ang) * sgn).astype(ml_dtypes.bfloat16)
    return cs, cosT, sinT


def build_program(layers):
    layers = tuple(layers)
    nc = bass.Bass("TRN2", target_bir_lowering=False)
    stack = contextlib.ExitStack()
    dram = {}
    input_names = []

    def din(name, shape, dtype):
        dram[name] = nc.dram_tensor(name, shape, dtype, kind="ExternalInput").ap()
        input_names.append(name)
        return dram[name]

    def dout(name, shape, dtype):
        dram[name] = nc.dram_tensor(name, shape, dtype, kind="ExternalOutput").ap()
        return dram[name]

    def dint(name, shape, dtype):
        dram[name] = nc.dram_tensor(name, shape, dtype, kind="Internal").ap()
        return dram[name]

    xin = din("x" if 0 in layers else "x1", [NBC, D, S], F32)
    xout = dout("x1" if layers == (0,) else "out", [NBC, D, S], F32)
    csmall = din("csmall", [128, 1024], F32)
    if 1 in layers:
        din("cosrope", [128, S], BF16)
        din("sinrope", [128, S], BF16)
    if 0 in layers:
        for nm in ("L0_attn_Wq", "L0_attn_Wk", "L0_attn_Wv", "L0_attn_Wo",
                   "L0_ffn_weight"):
            din(nm, [D, D], F32)
        for nm in ("L0_attn_bq", "L0_attn_bk", "L0_attn_bv", "L0_attn_bo",
                   "L0_attn_norm_q_w", "L0_attn_norm_k_w"):
            din(nm, [D], F32)
        din("L0_ffn_disha", [64, D], F32)
    if 1 in layers:
        for nm in ("L1_attn_wq", "L1_attn_wk", "L1_attn_wv"):
            din(nm, [D, D], F32)
        din("L1_ffn_ln_gamma", [D], F32)
        din("L1_ffn_ln_beta", [D], F32)
        din("L1_ffn_fc1_q", [D, FF], I32)
        din("L1_ffn_fc1_scales", [D // 128, FF], F32)
        din("L1_ffn_fc1_bias", [FF], F32)
        din("L1_ffn_fc2_q", [FF, D], I32)
        din("L1_ffn_fc2_scales", [FF // 128, D], F32)
        din("L1_ffn_fc2_bias", [D], F32)

    tw_names = []
    if 0 in layers:
        tw_names += ["L0_attn_Wq", "L0_attn_Wk", "L0_attn_Wv", "L0_attn_Wo",
                     "L0_ffn_weight"]
    if 1 in layers:
        tw_names += ["L1_attn_wq", "L1_attn_wk", "L1_attn_wv"]
    for nm in tw_names:
        dint(nm + "_bf", [D, D], BF16)
    xspill = dint("xspill", [D, S], F32)

    with stack:
        sbuf = lambda name, shape, dtype, **kw: stack.enter_context(
            nc.sbuf_tensor(name, shape, dtype, **kw))[:]
        R1 = sbuf("R1", [128, 16384], F32)
        R2 = sbuf("R2", [128, 16384], F32)
        XNT = sbuf("XNT", [128, 16384], BF16)
        WTA = sbuf("WTA", [128, 8192], BF16, is_transpose_dest=True)
        CS8 = sbuf("CS8", [128, 4096], BF16)
        CSM = sbuf("CSM", [128, 1024], F32)
        SR = sbuf("SR", [128, 4096], F32)
        VECS = sbuf("VECS", [128, 256], F32)
        VECB = sbuf("VECB", [128, 64], BF16)
        STG = SR[0:32, 0:1024]
        SRb = SR.bitcast(BF16)
        SRA = SR[0:1, 0:2048]
        SRB = SR[0:1, 2048:4096]
        OST = SRb[0:64, 0:1024]

        R1b = R1.bitcast(BF16)
        R2b = R2.bitcast(BF16)
        CSMb = CSM.bitcast(BF16)
        CS8i = CS8.bitcast(I32)
        CS8f = CS8.bitcast(F32)

        XR = R1
        VA = R1b[:, 0:16640]
        WTB = R1b[:, 0:8192]
        PT = [R1b[:, 16640:17664], R1b[:, 17664:18688],
              R1b[:, 28928:29952], R1b[:, 29952:30976]]
        DEN = R1[0:16, 9344:11392]
        RDEN = R1b[0:16, 22784:24832]
        QSQ = R1b[:, 24832:26880]
        TMPR = R1b[:, 26880:28928]

        Q16 = R2b[:, 0:16384]
        K16 = R2b[:, 16384:32768]
        XSQ32 = R2[:, 0:2048]
        TMP32 = R2[:, 2048:4096]
        Y8 = R2b
        W2X = R2b[:, 16384:32768]

        W1I = CS8i[:, 0:1024]
        SCB = CS8f[:, 1024:2048]
        SRi = SR.bitcast(I32)
        DQ_INT = [CS8i[:, 0:1024], SRi[:, 0:1024], SRi[:, 2048:3072]]
        DQ_SCL = [CS8f[:, 1024:2048], SR[:, 1024:2048], SR[:, 3072:4096]]
        COSR = CS8[:, 0:2048]
        SINR = CS8[:, 2048:4096]
        POOLPAD = CS8[:, 0:2048]
        DISHA = CS8[:, 2048:3072]

        ONE128 = CSM[:, 0:1]
        ONE128B = CSMb[:, 2:3]
        ONESROW = CSM[0:1, 4:132]
        ONESROW32 = CSM[32:33, 4:132]
        IDENT32 = CSM[0:32, 132:164]
        MASKNEG = CSM[:, 164:180]
        ONESSQ = CSM[:, 180:308]
        SEL16S = CSMb[0:16, 616:1640]
        PSEL = CSMb[:, 1640:1704]

        VC = {}
        _vc = [0]

        def vcol(name, n=1):
            VC[name] = VECS[:, _vc[0]:_vc[0] + n]
            _vc[0] += n

        for nm, n in (("bqT", 8), ("bkT", 8), ("boT", 8), ("wq0125T", 8),
                      ("wknT", 8), ("gammaT", 8), ("betaT", 8), ("hb2T", 8),
                      ("QmaxT", 8), ("KmaxT", 8), ("bo1T", 8), ("b1T", 32),
                      ("negMrhs", 16), ("negMb", 16), ("zeroT", 1), ("oneT", 1),
                      ("c0125T", 1)):
            vcol(nm, n)
        BV16 = VECB[:, 0:8]
        KMAX16 = VECB[:, 8:16]

        PS_A = stack.enter_context(nc.psum_tensor("PS_A", [128, 2048], F32))[:]
        PS_B = stack.enter_context(nc.psum_tensor("PS_B", [128, 2048], F32))[:]

        sems = {nm: Sem(stack, nc, nm)
                for nm in ("dW", "dS", "dX", "dO", "dP", "cT", "cV", "cS", "cP",
                           "dU0", "dU1", "dU2", "dDen", "dOD", "dSB")}
        dW, dS, dX, dO, dP = (sems[k] for k in ("dW", "dS", "dX", "dO", "dP"))
        cT, cV, cS, cP = (sems[k] for k in ("cT", "cV", "cS", "cP"))
        DU = [sems["dU0"], sems["dU1"], sems["dU2"]]
        dDen, dOD, dSB = sems["dDen"], sems["dOD"], sems["dSB"]

        PF = {}

        def xr_c(c):
            return XR[:, 2048 * c:2048 * (c + 1)]

        def xn_c(c):
            return XNT[:, 2048 * c:2048 * (c + 1)]

        def q_c(c):
            return Q16[:, 2048 * c:2048 * (c + 1)]

        def k_c(c):
            return K16[:, 2048 * c:2048 * (c + 1)]

        def load_wt(e, nm, buf=None, sem=None):
            buf = WTA if buf is None else buf
            sem = dS if sem is None else sem
            wbf = dram[nm + "_bf"]
            for c in range(NCH):
                sem.inc(e.dma_start(out=buf[:, 1024 * c:1024 * (c + 1)],
                                    in_=wbf[:, 128 * c:128 * (c + 1)],
                                    transpose=True), 16)
            return sem.n

        with nc.Block() as blk:
            @blk.sync
            def _(e):
                dW.inc(e.dma_start(out=CSM, in_=csmall), 16)

            @blk.gpsimd
            def _(e):
                for nm in tw_names:
                    dP.inc(e.dma_start(out=dram[nm + "_bf"], in_=dram[nm]), 16)
        cast_mark = dP.n

        vec_plan = []
        if 0 in layers:
            vec_plan += [("L0_attn_bq", "bqT", 8), ("L0_attn_bk", "bkT", 8),
                         ("L0_attn_bv", "_bv", 8), ("L0_attn_bo", "boT", 8),
                         ("L0_attn_norm_q_w", "wq0125T", 8),
                         ("L0_attn_norm_k_w", "wknT", 8)]
        if 1 in layers:
            vec_plan += [("L1_ffn_ln_gamma", "gammaT", 8),
                         ("L1_ffn_ln_beta", "betaT", 8),
                         ("L1_ffn_fc2_bias", "hb2T", 8),
                         ("L1_ffn_fc1_bias", "b1T", 32)]

        for g0 in range(0, len(vec_plan), 8):
            grp = vec_plan[g0:g0 + 8]
            with nc.Block() as blk:
                @blk.sync
                def _(e, grp=grp):
                    e.wait_ge(dW.h, 16)
                    for idx, (src, dst, rows) in enumerate(grp):
                        srcap = dram[src].rearrange("(a b) -> a b", b=128)
                        dX.inc(e.dma_start(
                            out=STG[0:rows, 128 * idx:128 * (idx + 1)],
                            in_=srcap), 16)
            ld_mark = dX.n
            with nc.Block() as blk:
                @blk.tensor
                def _(e, grp=grp):
                    e.wait_ge(dX.h, ld_mark)
                    for idx, (src, dst, rows) in enumerate(grp):
                        li = e.transpose(PS_A[:, 64 * idx:64 * idx + rows],
                                         STG[0:rows, 128 * idx:128 * (idx + 1)],
                                         IDENT32[0:rows, 0:rows])
                    cT.inc(li, 1)
            tr_mark = cT.n
            with nc.Block() as blk:
                @blk.vector
                def _(e, grp=grp):
                    e.wait_ge(cT.h, tr_mark)
                    for idx, (src, dst, rows) in enumerate(grp):
                        ps = PS_A[:, 64 * idx:64 * idx + rows]
                        if dst == "_bv":
                            e.tensor_copy(BV16, ps)
                        else:
                            e.tensor_copy(VC[dst], ps)

        with nc.Block() as blk:
            @blk.vector
            def _(e):
                if 0 in layers:
                    e.tensor_scalar_mul(VC["wq0125T"], VC["wq0125T"], 0.125)
                if 1 in layers:
                    e.tensor_scalar_mul(VC["hb2T"], VC["hb2T"], 0.5)
                e.memset(VC["zeroT"], 0.0)
                e.memset(VC["oneT"], 1.0)
                e.memset(VC["c0125T"], 0.125)
                e.memset(VC["bo1T"], 0.0)

        def phase_p1(b, layer, load_x):
            with nc.Block() as blk:
                dX0, cS0, cT0, cV0 = dX.n, cS.n, cT.n, cV.n
                if load_x:
                    @blk.sync
                    def _(e):
                        dX.inc(e.dma_start(
                            out=XR.rearrange("p (c t) -> p c t", t=S),
                            in_=xin[b].rearrange("(c p) t -> p c t", p=128)), 16)
                xload = dX.n

                XB = [R2b[:, 8192:10240], R2b[:, 10240:12288]]
                XSQ16 = [R2b[:, 12288:14336], R2b[:, 14336:16384]]

                @blk.scalar
                def _(e):
                    if load_x:
                        e.wait_ge(dX.h, xload)
                    for c in range(NCH):
                        if c >= 2:
                            e.wait_ge(cT.h, cT0 + 2 * (c - 2) + 2)
                        with nc.allow_low_precision(
                                reason="LN stats operands in bf16; fp32 "
                                       "accumulation (error ~1e-4 of sigma)"):
                            cS.inc(e.activation(XSQ16[c % 2], xr_c(c),
                                                AF.Square), 1)
                assert cS.n == cS0 + NCH

                @blk.vector
                def _(e):
                    if load_x:
                        e.wait_ge(dX.h, xload)
                    for c in range(NCH):
                        if c >= 2:
                            e.wait_ge(cT.h, cT0 + 2 * (c - 2) + 1)
                        with nc.allow_low_precision(
                                reason="LN stats operands in bf16"):
                            cV.inc(e.tensor_copy(XB[c % 2], xr_c(c)), 1)
                assert cV.n == cV0 + NCH

                @blk.tensor
                def _(e):
                    for c in range(NCH):
                        e.wait_ge(cV.h, cV0 + c + 1)
                        for j in range(4):
                            li = e.matmul(PS_A[0:1, 512 * j:512 * (j + 1)], ONE128B,
                                          XB[c % 2][:, 512 * j:512 * (j + 1)],
                                          start=(c == 0), stop=(c == NCH - 1))
                        cT.inc(li, 1)
                        e.wait_ge(cS.h, cS0 + c + 1)
                        for j in range(4):
                            li = e.matmul(PS_B[0:1, 512 * j:512 * (j + 1)], ONE128B,
                                          XSQ16[c % 2][:, 512 * j:512 * (j + 1)],
                                          start=(c == 0), stop=(c == NCH - 1))
                        cT.inc(li, 1)
                assert cT.n == cT0 + 2 * NCH

            with nc.Block() as blk:
                cV0, cS0, cT0 = cV.n, cS.n, cT.n

                @blk.vector
                def _(e):
                    T = PS_A[0:1, 0:2048]
                    e.tensor_copy(SRA, PS_A[0:1, :])
                    e.tensor_copy(SRB, PS_B[0:1, :])
                    e.tensor_scalar_mul(SRA, SRA, 1.0 / D)
                    e.tensor_scalar(SRB, SRB, 1.0 / D, EPS_LN,
                                    ALU.mult, ALU.add)
                    e.tensor_tensor(T, SRA, SRA, ALU.mult)
                    e.tensor_tensor(SRB, SRB, T, ALU.subtract)
                    cV.inc(e.reciprocal(T, SRB), 1)
                    e.wait_ge(cS.h, cS0 + 1)
                    cV.inc(e.tensor_tensor(SRA, SRA, SRB, ALU.mult), 1)
                    e.wait_ge(cT.h, cT0 + 1)
                    for c in range(NCH):
                        e.tensor_tensor(TMP32, xr_c(c), PS_A, ALU.mult)
                        li = e.tensor_tensor(xn_c(c), TMP32, PS_B, ALU.subtract)
                        if layer == 1:
                            li = e.tensor_tensor(xr_c(c), xr_c(c), xn_c(c), ALU.add)
                        if c == 1:
                            cV.inc(li, 1)
                    cV.inc(li, 1)

                @blk.scalar
                def _(e):
                    e.wait_ge(cV.h, cV0 + 1)
                    cS.inc(e.activation(SRB, PS_A[0:1, 0:2048], AF.Sqrt), 1)

                @blk.tensor
                def _(e):
                    e.wait_ge(cV.h, cV0 + 2)
                    for j in range(4):
                        e.matmul(PS_A[:, 512 * j:512 * (j + 1)], ONESROW,
                                 SRB[:, 512 * j:512 * (j + 1)],
                                 start=True, stop=True)
                    for j in range(4):
                        li = e.matmul(PS_B[:, 512 * j:512 * (j + 1)], ONESROW,
                                      SRA[:, 512 * j:512 * (j + 1)],
                                      start=True, stop=True)
                    cT.inc(li, 1)

                @blk.sync
                def _(e):
                    e.wait_ge(dP.h, cast_mark)
                    if layer == 1:
                        dS.inc(e.dma_start(out=COSR, in_=dram["cosrope"]), 16)
                        dS.inc(e.dma_start(out=SINR, in_=dram["sinrope"]), 16)
                    wq = "L0_attn_Wq" if layer == 0 else "L1_attn_wq"
                    PF["wq"] = load_wt(e, wq)
                    if layer == 0:
                        if load_x:
                            e.wait_ge(dX.h, xload)
                    else:
                        e.wait_ge(cV.h, cV0 + 3)
                    dX.inc(e.dma_start(
                        out=xspill[0:256].rearrange("(c p) t -> p c t", p=128),
                        in_=XR[:, 0:2 * S].rearrange("p (c t) -> p c t", t=S)),
                        16)
                    sp01 = dX.n
                    wk = "L0_attn_Wk" if layer == 0 else "L1_attn_wk"
                    e.wait_ge(dS.h, PF["wq"])
                    e.wait_ge(dX.h, sp01)
                    e.wait_ge(cV.h, cV0 + 3)
                    PF["wk"] = load_wt(e, wk, WTB, dSB)
                    if layer == 1:
                        e.wait_ge(cV.h, cV0 + 4)
                    dX.inc(e.dma_start(
                        out=xspill[256:1024].rearrange("(c p) t -> p c t",
                                                       p=128),
                        in_=XR[:, 2 * S:8 * S].rearrange("p (c t) -> p c t",
                                                         t=S)), 16)
                    PF["spill"] = dX.n
            return PF["spill"]

        def emit_feature_proj(blk, wname, dst_c, evac_kind, evac_args,
                              preloaded=None, wbuf=None, wsem=None,
                              extra_load=None, vector_fn=None):
            wbuf = WTA if wbuf is None else wbuf
            wsem = dS if wsem is None else wsem
            with blk:
                if vector_fn is not None:
                    @blk.vector
                    def _(e):
                        vector_fn(e)
                box = {}
                if preloaded is None or extra_load is not None:
                    @blk.sync
                    def _(e):
                        if preloaded is None:
                            e.wait_ge(dP.h, cast_mark)
                            box["m"] = load_wt(e, wname, wbuf, wsem)
                        if extra_load is not None:
                            key, nm2, buf2, sem2 = extra_load
                            PF[key] = load_wt(e, nm2, buf2, sem2)
                wt_mark = box.get("m", preloaded)
                cT0, cS0 = cT.n, cS.n

                @blk.tensor
                def _(e):
                    e.wait_ge(wsem.h, wt_mark)
                    for oc in range(NCH):
                        if oc >= 2:
                            e.wait_ge(cS.h, cS0 + oc - 1)
                        ps = PS_A if oc % 2 == 0 else PS_B
                        for dc in range(NCH):
                            lhsT = wbuf[:, 1024 * dc + 128 * oc:
                                        1024 * dc + 128 * (oc + 1)]
                            for j in range(4):
                                li = e.matmul(ps[:, 512 * j:512 * (j + 1)], lhsT,
                                              xn_c(dc)[:, 512 * j:512 * (j + 1)],
                                              start=(dc == 0), stop=(dc == NCH - 1))
                        cT.inc(li, 1)
                assert cT.n == cT0 + NCH

                @blk.scalar
                def _(e):
                    for oc in range(NCH):
                        e.wait_ge(cT.h, cT0 + oc + 1)
                        ps = PS_A if oc % 2 == 0 else PS_B
                        if evac_kind == "bias":
                            cS.inc(e.activation(dst_c(oc), ps, AF.Identity,
                                                bias=evac_args[:, oc:oc + 1],
                                                scale=VC["oneT"]), 1)
                        elif evac_kind == "scale0125":
                            cS.inc(e.activation(dst_c(oc), ps, AF.Identity,
                                                bias=VC["zeroT"],
                                                scale=VC["c0125T"]), 1)
                        else:
                            cS.inc(e.activation(dst_c(oc), ps, AF.Copy), 1)
                assert cS.n == cS0 + NCH

        KMAXD = SRb[:, 6400:6528]

        def emit_kmax(e, c):
            e.reduce_max(KMAX16[:, c:c + 1], k_c(c), axis=AXX,
                         apply_absolute_value=True)
            h = 2 * c
            e.tensor_copy(KMAXD[0:64, 16 * c + h:16 * c + h + 1],
                          KMAX16[0:64, c:c + 1])
            cV.inc(e.tensor_copy(
                KMAXD[64:128, 16 * c + h + 1:16 * c + h + 2],
                KMAX16[64:128, c:c + 1]), 1)
            PF["kmaxd"][c] = cV.n

        def phase_p2(b, layer, spill_mark):
            PF["kmaxd"] = {}
            if layer == 0:
                wq, wk, wv = "L0_attn_Wq", "L0_attn_Wk", "L0_attn_Wv"
            else:
                wq, wk, wv = "L1_attn_wq", "L1_attn_wk", "L1_attn_wv"

            ROPE_MASK = []
            for i in range(16):
                ROPE_MASK += [2 * i + 1, 2 * i]

            def rope_chunk(e, tgt, c):
                e.stream_shuffle(TMPR, tgt(c), ROPE_MASK)
                e.tensor_tensor(TMPR, TMPR, SINR, ALU.mult)
                e.tensor_tensor(tgt(c), tgt(c), COSR, ALU.mult)
                e.tensor_tensor(tgt(c), tgt(c), TMPR, ALU.add)

            def rope_q(e):
                e.wait_ge(dS.h, dS.n)
                e.wait_ge(dX.h, spill_mark)
                for c in range(NCH):
                    rope_chunk(e, q_c, c)

            emit_feature_proj(nc.Block(), wq, q_c,
                              "bias" if layer == 0 else "scale0125",
                              VC["bqT"] if layer == 0 else None,
                              preloaded=PF.get("wq"))
            emit_feature_proj(nc.Block(), wk, k_c,
                              "bias" if layer == 0 else "copy",
                              VC["bkT"] if layer == 0 else None,
                              preloaded=PF.get("wk"), wbuf=WTB, wsem=dSB,
                              extra_load=("wv", wv, WTA, dS),
                              vector_fn=rope_q if layer == 1 else None)

            with nc.Block() as blk:
                wt_mark = PF["wv"]
                cT0, cS0 = cT.n, cS.n

                @blk.gpsimd
                def _(e):
                    e.wait_ge(dX.h, spill_mark)
                    cP.inc(e.memset(VA[:, 64::65], 1.0), 1)

                @blk.tensor
                def _(e):
                    e.wait_ge(dS.h, wt_mark)
                    for tc in range(NTC):
                        if tc >= 2:
                            e.wait_ge(cS.h, cS0 + tc - 1)
                        ps = PS_A if tc % 2 == 0 else PS_B
                        for dc in range(NCH):
                            lhsT = XNT[:, 2048 * dc + 128 * tc:
                                       2048 * dc + 128 * (tc + 1)]
                            for j in range(2):
                                li = e.matmul(ps[:, 512 * j:512 * (j + 1)], lhsT,
                                              WTA[:, 1024 * dc + 512 * j:
                                                  1024 * dc + 512 * (j + 1)],
                                              start=(dc == 0), stop=(dc == NCH - 1))
                        cT.inc(li, 1)

                if layer == 1:
                    @blk.vector
                    def _(e):
                        with nc.allow_low_precision(
                                reason="softmax bound only (KMAXD)"):
                            e.memset(KMAXD, 0.0)
                            for c in range(NCH):
                                rope_chunk(e, k_c, c)
                                emit_kmax(e, c)

                @blk.scalar
                def _(e):
                    e.wait_ge(dX.h, spill_mark)
                    for tc in range(NTC):
                        e.wait_ge(cT.h, cT0 + tc + 1)
                        ps = PS_A if tc % 2 == 0 else PS_B
                        va_t = bass.AP(VA.tensor, VA.offset + 1040 * tc,
                                       [list(VA.ap[0]), [65, 16], [1, 64]])
                        cS.inc(e.activation(
                            va_t, ps[:, 0:1024].rearrange("p (h j) -> p h j", j=64),
                            AF.Copy), 1)

            if layer == 0:
                with nc.Block() as blk:
                    cS0, cT0 = cS.n, cT.n

                    @blk.scalar
                    def _(e):
                        for c in range(NCH):
                            if c >= 1:
                                e.wait_ge(cT.h, cT0 + 2 * c - 1)
                            cS.inc(e.activation(QSQ, q_c(c), AF.Square), 1)
                            if c >= 1:
                                e.wait_ge(cT.h, cT0 + 2 * c)
                            cS.inc(e.activation(TMPR, k_c(c), AF.Square), 1)

                    @blk.tensor
                    def _(e):
                        for c in range(NCH):
                            e.wait_ge(cS.h, cS0 + 2 * c + 1)
                            for j in range(4):
                                li = e.matmul(PS_A[0:1, 512 * j:512 * (j + 1)],
                                              ONE128B, QSQ[:, 512 * j:512 * (j + 1)],
                                              start=(c == 0), stop=(c == NCH - 1))
                            cT.inc(li, 1)
                            e.wait_ge(cS.h, cS0 + 2 * c + 2)
                            for j in range(4):
                                li = e.matmul(PS_B[0:1, 512 * j:512 * (j + 1)],
                                              ONE128B, TMPR[:, 512 * j:512 * (j + 1)],
                                              start=(c == 0), stop=(c == NCH - 1))
                            cT.inc(li, 1)

                with nc.Block() as blk:
                    cV0, cS0, cT0 = cV.n, cS.n, cT.n

                    @blk.vector
                    def _(e):
                        T = PS_A[0:1, 0:2048]
                        T2 = PS_B[0:1, 0:2048]
                        e.tensor_copy(SRA, PS_A[0:1, :])
                        e.tensor_copy(SRB, PS_B[0:1, :])
                        e.tensor_scalar(SRA, SRA, 1.0 / D, EPS_RMS,
                                        ALU.mult, ALU.add)
                        e.tensor_scalar(SRB, SRB, 1.0 / D, EPS_RMS,
                                        ALU.mult, ALU.add)
                        e.reciprocal(T, SRA)
                        cV.inc(e.reciprocal(T2, SRB), 1)
                        e.wait_ge(cT.h, cT0 + 1)
                        with nc.allow_low_precision(
                                reason="softmax bound only (KMAXD)"):
                            e.memset(KMAXD, 0.0)
                            for c in range(NCH):
                                e.scalar_tensor_tensor(
                                    q_c(c), q_c(c), VC["wq0125T"][:, c:c + 1],
                                    PS_A, ALU.mult, ALU.mult)
                                e.scalar_tensor_tensor(
                                    k_c(c), k_c(c), VC["wknT"][:, c:c + 1],
                                    PS_B, ALU.mult, ALU.mult)
                                emit_kmax(e, c)

                    @blk.scalar
                    def _(e):
                        e.wait_ge(cV.h, cV0 + 1)
                        e.activation(SRA, PS_A[0:1, 0:2048], AF.Sqrt)
                        cS.inc(e.activation(SRB, PS_B[0:1, 0:2048],
                                            AF.Sqrt), 1)

                    @blk.tensor
                    def _(e):
                        e.wait_ge(cS.h, cS0 + 1)
                        for j in range(4):
                            e.matmul(PS_A[:, 512 * j:512 * (j + 1)], ONESROW,
                                     SRA[:, 512 * j:512 * (j + 1)],
                                     start=True, stop=True)
                        for j in range(4):
                            li = e.matmul(PS_B[:, 512 * j:512 * (j + 1)], ONESROW,
                                          SRB[:, 512 * j:512 * (j + 1)],
                                          start=True, stop=True)
                        cT.inc(li, 1)
            MROW = SR[0:1, 4080:4096]
            KMCOL = SR[0:16, 4079:4080]
            with nc.Block() as blk:
                cV0, cT0, cS0 = cV.n, cT.n, cS.n
                dDen0 = dDen.n

                @blk.vector
                def _(e):
                    with nc.allow_low_precision(reason="softmax bound only"):
                        e.wait_ge(cT.h, cT0 + NCH)
                        cV.inc(e.reduce_max(KMCOL, PS_B[0:16, :], axis=AXX), 1)
                        e.wait_ge(cT.h, cT0 + NCH + 1)
                        e.tensor_scalar_mul(VC["negMb"], PS_A[:, 0:16], -1.0)

                @blk.scalar
                def _(e):
                    for c in range(NCH):
                        if c >= 2:
                            e.wait_ge(cT.h, cT0 + c - 1)
                        buf = QSQ if c % 2 == 0 else TMPR
                        cS.inc(e.activation(buf, q_c(c), AF.Abs), 1)

                @blk.gpsimd
                def _(e):
                    e.wait_ge(cV.h, cV0 + 1)
                    dDen.inc(e.dma_start(out=MROW, in_=KMCOL), 16)

                @blk.tensor
                def _(e):
                    for c in range(NCH):
                        e.wait_ge(cS.h, cS0 + c + 1)
                        buf = QSQ if c % 2 == 0 else TMPR
                        for j in range(4):
                            li = e.matmul(PS_B[0:16, 512 * j:512 * (j + 1)],
                                          KMAXD[:, 16 * c:16 * (c + 1)],
                                          buf[:, 512 * j:512 * (j + 1)],
                                          start=(c == 0), stop=(c == NCH - 1))
                        cT.inc(li, 1)
                    e.wait_ge(dDen.h, dDen0 + 16)
                    cT.inc(e.matmul(PS_A[:, 0:16], ONESROW, MROW,
                                    start=True, stop=True), 1)

        def phase_p25(b):
            with nc.Block() as blk:
                cS0, cT0 = cS.n, cT.n

                @blk.scalar
                def _(e):
                    for c in range(NCH):
                        if c >= 1:
                            e.wait_ge(cT.h, cT0 + 2 * c)
                        cS.inc(e.activation(QSQ, xn_c(c), AF.Square), 1)

                @blk.tensor
                def _(e):
                    for c in range(NCH):
                        for j in range(4):
                            li = e.matmul(PS_A[0:1, 512 * j:512 * (j + 1)], ONE128B,
                                          xn_c(c)[:, 512 * j:512 * (j + 1)],
                                          start=(c == 0), stop=(c == NCH - 1))
                        cT.inc(li, 1)
                        e.wait_ge(cS.h, cS0 + c + 1)
                        for j in range(4):
                            li = e.matmul(PS_B[0:1, 512 * j:512 * (j + 1)], ONE128B,
                                          QSQ[:, 512 * j:512 * (j + 1)],
                                          start=(c == 0), stop=(c == NCH - 1))
                        cT.inc(li, 1)

            with nc.Block() as blk:
                cV0, cT0, cS0 = cV.n, cT.n, cS.n

                @blk.vector
                def _(e):
                    T = PS_A[0:1, 0:2048]
                    e.tensor_copy(SRA, PS_A[0:1, :])
                    e.tensor_copy(SRB, PS_B[0:1, :])
                    e.tensor_scalar_mul(SRA, SRA, 1.0 / D)
                    e.tensor_scalar(SRB, SRB, 1.0 / D, EPS_LN,
                                    ALU.mult, ALU.add)
                    e.tensor_tensor(T, SRA, SRA, ALU.mult)
                    e.tensor_tensor(SRB, SRB, T, ALU.subtract)
                    cV.inc(e.reciprocal(T, SRB), 1)
                    e.wait_ge(cS.h, cS0 + 1)
                    cV.inc(e.tensor_tensor(SRA, SRA, SRB, ALU.mult), 1)
                    e.wait_ge(cT.h, cT0 + 1)
                    for c in range(NCH):
                        if c >= 1:
                            e.wait_ge(cS.h, cS0 + 1 + c)
                        e.tensor_tensor(TMPR, xn_c(c), PS_A, ALU.mult)
                        cV.inc(e.tensor_tensor(TMPR, TMPR, PS_B, ALU.subtract), 1)

                @blk.scalar
                def _(e):
                    e.wait_ge(cV.h, cV0 + 1)
                    cS.inc(e.activation(SRB, PS_A[0:1, 0:2048], AF.Sqrt), 1)
                    for c in range(NCH):
                        e.wait_ge(cV.h, cV0 + 3 + c)
                        cS.inc(e.activation(xn_c(c), TMPR, AF.Identity,
                                            bias=VC["betaT"][:, c:c + 1],
                                            scale=VC["gammaT"][:, c:c + 1]), 1)

                @blk.tensor
                def _(e):
                    e.wait_ge(cV.h, cV0 + 2)
                    for j in range(4):
                        e.matmul(PS_A[:, 512 * j:512 * (j + 1)], ONESROW,
                                 SRB[:, 512 * j:512 * (j + 1)],
                                 start=True, stop=True)
                    for j in range(4):
                        li = e.matmul(PS_B[:, 512 * j:512 * (j + 1)], ONESROW,
                                      SRA[:, 512 * j:512 * (j + 1)],
                                      start=True, stop=True)
                    cT.inc(li, 1)

        def phase_p3(b, layer):
            NU = NH * NTC
            STA, STB = PS_A[:, 0:1024], PS_A[:, 1024:2048]
            O_A, O_B = PS_B[0:65, 0:1024], PS_B[0:65, 1024:2048]
            PTA, PTB = [PT[0], PT[1]], [PT[2], PT[3]]

            with nc.Block() as blk:
                if layer == 0:
                    @blk.sync
                    def _(e):
                        PF["ffnw"] = load_wt(e, "L0_ffn_weight")
                else:
                    PF["du"] = {}

                    @blk.sync
                    def _(e):
                        q1 = dram["L1_ffn_fc1_q"]
                        s1 = dram["L1_ffn_fc1_scales"]
                        src = bass.AP(q1.tensor, q1.offset,
                                      [[4096, 128], [4096 * 128, 8], [1, 128]])
                        DU[0].inc(e.dma_start(out=DQ_INT[0], in_=src), 16)
                        ssrc = bass.AP(s1.tensor, s1.offset,
                                       [[0, 128], [4096, 8], [1, 128]])
                        DU[0].inc(e.dma_start(out=DQ_SCL[0], in_=ssrc), 16)
                        PF["du"][0] = DU[0].n
                        src = bass.AP(q1.tensor, q1.offset + 128,
                                      [[4096, 128], [4096 * 128, 8], [1, 128]])
                        DU[2].inc(e.dma_start(out=DQ_INT[2], in_=src), 16)
                        ssrc = bass.AP(s1.tensor, s1.offset + 128,
                                       [[0, 128], [4096, 8], [1, 128]])
                        DU[2].inc(e.dma_start(out=DQ_SCL[2], in_=ssrc), 16)
                        PF["du"][2] = DU[2].n
                cT0, cS0, cV0 = cT.n, cS.n, cV.n
                dDen0, dOD0 = dDen.n, dOD.n
                stA_mark = {}
                stB_mark = {}
                avA_mark = {}
                avB_mark = {}

                def uhc(u):
                    return 2 * (u // 32), (u // 16) % 2, u % 16

                def expA_mark(u):
                    return cS0 + 2 * u + 1

                def expB_mark(u):
                    return cS0 + 2 * u + 2

                def denA_mark(m):
                    return cV0 + 4 * m + 1

                def evacA_mark(m):
                    return cV0 + 4 * m + 2

                def denB_mark(m):
                    return cV0 + 4 * m + 3

                def evacB_mark(m):
                    return cV0 + 4 * m + 4

                def denAdma_mark(m):
                    return dDen0 + 32 * m + 16

                def denBdma_mark(m):
                    return dDen0 + 32 * m + 32

                def odma_mark(m):
                    return dOD0 + 16 * (m + 1)

                @blk.tensor
                def _(e):
                    def emit_st(u):
                        hA, H, c = uhc(u)
                        ch = 2048 * (u // 32)
                        if u >= 1:
                            e.wait_ge(cS.h, expA_mark(u - 1))
                        lhsT = K16[0:64, ch + 128 * c:ch + 128 * (c + 1)]
                        rhs = Q16[0:64, ch + 1024 * H:ch + 1024 * (H + 1)]
                        e.matmul(STA[:, 0:512], lhsT, rhs[:, 0:512],
                                 start=True, stop=True)
                        li = e.matmul(STA[:, 512:1024], lhsT, rhs[:, 512:1024],
                                      start=True, stop=True)
                        cT.inc(li, 1)
                        stA_mark[u] = cT.n
                        if u >= 1:
                            e.wait_ge(cS.h, expB_mark(u - 1))
                        lhsT = K16[64:128, ch + 128 * c:ch + 128 * (c + 1)]
                        rhs = Q16[64:128, ch + 1024 * H:ch + 1024 * (H + 1)]
                        e.matmul(STB[:, 0:512], lhsT, rhs[:, 0:512],
                                 start=True, stop=True)
                        li = e.matmul(STB[:, 512:1024], lhsT, rhs[:, 512:1024],
                                      start=True, stop=True)
                        cT.inc(li, 1)
                        stB_mark[u] = cT.n

                    def emit_av(u):
                        hA, H, c = uhc(u)
                        m = u // 16
                        e.wait_ge(cS.h, expA_mark(u))
                        if c == 0 and m >= 1:
                            e.wait_ge(cV.h, evacA_mark(m - 1))
                        lhsT = VA[:, 1040 * c + 65 * hA:1040 * c + 65 * (hA + 1)]
                        rhs = PTA[u % 2]
                        e.matmul(O_A[:, 0:512], lhsT, rhs[:, 0:512],
                                 start=(c == 0), stop=(c == 15))
                        li = e.matmul(O_A[:, 512:1024], lhsT, rhs[:, 512:1024],
                                      start=(c == 0), stop=(c == 15))
                        cT.inc(li, 1)
                        avA_mark[u] = cT.n
                        e.wait_ge(cS.h, expB_mark(u))
                        if c == 0 and m >= 1:
                            e.wait_ge(cV.h, evacB_mark(m - 1))
                        lhsT = VA[:, 1040 * c + 65 * (hA + 1):1040 * c + 65 * (hA + 2)]
                        rhs = PTB[u % 2]
                        e.matmul(O_B[:, 0:512], lhsT, rhs[:, 0:512],
                                 start=(c == 0), stop=(c == 15))
                        li = e.matmul(O_B[:, 512:1024], lhsT, rhs[:, 512:1024],
                                      start=(c == 0), stop=(c == 15))
                        cT.inc(li, 1)
                        avB_mark[u] = cT.n

                    emit_st(0)
                    for u in range(NU):
                        if u + 1 < NU:
                            emit_st(u + 1)
                        emit_av(u)
                assert cT.n == cT0 + 4 * NU

                @blk.scalar
                def _(e):
                    for u in range(NU):
                        hA, H, c = uhc(u)
                        e.wait_ge(cT.h, stA_mark[u])
                        if u >= 2:
                            e.wait_ge(cT.h, avA_mark[u - 2])
                        cS.inc(e.activation(PTA[u % 2], STA, AF.Exp,
                                            bias=VC["negMb"][:, hA:hA + 1],
                                            scale=1.0), 1)
                        e.wait_ge(cT.h, stB_mark[u])
                        if u >= 2:
                            e.wait_ge(cT.h, avB_mark[u - 2])
                        cS.inc(e.activation(PTB[u % 2], STB, AF.Exp,
                                            bias=VC["negMb"][:, hA + 1:hA + 2],
                                            scale=1.0), 1)
                assert cS.n == cS0 + 2 * NU

                @blk.vector
                def _(e):
                    prev_m = None
                    for m in range(NU // 16):
                        u = m * 16 + 15
                        hA, H, c = uhc(u)
                        ch = 2048 * (u // 32)
                        e.wait_ge(cT.h, avA_mark[u])
                        if m >= 1:
                            e.wait_ge(dDen.h, denAdma_mark(m - 1))
                        cV.inc(e.tensor_copy(SR[64:65, 0:1024],
                                             O_A[64:65, :]), 1)
                        assert cV.n == denA_mark(m)
                        dstA = Q16[0:64, ch + 1024 * H:ch + 1024 * (H + 1)]
                        cV.inc(e.tensor_copy(dstA, O_A[0:64, :]), 1)
                        assert cV.n == evacA_mark(m)
                        e.wait_ge(cT.h, avB_mark[u])
                        if m >= 1:
                            e.wait_ge(dDen.h, denBdma_mark(m - 1))
                        cV.inc(e.tensor_copy(SR[64:65, 1024:2048],
                                             O_B[64:65, :]), 1)
                        assert cV.n == denB_mark(m)
                        if prev_m is not None:
                            e.wait_ge(dOD.h, odma_mark(prev_m))
                        prev_m = m
                        cV.inc(e.tensor_copy(OST, O_B[0:64, :]), 1)
                        assert cV.n == evacB_mark(m)
                assert cV.n == cV0 + 4 * (NU // 16)

                @blk.gpsimd
                def _(e):
                    for m in range(NU // 16):
                        u = m * 16 + 15
                        hA = 2 * (u // 32)
                        H = (u // 16) % 2
                        ch = 2048 * (u // 32)
                        e.wait_ge(cV.h, denA_mark(m))
                        dDen.inc(e.dma_start(
                            out=DEN[hA:hA + 1, 1024 * H:1024 * (H + 1)],
                            in_=SR[64:65, 0:1024]), 16)
                        assert dDen.n == denAdma_mark(m)
                        e.wait_ge(cV.h, denB_mark(m))
                        dDen.inc(e.dma_start(
                            out=DEN[hA + 1:hA + 2, 1024 * H:1024 * (H + 1)],
                            in_=SR[64:65, 1024:2048]), 16)
                        assert dDen.n == denBdma_mark(m)
                        e.wait_ge(cV.h, evacB_mark(m))
                        dOD.inc(e.dma_start(
                            out=Q16[64:128, ch + 1024 * H:ch + 1024 * (H + 1)],
                            in_=OST), 16)
                        assert dOD.n == odma_mark(m)

        def phase_p35(b, layer):
            with nc.Block() as blk:
                cV0, cT0 = cV.n, cT.n
                reload_mark = dX.n + 16

                @blk.vector
                def _(e):
                    e.wait_ge(dDen.h, dDen.n)
                    e.wait_ge(dOD.h, dOD.n)
                    with nc.allow_low_precision(
                            reason="bf16 softmax denominators; matmul operands "
                                   "are bf16 throughout"):
                        cV.inc(e.reciprocal(RDEN, DEN), 1)
                    for c in range(NCH):
                        e.wait_ge(cT.h, cT0 + c + 1)
                        ps = PS_A if c % 2 == 0 else PS_B
                        cV.inc(e.tensor_tensor(q_c(c), q_c(c), ps, ALU.mult), 1)
                    if layer == 1:
                        e.wait_ge(dX.h, reload_mark)
                        for c in range(NCH):
                            e.scalar_tensor_tensor(
                                xr_c(c), q_c(c), VC["hb2T"][:, c:c + 1], xr_c(c),
                                ALU.add, ALU.add)

                @blk.tensor
                def _(e):
                    e.wait_ge(cV.h, cV0 + 1)
                    for c in range(NCH):
                        if c >= 2:
                            e.wait_ge(cV.h, cV0 + c)
                        ps = PS_A if c % 2 == 0 else PS_B
                        for j in range(4):
                            li = e.matmul(ps[:, 512 * j:512 * (j + 1)],
                                          SEL16S[:, 128 * c:128 * (c + 1)],
                                          RDEN[:, 512 * j:512 * (j + 1)],
                                          start=True, stop=True)
                        cT.inc(li, 1)

                @blk.sync
                def _(e):
                    e.wait_ge(cV.h, cV0 + 1 + NCH)
                    dX.inc(e.dma_start(
                        out=XR.rearrange("p (c t) -> p c t", t=S),
                        in_=xspill.rearrange("(c p) t -> p c t", p=128)), 16)
                    assert dX.n == reload_mark

                @blk.gpsimd
                def _(e):
                    e.wait_ge(dX.h, dX.n)

        def phase_p4_l0(b):
            with nc.Block() as blk:
                cT0, cP0, dP0 = cT.n, cP.n, dP.n

                @blk.gpsimd
                def _(e):
                    cP.inc(e.memset(POOLPAD, 0.0), 1)
                    cP.inc(e.memset(DISHA, 0.0), 1)
                    dP.inc(e.dma_start(out=DISHA[0:64, :],
                                       in_=dram["L0_ffn_disha"]), 16)

                @blk.tensor
                def _(e):
                    for dc in range(NCH):
                        for j in range(4):
                            li = e.matmul(PS_A[0:64, 512 * j:512 * (j + 1)], PSEL,
                                          xn_c(dc)[:, 512 * j:512 * (j + 1)],
                                          start=(dc == 0), stop=(dc == NCH - 1))
                    cT.inc(li, 1)

                @blk.scalar
                def _(e):
                    e.wait_ge(cT.h, cT0 + 1)
                    e.wait_ge(cP.h, cP0 + 1)
                    cS.inc(e.activation(POOLPAD[0:64, :], PS_A[0:64, :],
                                        AF.Copy), 1)

            with nc.Block() as blk:
                cT0, cV0 = cT.n, cV.n

                @blk.tensor
                def _(e):
                    e.wait_ge(dS.h, dS.n)
                    e.wait_ge(dP.h, dP.n)
                    for oc in range(NCH):
                        if oc >= 2:
                            e.wait_ge(cV.h, cV0 + oc - 1)
                        ps = PS_A if oc % 2 == 0 else PS_B
                        for dc in range(NCH):
                            lhsT = WTA[:, 1024 * dc + 128 * oc:
                                       1024 * dc + 128 * (oc + 1)]
                            for j in range(4):
                                e.matmul(ps[:, 512 * j:512 * (j + 1)], lhsT,
                                         xn_c(dc)[:, 512 * j:512 * (j + 1)],
                                         start=(dc == 0), stop=False)
                        lhsT = DISHA[:, 128 * oc:128 * (oc + 1)]
                        for j in range(4):
                            li = e.matmul(ps[:, 512 * j:512 * (j + 1)], lhsT,
                                          POOLPAD[:, 512 * j:512 * (j + 1)],
                                          start=False, stop=True)
                        cT.inc(li, 1)

                @blk.vector
                def _(e):
                    for oc in range(NCH):
                        e.wait_ge(cT.h, cT0 + oc + 1)
                        ps = PS_A if oc % 2 == 0 else PS_B
                        cV.inc(e.tensor_tensor(xr_c(oc), xr_c(oc), ps,
                                               ALU.add), 1)

            with nc.Block() as blk:
                @blk.sync
                def _(e):
                    load_wt(e, "L0_attn_Wo")
                wt_mark = dS.n
                cT0 = cT.n

                @blk.tensor
                def _(e):
                    e.wait_ge(dS.h, wt_mark)
                    if b == 0:
                        for oc in range(NCH):
                            for dc in range(NCH):
                                lhsT = WTA[:, 1024 * dc + 128 * oc:
                                           1024 * dc + 128 * (oc + 1)]
                                li = e.matmul(PS_A[:, 64 * oc:64 * oc + 1], lhsT,
                                              BV16[:, dc:dc + 1],
                                              start=(dc == 0), stop=(dc == NCH - 1))
                        cT.inc(li, 1)

                @blk.vector
                def _(e):
                    if b == 0:
                        e.wait_ge(cT.h, cT0 + 1)
                        ps = bass.AP(PS_A.tensor, PS_A.offset,
                                     [list(PS_A.ap[0]), [64, 8]])
                        e.tensor_tensor(VC["bo1T"], VC["boT"], ps, ALU.add)

            with nc.Block() as blk:
                cT0, cV0 = cT.n, cV.n

                @blk.tensor
                def _(e):
                    for oc in range(NCH):
                        if oc >= 2:
                            e.wait_ge(cV.h, cV0 + oc - 1)
                        ps = PS_A if oc % 2 == 0 else PS_B
                        for dc in range(NCH):
                            lhsT = WTA[:, 1024 * dc + 128 * oc:
                                       1024 * dc + 128 * (oc + 1)]
                            for j in range(4):
                                li = e.matmul(ps[:, 512 * j:512 * (j + 1)], lhsT,
                                              Q16[:, 2048 * dc + 512 * j:
                                                  2048 * dc + 512 * (j + 1)],
                                              start=(dc == 0), stop=(dc == NCH - 1))
                        cT.inc(li, 1)

                @blk.vector
                def _(e):
                    for oc in range(NCH):
                        e.wait_ge(cT.h, cT0 + oc + 1)
                        ps = PS_A if oc % 2 == 0 else PS_B
                        cV.inc(e.scalar_tensor_tensor(
                            xr_c(oc), ps, VC["bo1T"][:, oc:oc + 1], xr_c(oc),
                            ALU.add, ALU.add), 1)

        def phase_p4_l1(b):
            q1 = dram["L1_ffn_fc1_q"]
            s1 = dram["L1_ffn_fc1_scales"]
            q2 = dram["L1_ffn_fc2_q"]
            s2 = dram["L1_ffn_fc2_scales"]
            with nc.Block() as blk:
                cV0, cT0, cS0 = cV.n, cT.n, cS.n
                du_mark = dict(PF.get("du", {}))

                def _vpos_stt(f):
                    sbk, fi = f // 8, f % 8
                    if sbk == 0:
                        return 2 * fi
                    base = {1: 16, 2: 48, 3: 80}[sbk]
                    return base + 2 * fi + (16 if fi >= 2 else 0)

                def stt1_mark(f):
                    return cV0 + _vpos_stt(f) + 1

                def stt2_mark(f):
                    return cV0 + _vpos_stt(f) + 2

                def zadd_mark(sbk, oc, th):
                    zbase = {0: 20, 1: 52, 2: 84, 3: 112}[sbk]
                    return cV0 + zbase + 2 * oc + th + 1

                def ymm_mark(f, th):
                    sbk = f // 8
                    return cT0 + 32 * sbk + 2 * (f % 8) + th + 1

                def zmm_mark(sbk, oc, th):
                    return cT0 + 32 * sbk + 16 + 2 * oc + th + 1

                def silu_mark(f, th):
                    sbk = f // 8
                    return cS0 + 16 * sbk + 2 * (f % 8) + th + 1

                def stt_i_mark(i):
                    return stt1_mark(i // 2) if i % 2 == 0 else stt2_mark(i // 2)

                @blk.sync
                def _(e):
                    for i in [1] + list(range(3, 64)):
                        f, w2flag = i // 2, i % 2
                        if i >= 3:
                            e.wait_ge(cV.h, stt_i_mark(i - 3))
                        if not w2flag:
                            src = bass.AP(q1.tensor, q1.offset + 128 * f,
                                          [[4096, 128], [4096 * 128, 8], [1, 128]])
                            DU[i % 3].inc(e.dma_start(out=DQ_INT[i % 3], in_=src), 16)
                            ssrc = bass.AP(s1.tensor, s1.offset + 128 * f,
                                           [[0, 128], [4096, 8], [1, 128]])
                            DU[i % 3].inc(e.dma_start(out=DQ_SCL[i % 3], in_=ssrc), 16)
                        else:
                            DU[i % 3].inc(e.dma_start(
                                out=DQ_INT[i % 3], in_=q2[128 * f:128 * (f + 1), :]), 16)
                            ssrc2 = bass.AP(s2.tensor, s2.offset + 1024 * f,
                                            [[0, 128], [1, 1024]])
                            DU[i % 3].inc(e.dma_start(out=DQ_SCL[i % 3], in_=ssrc2), 16)
                        du_mark[i] = DU[i % 3].n

                @blk.vector
                def _(e):
                    def emit_stt_pair(f):
                        sbk = f // 8
                        i1, i2 = 2 * f, 2 * f + 1
                        e.wait_ge(DU[i1 % 3].h, du_mark[i1])
                        if f >= 2:
                            e.wait_ge(cT.h, ymm_mark(f - 2, 1))
                        w1f = WTA[:, 4096 * (f % 2):4096 * (f % 2) + 1024]
                        cV.inc(e.scalar_tensor_tensor(
                            w1f, DQ_INT[i1 % 3], -128.0, DQ_SCL[i1 % 3],
                            ALU.add, ALU.mult), 1)
                        e.wait_ge(DU[i2 % 3].h, du_mark[i2])
                        if sbk >= 2:
                            e.wait_ge(cT.h, zmm_mark(sbk - 2, 7, 1))
                        slot = (sbk % 2) * 8 + (f % 8)
                        w2sb = W2X[:, 1024 * slot:1024 * (slot + 1)]
                        cV.inc(e.scalar_tensor_tensor(
                            w2sb, DQ_INT[i2 % 3], -128.0, DQ_SCL[i2 % 3],
                            ALU.add, ALU.mult), 1)

                    def emit_zadds(sbk):
                        for oc in range(NCH):
                            for th in range(2):
                                e.wait_ge(cT.h, zmm_mark(sbk, oc, th))
                                dst = XR[:, 2048 * oc + 1024 * th:
                                         2048 * oc + 1024 * (th + 1)]
                                cV.inc(e.scalar_tensor_tensor(
                                    dst, PS_B[:, 1024 * th:1024 * (th + 1)],
                                    0.5, dst, ALU.mult, ALU.add), 1)

                    for sbk in range(4):
                        emit_stt_pair(8 * sbk)
                        emit_stt_pair(8 * sbk + 1)
                        if sbk >= 1:
                            emit_zadds(sbk - 1)
                        for fi in range(2, 8):
                            emit_stt_pair(8 * sbk + fi)
                    emit_zadds(3)
                assert cV.n == cV0 + 128

                @blk.tensor
                def _(e):
                    for sbk in range(4):
                        for fi in range(8):
                            f = sbk * 8 + fi
                            e.wait_ge(cV.h, stt1_mark(f))
                            w1f = WTA[:, 4096 * (f % 2):4096 * (f % 2) + 1024]
                            for th in range(2):
                                yi = 2 * f + th
                                if yi >= 2:
                                    e.wait_ge(cS.h, silu_mark((yi - 2) // 2,
                                                              (yi - 2) % 2))
                                ps = PS_A[:, 1024 * (yi % 2):1024 * (yi % 2 + 1)]
                                for hc in range(NCH):
                                    lhsT = w1f[:, 128 * hc:128 * (hc + 1)]
                                    e.matmul(ps[:, 0:512], lhsT,
                                             xn_c(hc)[:, 1024 * th:1024 * th + 512],
                                             start=(hc == 0), stop=(hc == NCH - 1))
                                    li = e.matmul(ps[:, 512:1024], lhsT,
                                                  xn_c(hc)[:, 1024 * th + 512:
                                                           1024 * (th + 1)],
                                                  start=(hc == 0), stop=(hc == NCH - 1))
                                cT.inc(li, 1)
                        for oc in range(NCH):
                            for th in range(2):
                                if oc >= 1:
                                    e.wait_ge(cV.h, zadd_mark(sbk, oc - 1, th))
                                elif sbk >= 1:
                                    e.wait_ge(cV.h, zadd_mark(sbk - 1, 7, th))
                                zs = PS_B[:, 1024 * th:1024 * (th + 1)]
                                for fi in range(8):
                                    f = sbk * 8 + fi
                                    e.wait_ge(cV.h, stt2_mark(f))
                                    e.wait_ge(cS.h, silu_mark(f, 1))
                                    slot = (sbk % 2) * 8 + fi
                                    lhsT = W2X[:, 1024 * slot + 128 * oc:
                                               1024 * slot + 128 * (oc + 1)]
                                    yv = Y8[:, 2048 * fi + 1024 * th:
                                            2048 * fi + 1024 * (th + 1)]
                                    for j in range(2):
                                        li = e.matmul(
                                            zs[:, 512 * j:512 * (j + 1)], lhsT,
                                            yv[:, 512 * j:512 * (j + 1)],
                                            start=(fi == 0), stop=(fi == 7))
                                cT.inc(li, 1)
                assert cT.n == cT0 + 4 * 32

                @blk.scalar
                def _(e):
                    for sbk in range(4):
                        for fi in range(8):
                            f = sbk * 8 + fi
                            for th in range(2):
                                yi = 2 * f + th
                                e.wait_ge(cT.h, ymm_mark(f, th))
                                if sbk >= 1 and th == 0:
                                    e.wait_ge(cT.h, zmm_mark(sbk - 1, 7, 1))
                                ps = PS_A[:, 1024 * (yi % 2):1024 * (yi % 2 + 1)]
                                ydst = Y8[:, 2048 * fi + 1024 * th:
                                          2048 * fi + 1024 * (th + 1)]
                                cS.inc(e.activation(ydst, ps, AF.Silu,
                                                    bias=VC["b1T"][:, f:f + 1],
                                                    scale=1.0), 1)
                assert cS.n == cS0 + 64

            with nc.Block() as blk:
                @blk.sync
                def _(e):
                    dO.inc(e.dma_start(
                        out=xout[b].rearrange("(c p) t -> p c t", p=128),
                        in_=XR.rearrange("p (c t) -> p c t", t=S)), 16)

        def phase_out_l0(b):
            with nc.Block() as blk:
                @blk.sync
                def _(e):
                    dO.inc(e.dma_start(
                        out=xout[b].rearrange("(c p) t -> p c t", p=128),
                        in_=XR.rearrange("p (c t) -> p c t", t=S)), 16)

        for b in range(NBC):
            for li, layer in enumerate(layers):
                spill_mark = phase_p1(b, layer, load_x=(li == 0))
                phase_p2(b, layer, spill_mark)
                if layer == 1:
                    phase_p25(b)
                phase_p3(b, layer)
                phase_p35(b, layer)
                if layer == 0:
                    phase_p4_l0(b)
                    if layers == (0,):
                        phase_out_l0(b)
                else:
                    phase_p4_l1(b)

        with nc.Block() as blk:
            @blk.sync
            def _(e):
                e.wait_ge(dO.h, dO.n)

    nc._megakernel_inputs = list(input_names)
    return nc


_PROG_CACHE = {}
_CONSTS = None


def _get_prog(layers):
    key = tuple(layers)
    if key not in _PROG_CACHE:
        _PROG_CACHE[key] = build_program(key)
    return _PROG_CACHE[key]


def _consts():
    global _CONSTS
    if _CONSTS is None:
        _CONSTS = host_constants()
    return _CONSTS


def _core_maps(prog, inputs, extra_per_core):
    cs, cosT, sinT = _consts()
    names = prog._megakernel_inputs
    base = {}
    for nm in names:
        if nm in inputs:
            base[nm] = np.ascontiguousarray(np.asarray(inputs[nm]))
    base["csmall"] = cs
    if "cosrope" in names:
        base["cosrope"] = cosT
        base["sinrope"] = sinT
    maps = []
    for i in range(8):
        m = dict(base)
        m.update(extra_per_core[i])
        maps.append(m)
    return maps


def kernel(**inputs):
    x = np.asarray(inputs["L0_attn_x"])
    if FUSED:
        prog = _get_prog((0, 1))
        extra = [{"x": np.ascontiguousarray(x[2 * i:2 * i + 2].transpose(0, 2, 1))}
                 for i in range(8)]
        res = run_bass_kernel_spmd(prog, _core_maps(prog, inputs, extra),
                                   core_ids=list(range(8)))
        outs = [res.results[i]["out"].transpose(0, 2, 1) for i in range(8)]
        return np.ascontiguousarray(np.concatenate(outs, axis=0),
                                    dtype=np.float32)
    prog0 = _get_prog((0,))
    extra = [{"x": np.ascontiguousarray(x[2 * i:2 * i + 2].transpose(0, 2, 1))}
             for i in range(8)]
    res0 = run_bass_kernel_spmd(prog0, _core_maps(prog0, inputs, extra),
                                core_ids=list(range(8)))
    prog1 = _get_prog((1,))
    extra1 = [{"x1": np.ascontiguousarray(res0.results[i]["x1"])}
              for i in range(8)]
    res1 = run_bass_kernel_spmd(prog1, _core_maps(prog1, inputs, extra1),
                                core_ids=list(range(8)))
    outs = [res1.results[i]["out"].transpose(0, 2, 1) for i in range(8)]
    return np.ascontiguousarray(np.concatenate(outs, axis=0), dtype=np.float32)


# version 37
# speedup vs baseline: 1.2270x; 1.0448x over previous
import contextlib
import numpy as np
import ml_dtypes
import concourse.bass as bass
import concourse.mybir as mybir
from concourse.bass_utils import run_bass_kernel_spmd

F32 = mybir.dt.float32
BF16 = mybir.dt.bfloat16
I32 = mybir.dt.int32
AF = mybir.ActivationFunctionType
ALU = mybir.AluOpType
AXX = mybir.AxisListType.X

D = 1024
S = 2048
NBC = 2
NH = 16
HDIM = 64
FF = 4096
NCH = 8
NTC = 16
EPS_LN = 1e-5
EPS_RMS = 1e-6

FUSED = True


class Sem:
    __slots__ = ("h", "n")

    def __init__(self, stack, nc, name):
        self.h = stack.enter_context(nc.semaphore(name))
        self.n = 0

    def inc(self, instr, k=1):
        instr.then_inc(self.h, k)
        self.n += k
        return self.n


def host_constants():
    cs = np.zeros((128, 1024), np.float32)
    csb = cs.view(ml_dtypes.bfloat16)
    cs[:, 0] = 1.0
    csb[:, 2] = ml_dtypes.bfloat16(1.0)
    cs[0, 4:132] = 1.0
    cs[32, 4:132] = 1.0
    cs[0:32, 132:164] = np.eye(32, dtype=np.float32)
    for h in range(16):
        cs[(h % 2) * 64:(h % 2) * 64 + 64, 164 + h] = -1.0
    cs[:, 180:308] = 1.0
    for c in range(8):
        sel = np.zeros((16, 128), np.float32)
        sel[2 * c, 0:64] = 1.0
        sel[2 * c + 1, 64:128] = 1.0
        csb[0:16, 616 + 128 * c:616 + 128 * (c + 1)] = sel.astype(ml_dtypes.bfloat16)
    psel = np.zeros((128, 64), np.float32)
    for p in range(128):
        psel[p, p % 64] = 1.0
    csb[:, 1640:1704] = psel.astype(ml_dtypes.bfloat16)
    i = (np.arange(128) % 64) // 2
    invf = 10000.0 ** (-(np.arange(0, HDIM, 2) / HDIM))
    ang = np.outer(invf[i], np.arange(S))
    cosT = np.ascontiguousarray(np.cos(ang)).astype(ml_dtypes.bfloat16)
    sgn = np.where(np.arange(128) % 2 == 0, -1.0, 1.0)[:, None]
    sinT = np.ascontiguousarray(np.sin(ang) * sgn).astype(ml_dtypes.bfloat16)
    return cs, cosT, sinT


def build_program(layers):
    layers = tuple(layers)
    nc = bass.Bass("TRN2", target_bir_lowering=False)
    stack = contextlib.ExitStack()
    dram = {}
    input_names = []

    def din(name, shape, dtype):
        dram[name] = nc.dram_tensor(name, shape, dtype, kind="ExternalInput").ap()
        input_names.append(name)
        return dram[name]

    def dout(name, shape, dtype):
        dram[name] = nc.dram_tensor(name, shape, dtype, kind="ExternalOutput").ap()
        return dram[name]

    def dint(name, shape, dtype):
        dram[name] = nc.dram_tensor(name, shape, dtype, kind="Internal").ap()
        return dram[name]

    xin = din("x" if 0 in layers else "x1", [NBC, D, S], F32)
    xout = dout("x1" if layers == (0,) else "out", [NBC, D, S], F32)
    csmall = din("csmall", [128, 1024], F32)
    if 1 in layers:
        din("cosrope", [128, S], BF16)
        din("sinrope", [128, S], BF16)
    if 0 in layers:
        for nm in ("L0_attn_Wq", "L0_attn_Wk", "L0_attn_Wv", "L0_attn_Wo",
                   "L0_ffn_weight"):
            din(nm, [D, D], F32)
        for nm in ("L0_attn_bq", "L0_attn_bk", "L0_attn_bv", "L0_attn_bo",
                   "L0_attn_norm_q_w", "L0_attn_norm_k_w"):
            din(nm, [D], F32)
        din("L0_ffn_disha", [64, D], F32)
    if 1 in layers:
        for nm in ("L1_attn_wq", "L1_attn_wk", "L1_attn_wv"):
            din(nm, [D, D], F32)
        din("L1_ffn_ln_gamma", [D], F32)
        din("L1_ffn_ln_beta", [D], F32)
        din("L1_ffn_fc1_q", [D, FF], I32)
        din("L1_ffn_fc1_scales", [D // 128, FF], F32)
        din("L1_ffn_fc1_bias", [FF], F32)
        din("L1_ffn_fc2_q", [FF, D], I32)
        din("L1_ffn_fc2_scales", [FF // 128, D], F32)
        din("L1_ffn_fc2_bias", [D], F32)

    tw_names = []
    if 0 in layers:
        tw_names += ["L0_attn_Wq", "L0_attn_Wk", "L0_attn_Wv", "L0_attn_Wo",
                     "L0_ffn_weight"]
    if 1 in layers:
        tw_names += ["L1_attn_wq", "L1_attn_wk", "L1_attn_wv"]
    for nm in tw_names:
        dint(nm + "_bf", [D, D], BF16)
    xspill = dint("xspill", [D, S], F32)

    with stack:
        sbuf = lambda name, shape, dtype, **kw: stack.enter_context(
            nc.sbuf_tensor(name, shape, dtype, **kw))[:]
        R1 = sbuf("R1", [128, 16384], F32)
        R2 = sbuf("R2", [128, 16384], F32)
        XNT = sbuf("XNT", [128, 16384], BF16)
        WTA = sbuf("WTA", [128, 8192], BF16, is_transpose_dest=True)
        CS8 = sbuf("CS8", [128, 4096], BF16)
        CSM = sbuf("CSM", [128, 1024], F32)
        SR = sbuf("SR", [128, 4096], F32)
        VECS = sbuf("VECS", [128, 256], F32)
        VECB = sbuf("VECB", [128, 64], BF16)
        STG = SR[0:32, 0:1024]
        SRb = SR.bitcast(BF16)
        SRA = SR[0:1, 0:2048]
        SRB = SR[0:1, 2048:4096]
        OST = SRb[0:64, 0:1024]

        R1b = R1.bitcast(BF16)
        R2b = R2.bitcast(BF16)
        CSMb = CSM.bitcast(BF16)
        CS8i = CS8.bitcast(I32)
        CS8f = CS8.bitcast(F32)

        XR = R1
        VA = R1b[:, 0:16640]
        WTB = R1b[:, 0:8192]
        PT = [R1b[:, 16640:17664], R1b[:, 17664:18688],
              R1b[:, 28928:29952], R1b[:, 29952:30976]]
        DEN = R1[0:16, 9344:11392]
        RDEN = R1b[0:16, 22784:24832]
        QSQ = R1b[:, 24832:26880]
        TMPR = R1b[:, 26880:28928]

        Q16 = R2b[:, 0:16384]
        K16 = R2b[:, 16384:32768]
        XSQ32 = R2[:, 0:2048]
        TMP32 = R2[:, 2048:4096]
        Y8 = R2b
        W2X = R2b[:, 16384:32768]

        W1I = CS8i[:, 0:1024]
        SCB = CS8f[:, 1024:2048]
        SRi = SR.bitcast(I32)
        DQ_INT = [CS8i[:, 0:1024], SRi[:, 0:1024], SRi[:, 2048:3072]]
        DQ_SCL = [CS8f[:, 1024:2048], SR[:, 1024:2048], SR[:, 3072:4096]]
        COSR = CS8[:, 0:2048]
        SINR = CS8[:, 2048:4096]
        POOLPAD = CS8[:, 0:2048]
        DISHA = CS8[:, 2048:3072]

        ONE128 = CSM[:, 0:1]
        ONE128B = CSMb[:, 2:3]
        ONESROW = CSM[0:1, 4:132]
        ONESROW32 = CSM[32:33, 4:132]
        IDENT32 = CSM[0:32, 132:164]
        MASKNEG = CSM[:, 164:180]
        ONESSQ = CSM[:, 180:308]
        SEL16S = CSMb[0:16, 616:1640]
        PSEL = CSMb[:, 1640:1704]

        VC = {}
        _vc = [0]

        def vcol(name, n=1):
            VC[name] = VECS[:, _vc[0]:_vc[0] + n]
            _vc[0] += n

        for nm, n in (("bqT", 8), ("bkT", 8), ("boT", 8), ("wq0125T", 8),
                      ("wknT", 8), ("gammaT", 8), ("betaT", 8), ("hb2T", 8),
                      ("QmaxT", 8), ("KmaxT", 8), ("bo1T", 8), ("b1T", 32),
                      ("negMrhs", 16), ("negMb", 16), ("zeroT", 1), ("oneT", 1),
                      ("c0125T", 1)):
            vcol(nm, n)
        BV16 = VECB[:, 0:8]
        KMAX16 = VECB[:, 8:16]

        PS_A = stack.enter_context(nc.psum_tensor("PS_A", [128, 2048], F32))[:]
        PS_B = stack.enter_context(nc.psum_tensor("PS_B", [128, 2048], F32))[:]

        sems = {nm: Sem(stack, nc, nm)
                for nm in ("dW", "dS", "dX", "dO", "dP", "cT", "cV", "cS", "cP",
                           "dU0", "dU1", "dU2", "dDen", "dOD", "dSB",
                           "dV0", "dV1", "dV2")}
        dW, dS, dX, dO, dP = (sems[k] for k in ("dW", "dS", "dX", "dO", "dP"))
        cT, cV, cS, cP = (sems[k] for k in ("cT", "cV", "cS", "cP"))
        DU = [sems["dU0"], sems["dU1"], sems["dU2"]]
        DV = [sems["dV0"], sems["dV1"], sems["dV2"]]
        dDen, dOD, dSB = sems["dDen"], sems["dOD"], sems["dSB"]

        PF = {}

        def xr_c(c):
            return XR[:, 2048 * c:2048 * (c + 1)]

        def XB8(c):
            return R2b[:, 16384 + 2048 * c:16384 + 2048 * (c + 1)]

        XSQ16 = [R2b[:, 12288:14336], R2b[:, 14336:16384]]
        PSA16 = R2b[:, 12288:14336]
        PSB16 = R2b[:, 14336:16384]
        TMP16 = R2b[:, 8192:10240]

        def xn_c(c):
            return XNT[:, 2048 * c:2048 * (c + 1)]

        def q_c(c):
            return Q16[:, 2048 * c:2048 * (c + 1)]

        def k_c(c):
            return K16[:, 2048 * c:2048 * (c + 1)]

        def load_wt(e, nm, buf=None, sem=None):
            buf = WTA if buf is None else buf
            sem = dS if sem is None else sem
            wbf = dram[nm + "_bf"]
            for c in range(NCH):
                sem.inc(e.dma_start(out=buf[:, 1024 * c:1024 * (c + 1)],
                                    in_=wbf[:, 128 * c:128 * (c + 1)],
                                    transpose=True), 16)
            return sem.n

        with nc.Block() as blk:
            @blk.sync
            def _(e):
                dW.inc(e.dma_start(out=CSM, in_=csmall), 16)

            @blk.gpsimd
            def _(e):
                for nm in tw_names:
                    dP.inc(e.dma_start(out=dram[nm + "_bf"], in_=dram[nm]), 16)
        cast_mark = dP.n

        vec_plan = []
        if 0 in layers:
            vec_plan += [("L0_attn_bq", "bqT", 8), ("L0_attn_bk", "bkT", 8),
                         ("L0_attn_bv", "_bv", 8), ("L0_attn_bo", "boT", 8),
                         ("L0_attn_norm_q_w", "wq0125T", 8),
                         ("L0_attn_norm_k_w", "wknT", 8)]
        if 1 in layers:
            vec_plan += [("L1_ffn_ln_gamma", "gammaT", 8),
                         ("L1_ffn_ln_beta", "betaT", 8),
                         ("L1_ffn_fc2_bias", "hb2T", 8),
                         ("L1_ffn_fc1_bias", "b1T", 32)]

        for g0 in range(0, len(vec_plan), 8):
            grp = vec_plan[g0:g0 + 8]
            with nc.Block() as blk:
                @blk.sync
                def _(e, grp=grp):
                    e.wait_ge(dW.h, 16)
                    for idx, (src, dst, rows) in enumerate(grp):
                        srcap = dram[src].rearrange("(a b) -> a b", b=128)
                        dX.inc(e.dma_start(
                            out=STG[0:rows, 128 * idx:128 * (idx + 1)],
                            in_=srcap), 16)
            ld_mark = dX.n
            with nc.Block() as blk:
                @blk.tensor
                def _(e, grp=grp):
                    e.wait_ge(dX.h, ld_mark)
                    for idx, (src, dst, rows) in enumerate(grp):
                        li = e.transpose(PS_A[:, 64 * idx:64 * idx + rows],
                                         STG[0:rows, 128 * idx:128 * (idx + 1)],
                                         IDENT32[0:rows, 0:rows])
                    cT.inc(li, 1)
            tr_mark = cT.n
            with nc.Block() as blk:
                @blk.vector
                def _(e, grp=grp):
                    e.wait_ge(cT.h, tr_mark)
                    for idx, (src, dst, rows) in enumerate(grp):
                        ps = PS_A[:, 64 * idx:64 * idx + rows]
                        if dst == "_bv":
                            e.tensor_copy(BV16, ps)
                        else:
                            e.tensor_copy(VC[dst], ps)

        with nc.Block() as blk:
            @blk.vector
            def _(e):
                if 0 in layers:
                    e.tensor_scalar_mul(VC["wq0125T"], VC["wq0125T"], 0.125)
                if 1 in layers:
                    e.tensor_scalar_mul(VC["hb2T"], VC["hb2T"], 0.5)
                e.memset(VC["zeroT"], 0.0)
                e.memset(VC["oneT"], 1.0)
                e.memset(VC["c0125T"], 0.125)
                e.memset(VC["bo1T"], 0.0)

        def phase_p1(b, layer, load_x):
            with nc.Block() as blk:
                dX0, cS0, cT0, cV0 = dX.n, cS.n, cT.n, cV.n
                if load_x:
                    @blk.sync
                    def _(e):
                        dX.inc(e.dma_start(
                            out=XR.rearrange("p (c t) -> p c t", t=S),
                            in_=xin[b].rearrange("(c p) t -> p c t", p=128)), 16)
                xload = dX.n

                @blk.scalar
                def _(e):
                    if load_x:
                        e.wait_ge(dX.h, xload)
                    for c in range(NCH):
                        if c >= 2:
                            e.wait_ge(cT.h, cT0 + 2 * (c - 2) + 2)
                        with nc.allow_low_precision(
                                reason="LN stats operands in bf16; fp32 "
                                       "accumulation (error ~1e-4 of sigma)"):
                            cS.inc(e.activation(XSQ16[c % 2], xr_c(c),
                                                AF.Square), 1)
                assert cS.n == cS0 + NCH

                @blk.vector
                def _(e):
                    if load_x:
                        e.wait_ge(dX.h, xload)
                    for c in range(NCH):
                        with nc.allow_low_precision(
                                reason="LN stats operands in bf16"):
                            cV.inc(e.tensor_copy(XB8(c), xr_c(c)), 1)
                assert cV.n == cV0 + NCH

                @blk.tensor
                def _(e):
                    for c in range(NCH):
                        e.wait_ge(cV.h, cV0 + c + 1)
                        for j in range(4):
                            li = e.matmul(PS_A[0:1, 512 * j:512 * (j + 1)], ONE128B,
                                          XB8(c)[:, 512 * j:512 * (j + 1)],
                                          start=(c == 0), stop=(c == NCH - 1))
                        cT.inc(li, 1)
                        e.wait_ge(cS.h, cS0 + c + 1)
                        for j in range(4):
                            li = e.matmul(PS_B[0:1, 512 * j:512 * (j + 1)], ONE128B,
                                          XSQ16[c % 2][:, 512 * j:512 * (j + 1)],
                                          start=(c == 0), stop=(c == NCH - 1))
                        cT.inc(li, 1)
                assert cT.n == cT0 + 2 * NCH

            with nc.Block() as blk:
                cV0, cS0, cT0 = cV.n, cS.n, cT.n

                @blk.vector
                def _(e):
                    T = PS_A[0:1, 0:2048]
                    e.tensor_copy(SRA, PS_A[0:1, :])
                    e.tensor_copy(SRB, PS_B[0:1, :])
                    e.tensor_scalar_mul(SRA, SRA, 1.0 / D)
                    e.tensor_scalar(SRB, SRB, 1.0 / D, EPS_LN,
                                    ALU.mult, ALU.add)
                    e.tensor_tensor(T, SRA, SRA, ALU.mult)
                    cV.inc(e.tensor_tensor(PS_B[0:1, 0:2048], SRB, T,
                                           ALU.subtract), 1)
                    e.wait_ge(cS.h, cS0 + 2)
                    cV.inc(e.tensor_tensor(SRA, SRA, SRB, ALU.mult), 1)
                    e.wait_ge(cT.h, cT0 + 1)
                    with nc.allow_low_precision(
                            reason="LN apply in bf16 (output is bf16)"):
                        e.tensor_copy(PSA16, PS_A)
                        cV.inc(e.tensor_copy(PSB16, PS_B), 1)

                @blk.scalar
                def _(e):
                    e.wait_ge(cV.h, cV0 + 1)
                    cS.inc(e.activation(PS_A[0:1, 0:2048], PS_B[0:1, 0:2048],
                                        AF.Ln), 1)
                    e.wait_ge(cS.h, cS0 + 1)
                    cS.inc(e.activation(SRB, PS_A[0:1, 0:2048], AF.Exp,
                                        scale=-0.5), 1)

                @blk.tensor
                def _(e):
                    e.wait_ge(cV.h, cV0 + 2)
                    for j in range(4):
                        e.matmul(PS_A[:, 512 * j:512 * (j + 1)], ONESROW,
                                 SRB[:, 512 * j:512 * (j + 1)],
                                 start=True, stop=True)
                    for j in range(4):
                        li = e.matmul(PS_B[:, 512 * j:512 * (j + 1)], ONESROW,
                                      SRA[:, 512 * j:512 * (j + 1)],
                                      start=True, stop=True)
                    cT.inc(li, 1)

                @blk.sync
                def _(e):
                    e.wait_ge(dP.h, cast_mark)
                    if layer == 1:
                        dS.inc(e.dma_start(out=COSR, in_=dram["cosrope"]), 16)
                        dS.inc(e.dma_start(out=SINR, in_=dram["sinrope"]), 16)
                    wq = "L0_attn_Wq" if layer == 0 else "L1_attn_wq"
                    PF["wq"] = load_wt(e, wq)

        def emit_feature_proj(blk, wname, dst_c, evac_kind, evac_args,
                              preloaded=None, wbuf=None, wsem=None,
                              extra_load=None, vector_fn=None):
            wbuf = WTA if wbuf is None else wbuf
            wsem = dS if wsem is None else wsem
            with blk:
                if vector_fn is not None:
                    @blk.vector
                    def _(e):
                        vector_fn(e)
                box = {}
                if preloaded is None or extra_load is not None:
                    @blk.sync
                    def _(e):
                        if preloaded is None:
                            e.wait_ge(dP.h, cast_mark)
                            box["m"] = load_wt(e, wname, wbuf, wsem)
                        if extra_load is not None:
                            key, nm2, buf2, sem2 = extra_load
                            PF[key] = load_wt(e, nm2, buf2, sem2)
                wt_mark = box.get("m", preloaded)
                cT0, cS0 = cT.n, cS.n

                @blk.tensor
                def _(e):
                    e.wait_ge(wsem.h, wt_mark)
                    for oc in range(NCH):
                        if oc >= 2:
                            e.wait_ge(cS.h, cS0 + oc - 1)
                        ps = PS_A if oc % 2 == 0 else PS_B
                        for dc in range(NCH):
                            lhsT = wbuf[:, 1024 * dc + 128 * oc:
                                        1024 * dc + 128 * (oc + 1)]
                            for j in range(4):
                                li = e.matmul(ps[:, 512 * j:512 * (j + 1)], lhsT,
                                              xn_c(dc)[:, 512 * j:512 * (j + 1)],
                                              start=(dc == 0), stop=(dc == NCH - 1))
                        cT.inc(li, 1)
                assert cT.n == cT0 + NCH

                @blk.scalar
                def _(e):
                    for oc in range(NCH):
                        e.wait_ge(cT.h, cT0 + oc + 1)
                        ps = PS_A if oc % 2 == 0 else PS_B
                        if evac_kind == "bias":
                            cS.inc(e.activation(dst_c(oc), ps, AF.Identity,
                                                bias=evac_args[:, oc:oc + 1],
                                                scale=VC["oneT"]), 1)
                        elif evac_kind == "scale0125":
                            cS.inc(e.activation(dst_c(oc), ps, AF.Identity,
                                                bias=VC["zeroT"],
                                                scale=VC["c0125T"]), 1)
                        else:
                            cS.inc(e.activation(dst_c(oc), ps, AF.Copy), 1)
                assert cS.n == cS0 + NCH

        KMAXD = SRb[:, 6400:6528]

        def emit_kmax(e, c):
            e.reduce_max(KMAX16[:, c:c + 1], k_c(c), axis=AXX,
                         apply_absolute_value=True)
            h = 2 * c
            e.tensor_copy(KMAXD[0:64, 16 * c + h:16 * c + h + 1],
                          KMAX16[0:64, c:c + 1])
            cV.inc(e.tensor_copy(
                KMAXD[64:128, 16 * c + h + 1:16 * c + h + 2],
                KMAX16[64:128, c:c + 1]), 1)
            PF["kmaxd"][c] = cV.n

        def phase_p2(b, layer):
            PF["kmaxd"] = {}
            if layer == 0:
                wq, wk, wv = "L0_attn_Wq", "L0_attn_Wk", "L0_attn_Wv"
            else:
                wq, wk, wv = "L1_attn_wq", "L1_attn_wk", "L1_attn_wv"

            ROPE_MASK = []
            for i in range(16):
                ROPE_MASK += [2 * i + 1, 2 * i]

            def rope_chunk(e, tgt, c):
                e.stream_shuffle(TMPR, tgt(c), ROPE_MASK)
                e.tensor_tensor(TMPR, TMPR, SINR, ALU.mult)
                e.tensor_tensor(tgt(c), tgt(c), COSR, ALU.mult)
                e.tensor_tensor(tgt(c), tgt(c), TMPR, ALU.add)

            def rope_q(e):
                e.wait_ge(dS.h, dS.n)
                e.wait_ge(dX.h, PF["spill"])
                for c in range(NCH):
                    rope_chunk(e, q_c, c)

            with nc.Block() as blk:
                cT0q, cS0q, cV0q = cT.n, cS.n, cV.n
                am, fm = {}, {}

                @blk.vector
                def _(e):
                    with nc.allow_low_precision(
                            reason="LN apply in bf16 (output is bf16)"):
                        for c in range(NCH):
                            e.tensor_tensor(TMP16, XB8(c), PSA16, ALU.mult)
                            cV.inc(e.tensor_tensor(xn_c(c), TMP16, PSB16,
                                                   ALU.subtract), 1)
                            am[c] = cV.n
                            if layer == 1 and c == 1:
                                li = None
                                for cc in (0, 1):
                                    li = e.tensor_tensor(
                                        xr_c(cc), xr_c(cc), xn_c(cc), ALU.add)
                                cV.inc(li, 1)
                                fm[1] = cV.n
                    if layer == 1:
                        li = None
                        for cc in range(2, NCH):
                            li = e.tensor_tensor(xr_c(cc), xr_c(cc),
                                                 xn_c(cc), ALU.add)
                        cV.inc(li, 1)
                        fm[7] = cV.n

                @blk.tensor
                def _(e):
                    e.wait_ge(dS.h, PF["wq"])
                    for p in range(4):
                        ocA, ocB = 2 * p, 2 * p + 1
                        if p >= 1:
                            e.wait_ge(cS.h, cS0q + 2 * p)
                        li = None
                        for dc in range(NCH):
                            if p == 0:
                                e.wait_ge(cV.h, am[dc])
                            for j in range(4):
                                e.matmul(
                                    PS_A[:, 512 * j:512 * (j + 1)],
                                    WTA[:, 1024 * dc + 128 * ocA:
                                        1024 * dc + 128 * (ocA + 1)],
                                    xn_c(dc)[:, 512 * j:512 * (j + 1)],
                                    start=(dc == 0), stop=(dc == NCH - 1))
                            for j in range(4):
                                li = e.matmul(
                                    PS_B[:, 512 * j:512 * (j + 1)],
                                    WTA[:, 1024 * dc + 128 * ocB:
                                        1024 * dc + 128 * (ocB + 1)],
                                    xn_c(dc)[:, 512 * j:512 * (j + 1)],
                                    start=(dc == 0), stop=(dc == NCH - 1))
                        cT.inc(li, 1)
                assert cT.n == cT0q + 4

                @blk.scalar
                def _(e):
                    for p in range(4):
                        e.wait_ge(cT.h, cT0q + p + 1)
                        for oc in (2 * p, 2 * p + 1):
                            ps = PS_A if oc % 2 == 0 else PS_B
                            if layer == 0:
                                cS.inc(e.activation(
                                    q_c(oc), ps, AF.Identity,
                                    bias=VC["bqT"][:, oc:oc + 1],
                                    scale=VC["oneT"]), 1)
                            else:
                                cS.inc(e.activation(
                                    q_c(oc), ps, AF.Identity,
                                    bias=VC["zeroT"],
                                    scale=VC["c0125T"]), 1)
                assert cS.n == cS0q + 8

                @blk.sync
                def _(e):
                    if layer == 1:
                        e.wait_ge(cV.h, fm[1])
                    dX.inc(e.dma_start(
                        out=xspill[0:256].rearrange("(c p) t -> p c t", p=128),
                        in_=XR[:, 0:2 * S].rearrange("p (c t) -> p c t", t=S)),
                        16)
                    sp01 = dX.n
                    e.wait_ge(dS.h, PF["wq"])
                    e.wait_ge(dX.h, sp01)
                    PF["wk"] = load_wt(e, wk, WTB, dSB)
                    if layer == 1:
                        e.wait_ge(cV.h, fm[7])
                    dX.inc(e.dma_start(
                        out=xspill[256:1024].rearrange("(c p) t -> p c t",
                                                       p=128),
                        in_=XR[:, 2 * S:8 * S].rearrange("p (c t) -> p c t",
                                                         t=S)), 16)
                    PF["spill"] = dX.n

            emit_feature_proj(nc.Block(), wk, k_c,
                              "bias" if layer == 0 else "copy",
                              VC["bkT"] if layer == 0 else None,
                              preloaded=PF.get("wk"), wbuf=WTB, wsem=dSB,
                              extra_load=("wv", wv, WTA, dS),
                              vector_fn=rope_q if layer == 1 else None)

            with nc.Block() as blk:
                wt_mark = PF["wv"]
                cT0, cS0 = cT.n, cS.n

                @blk.gpsimd
                def _(e):
                    e.wait_ge(dX.h, PF["spill"])
                    cP.inc(e.memset(VA[:, 64::65], 1.0), 1)

                @blk.tensor
                def _(e):
                    e.wait_ge(dS.h, wt_mark)
                    for tc in range(NTC):
                        if tc >= 2:
                            e.wait_ge(cS.h, cS0 + tc - 1)
                        ps = PS_A if tc % 2 == 0 else PS_B
                        for dc in range(NCH):
                            lhsT = XNT[:, 2048 * dc + 128 * tc:
                                       2048 * dc + 128 * (tc + 1)]
                            for j in range(2):
                                li = e.matmul(ps[:, 512 * j:512 * (j + 1)], lhsT,
                                              WTA[:, 1024 * dc + 512 * j:
                                                  1024 * dc + 512 * (j + 1)],
                                              start=(dc == 0), stop=(dc == NCH - 1))
                        cT.inc(li, 1)

                if layer == 1:
                    @blk.vector
                    def _(e):
                        with nc.allow_low_precision(
                                reason="softmax bound only (KMAXD)"):
                            e.memset(KMAXD, 0.0)
                            for c in range(NCH):
                                rope_chunk(e, k_c, c)
                                emit_kmax(e, c)

                @blk.scalar
                def _(e):
                    e.wait_ge(dX.h, PF["spill"])
                    for tc in range(NTC):
                        e.wait_ge(cT.h, cT0 + tc + 1)
                        ps = PS_A if tc % 2 == 0 else PS_B
                        va_t = bass.AP(VA.tensor, VA.offset + 1040 * tc,
                                       [list(VA.ap[0]), [65, 16], [1, 64]])
                        cS.inc(e.activation(
                            va_t, ps[:, 0:1024].rearrange("p (h j) -> p h j", j=64),
                            AF.Copy), 1)

            if layer == 0:
                with nc.Block() as blk:
                    cS0, cT0 = cS.n, cT.n

                    @blk.scalar
                    def _(e):
                        for c in range(NCH):
                            if c >= 1:
                                e.wait_ge(cT.h, cT0 + 2 * c - 1)
                            cS.inc(e.activation(QSQ, q_c(c), AF.Square), 1)
                            if c >= 1:
                                e.wait_ge(cT.h, cT0 + 2 * c)
                            cS.inc(e.activation(TMPR, k_c(c), AF.Square), 1)

                    @blk.tensor
                    def _(e):
                        for c in range(NCH):
                            e.wait_ge(cS.h, cS0 + 2 * c + 1)
                            for j in range(4):
                                li = e.matmul(PS_A[0:1, 512 * j:512 * (j + 1)],
                                              ONE128B, QSQ[:, 512 * j:512 * (j + 1)],
                                              start=(c == 0), stop=(c == NCH - 1))
                            cT.inc(li, 1)
                            e.wait_ge(cS.h, cS0 + 2 * c + 2)
                            for j in range(4):
                                li = e.matmul(PS_B[0:1, 512 * j:512 * (j + 1)],
                                              ONE128B, TMPR[:, 512 * j:512 * (j + 1)],
                                              start=(c == 0), stop=(c == NCH - 1))
                            cT.inc(li, 1)

                with nc.Block() as blk:
                    cV0, cS0, cT0 = cV.n, cS.n, cT.n

                    @blk.vector
                    def _(e):
                        T = PS_A[0:1, 0:2048]
                        T2 = PS_B[0:1, 0:2048]
                        e.tensor_copy(SRA, PS_A[0:1, :])
                        e.tensor_copy(SRB, PS_B[0:1, :])
                        e.tensor_scalar(T, SRA, 1.0 / D, EPS_RMS,
                                        ALU.mult, ALU.add)
                        cV.inc(e.tensor_scalar(T2, SRB, 1.0 / D, EPS_RMS,
                                               ALU.mult, ALU.add), 1)
                        e.wait_ge(cT.h, cT0 + 1)
                        with nc.allow_low_precision(
                                reason="softmax bound only (KMAXD)"):
                            e.memset(KMAXD, 0.0)
                            for c in range(NCH):
                                e.scalar_tensor_tensor(
                                    q_c(c), q_c(c), VC["wq0125T"][:, c:c + 1],
                                    PS_A, ALU.mult, ALU.mult)
                                e.scalar_tensor_tensor(
                                    k_c(c), k_c(c), VC["wknT"][:, c:c + 1],
                                    PS_B, ALU.mult, ALU.mult)
                                emit_kmax(e, c)

                    @blk.scalar
                    def _(e):
                        e.wait_ge(cV.h, cV0 + 1)
                        cS.inc(e.activation(SRB, PS_A[0:1, 0:2048], AF.Ln), 1)
                        e.wait_ge(cS.h, cS0 + 1)
                        cS.inc(e.activation(SRA, SRB, AF.Exp, scale=-0.5), 1)
                        e.wait_ge(cS.h, cS0 + 2)
                        cS.inc(e.activation(PS_A[0:1, 0:2048],
                                            PS_B[0:1, 0:2048], AF.Ln), 1)
                        e.wait_ge(cS.h, cS0 + 3)
                        cS.inc(e.activation(SRB, PS_A[0:1, 0:2048],
                                            AF.Exp, scale=-0.5), 1)

                    @blk.tensor
                    def _(e):
                        e.wait_ge(cS.h, cS0 + 4)
                        for j in range(4):
                            e.matmul(PS_A[:, 512 * j:512 * (j + 1)], ONESROW,
                                     SRA[:, 512 * j:512 * (j + 1)],
                                     start=True, stop=True)
                        for j in range(4):
                            li = e.matmul(PS_B[:, 512 * j:512 * (j + 1)], ONESROW,
                                          SRB[:, 512 * j:512 * (j + 1)],
                                          start=True, stop=True)
                        cT.inc(li, 1)
            MROW = SR[0:1, 4080:4096]
            KMCOL = SR[0:16, 4079:4080]
            with nc.Block() as blk:
                cV0, cT0, cS0 = cV.n, cT.n, cS.n
                dDen0 = dDen.n

                @blk.vector
                def _(e):
                    with nc.allow_low_precision(reason="softmax bound only"):
                        e.wait_ge(cT.h, cT0 + NCH)
                        cV.inc(e.reduce_max(KMCOL, PS_B[0:16, :], axis=AXX), 1)
                        e.wait_ge(cT.h, cT0 + NCH + 1)
                        e.tensor_scalar_mul(VC["negMb"], PS_A[:, 0:16], -1.0)

                @blk.scalar
                def _(e):
                    for c in range(NCH):
                        if c >= 2:
                            e.wait_ge(cT.h, cT0 + c - 1)
                        buf = QSQ if c % 2 == 0 else TMPR
                        cS.inc(e.activation(buf, q_c(c), AF.Abs), 1)

                @blk.gpsimd
                def _(e):
                    e.wait_ge(cV.h, cV0 + 1)
                    dDen.inc(e.dma_start(out=MROW, in_=KMCOL), 16)

                @blk.tensor
                def _(e):
                    for c in range(NCH):
                        e.wait_ge(cS.h, cS0 + c + 1)
                        buf = QSQ if c % 2 == 0 else TMPR
                        for j in range(4):
                            li = e.matmul(PS_B[0:16, 512 * j:512 * (j + 1)],
                                          KMAXD[:, 16 * c:16 * (c + 1)],
                                          buf[:, 512 * j:512 * (j + 1)],
                                          start=(c == 0), stop=(c == NCH - 1))
                        cT.inc(li, 1)
                    e.wait_ge(dDen.h, dDen0 + 16)
                    cT.inc(e.matmul(PS_A[:, 0:16], ONESROW, MROW,
                                    start=True, stop=True), 1)

        def phase_p25(b):
            with nc.Block() as blk:
                cS0, cT0 = cS.n, cT.n

                @blk.scalar
                def _(e):
                    for c in range(NCH):
                        if c >= 1:
                            e.wait_ge(cT.h, cT0 + 2 * c)
                        cS.inc(e.activation(QSQ, xn_c(c), AF.Square), 1)

                @blk.tensor
                def _(e):
                    for c in range(NCH):
                        for j in range(4):
                            li = e.matmul(PS_A[0:1, 512 * j:512 * (j + 1)], ONE128B,
                                          xn_c(c)[:, 512 * j:512 * (j + 1)],
                                          start=(c == 0), stop=(c == NCH - 1))
                        cT.inc(li, 1)
                        e.wait_ge(cS.h, cS0 + c + 1)
                        for j in range(4):
                            li = e.matmul(PS_B[0:1, 512 * j:512 * (j + 1)], ONE128B,
                                          QSQ[:, 512 * j:512 * (j + 1)],
                                          start=(c == 0), stop=(c == NCH - 1))
                        cT.inc(li, 1)

            with nc.Block() as blk:
                cV0, cT0, cS0 = cV.n, cT.n, cS.n

                @blk.vector
                def _(e):
                    T = PS_A[0:1, 0:2048]
                    e.tensor_copy(SRA, PS_A[0:1, :])
                    e.tensor_copy(SRB, PS_B[0:1, :])
                    e.tensor_scalar_mul(SRA, SRA, 1.0 / D)
                    e.tensor_scalar(SRB, SRB, 1.0 / D, EPS_LN,
                                    ALU.mult, ALU.add)
                    e.tensor_tensor(T, SRA, SRA, ALU.mult)
                    cV.inc(e.tensor_tensor(PS_B[0:1, 0:2048], SRB, T,
                                           ALU.subtract), 1)
                    e.wait_ge(cS.h, cS0 + 2)
                    cV.inc(e.tensor_tensor(SRA, SRA, SRB, ALU.mult), 1)
                    e.wait_ge(cT.h, cT0 + 1)
                    for c in range(NCH):
                        if c >= 1:
                            e.wait_ge(cS.h, cS0 + 2 + c)
                        e.tensor_tensor(TMPR, xn_c(c), PS_A, ALU.mult)
                        cV.inc(e.tensor_tensor(TMPR, TMPR, PS_B, ALU.subtract), 1)

                @blk.scalar
                def _(e):
                    e.wait_ge(cV.h, cV0 + 1)
                    cS.inc(e.activation(PS_A[0:1, 0:2048], PS_B[0:1, 0:2048],
                                        AF.Ln), 1)
                    e.wait_ge(cS.h, cS0 + 1)
                    cS.inc(e.activation(SRB, PS_A[0:1, 0:2048], AF.Exp,
                                        scale=-0.5), 1)
                    for c in range(NCH):
                        e.wait_ge(cV.h, cV0 + 3 + c)
                        cS.inc(e.activation(xn_c(c), TMPR, AF.Identity,
                                            bias=VC["betaT"][:, c:c + 1],
                                            scale=VC["gammaT"][:, c:c + 1]), 1)

                @blk.tensor
                def _(e):
                    e.wait_ge(cV.h, cV0 + 2)
                    for j in range(4):
                        e.matmul(PS_A[:, 512 * j:512 * (j + 1)], ONESROW,
                                 SRB[:, 512 * j:512 * (j + 1)],
                                 start=True, stop=True)
                    for j in range(4):
                        li = e.matmul(PS_B[:, 512 * j:512 * (j + 1)], ONESROW,
                                      SRA[:, 512 * j:512 * (j + 1)],
                                      start=True, stop=True)
                    cT.inc(li, 1)

        def phase_p3(b, layer):
            NU = NH * NTC
            STA, STB = PS_A[:, 0:1024], PS_A[:, 1024:2048]
            O_A, O_B = PS_B[0:65, 0:1024], PS_B[0:65, 1024:2048]
            PTA, PTB = [PT[0], PT[1]], [PT[2], PT[3]]

            with nc.Block() as blk:
                if layer == 0:
                    @blk.sync
                    def _(e):
                        PF["ffnw"] = load_wt(e, "L0_ffn_weight")
                else:
                    PF["du"] = {}

                    @blk.sync
                    def _(e):
                        q1 = dram["L1_ffn_fc1_q"]
                        s1 = dram["L1_ffn_fc1_scales"]
                        src = bass.AP(q1.tensor, q1.offset,
                                      [[4096, 128], [4096 * 128, 8], [1, 128]])
                        DU[0].inc(e.dma_start(out=DQ_INT[0], in_=src), 16)
                        ssrc = bass.AP(s1.tensor, s1.offset,
                                       [[0, 128], [4096, 8], [1, 128]])
                        DU[0].inc(e.dma_start(out=DQ_SCL[0], in_=ssrc), 16)
                        PF["du"][0] = DU[0].n
                        src = bass.AP(q1.tensor, q1.offset + 128,
                                      [[4096, 128], [4096 * 128, 8], [1, 128]])
                        DU[2].inc(e.dma_start(out=DQ_INT[2], in_=src), 16)
                        ssrc = bass.AP(s1.tensor, s1.offset + 128,
                                       [[0, 128], [4096, 8], [1, 128]])
                        DU[2].inc(e.dma_start(out=DQ_SCL[2], in_=ssrc), 16)
                        PF["du"][2] = DU[2].n
                cT0, cS0, cV0 = cT.n, cS.n, cV.n
                dDen0, dOD0 = dDen.n, dOD.n
                stA_mark = {}
                stB_mark = {}
                avA_mark = {}
                avB_mark = {}

                def uhc(u):
                    return 2 * (u // 32), (u // 16) % 2, u % 16

                def expA_mark(u):
                    return cS0 + 2 * u + 1

                def expB_mark(u):
                    return cS0 + 2 * u + 2

                def denA_mark(m):
                    return cV0 + 4 * m + 1

                def evacA_mark(m):
                    return cV0 + 4 * m + 2

                def denB_mark(m):
                    return cV0 + 4 * m + 3

                def evacB_mark(m):
                    return cV0 + 4 * m + 4

                def denAdma_mark(m):
                    return dDen0 + 32 * m + 16

                def denBdma_mark(m):
                    return dDen0 + 32 * m + 32

                def odma_mark(m):
                    return dOD0 + 16 * (m + 1)

                @blk.tensor
                def _(e):
                    def emit_st(u):
                        hA, H, c = uhc(u)
                        ch = 2048 * (u // 32)
                        if u >= 1:
                            e.wait_ge(cS.h, expA_mark(u - 1))
                        lhsT = K16[0:64, ch + 128 * c:ch + 128 * (c + 1)]
                        rhs = Q16[0:64, ch + 1024 * H:ch + 1024 * (H + 1)]
                        e.matmul(STA[:, 0:512], lhsT, rhs[:, 0:512],
                                 start=True, stop=True)
                        li = e.matmul(STA[:, 512:1024], lhsT, rhs[:, 512:1024],
                                      start=True, stop=True)
                        cT.inc(li, 1)
                        stA_mark[u] = cT.n
                        if u >= 1:
                            e.wait_ge(cS.h, expB_mark(u - 1))
                        lhsT = K16[64:128, ch + 128 * c:ch + 128 * (c + 1)]
                        rhs = Q16[64:128, ch + 1024 * H:ch + 1024 * (H + 1)]
                        e.matmul(STB[:, 0:512], lhsT, rhs[:, 0:512],
                                 start=True, stop=True)
                        li = e.matmul(STB[:, 512:1024], lhsT, rhs[:, 512:1024],
                                      start=True, stop=True)
                        cT.inc(li, 1)
                        stB_mark[u] = cT.n

                    def emit_av(u):
                        hA, H, c = uhc(u)
                        m = u // 16
                        e.wait_ge(cS.h, expA_mark(u))
                        if c == 0 and m >= 1:
                            e.wait_ge(cV.h, evacA_mark(m - 1))
                        lhsT = VA[:, 1040 * c + 65 * hA:1040 * c + 65 * (hA + 1)]
                        rhs = PTA[u % 2]
                        e.matmul(O_A[:, 0:512], lhsT, rhs[:, 0:512],
                                 start=(c == 0), stop=(c == 15))
                        li = e.matmul(O_A[:, 512:1024], lhsT, rhs[:, 512:1024],
                                      start=(c == 0), stop=(c == 15))
                        cT.inc(li, 1)
                        avA_mark[u] = cT.n
                        e.wait_ge(cS.h, expB_mark(u))
                        if c == 0 and m >= 1:
                            e.wait_ge(cV.h, evacB_mark(m - 1))
                        lhsT = VA[:, 1040 * c + 65 * (hA + 1):1040 * c + 65 * (hA + 2)]
                        rhs = PTB[u % 2]
                        e.matmul(O_B[:, 0:512], lhsT, rhs[:, 0:512],
                                 start=(c == 0), stop=(c == 15))
                        li = e.matmul(O_B[:, 512:1024], lhsT, rhs[:, 512:1024],
                                      start=(c == 0), stop=(c == 15))
                        cT.inc(li, 1)
                        avB_mark[u] = cT.n

                    emit_st(0)
                    for u in range(NU):
                        if u + 1 < NU:
                            emit_st(u + 1)
                        emit_av(u)
                assert cT.n == cT0 + 4 * NU

                @blk.scalar
                def _(e):
                    for u in range(NU):
                        hA, H, c = uhc(u)
                        e.wait_ge(cT.h, stA_mark[u])
                        if u >= 2:
                            e.wait_ge(cT.h, avA_mark[u - 2])
                        cS.inc(e.activation(PTA[u % 2], STA, AF.Exp,
                                            bias=VC["negMb"][:, hA:hA + 1],
                                            scale=1.0), 1)
                        e.wait_ge(cT.h, stB_mark[u])
                        if u >= 2:
                            e.wait_ge(cT.h, avB_mark[u - 2])
                        cS.inc(e.activation(PTB[u % 2], STB, AF.Exp,
                                            bias=VC["negMb"][:, hA + 1:hA + 2],
                                            scale=1.0), 1)
                assert cS.n == cS0 + 2 * NU

                @blk.vector
                def _(e):
                    prev_m = None
                    for m in range(NU // 16):
                        u = m * 16 + 15
                        hA, H, c = uhc(u)
                        ch = 2048 * (u // 32)
                        e.wait_ge(cT.h, avA_mark[u])
                        if m >= 1:
                            e.wait_ge(dDen.h, denAdma_mark(m - 1))
                        cV.inc(e.tensor_copy(SR[64:65, 0:1024],
                                             O_A[64:65, :]), 1)
                        assert cV.n == denA_mark(m)
                        dstA = Q16[0:64, ch + 1024 * H:ch + 1024 * (H + 1)]
                        cV.inc(e.tensor_copy(dstA, O_A[0:64, :]), 1)
                        assert cV.n == evacA_mark(m)
                        e.wait_ge(cT.h, avB_mark[u])
                        if m >= 1:
                            e.wait_ge(dDen.h, denBdma_mark(m - 1))
                        cV.inc(e.tensor_copy(SR[64:65, 1024:2048],
                                             O_B[64:65, :]), 1)
                        assert cV.n == denB_mark(m)
                        if prev_m is not None:
                            e.wait_ge(dOD.h, odma_mark(prev_m))
                        prev_m = m
                        cV.inc(e.tensor_copy(OST, O_B[0:64, :]), 1)
                        assert cV.n == evacB_mark(m)
                assert cV.n == cV0 + 4 * (NU // 16)

                @blk.gpsimd
                def _(e):
                    for m in range(NU // 16):
                        u = m * 16 + 15
                        hA = 2 * (u // 32)
                        H = (u // 16) % 2
                        ch = 2048 * (u // 32)
                        e.wait_ge(cV.h, denA_mark(m))
                        dDen.inc(e.dma_start(
                            out=DEN[hA:hA + 1, 1024 * H:1024 * (H + 1)],
                            in_=SR[64:65, 0:1024]), 16)
                        assert dDen.n == denAdma_mark(m)
                        e.wait_ge(cV.h, denB_mark(m))
                        dDen.inc(e.dma_start(
                            out=DEN[hA + 1:hA + 2, 1024 * H:1024 * (H + 1)],
                            in_=SR[64:65, 1024:2048]), 16)
                        assert dDen.n == denBdma_mark(m)
                        e.wait_ge(cV.h, evacB_mark(m))
                        dOD.inc(e.dma_start(
                            out=Q16[64:128, ch + 1024 * H:ch + 1024 * (H + 1)],
                            in_=OST), 16)
                        assert dOD.n == odma_mark(m)

        def phase_p35(b, layer):
            with nc.Block() as blk:
                cV0, cT0, cS0 = cV.n, cT.n, cS.n
                reload_mark = dX.n + 16

                @blk.scalar
                def _(e):
                    e.wait_ge(dDen.h, dDen.n)
                    with nc.allow_low_precision(
                            reason="bf16 softmax denominators; matmul operands "
                                   "are bf16 throughout"):
                        cS.inc(e.activation(PS_A[0:16, 0:2048], DEN,
                                            AF.Ln), 1)
                        e.wait_ge(cS.h, cS0 + 1)
                        cS.inc(e.activation(RDEN, PS_A[0:16, 0:2048],
                                            AF.Exp, scale=-1.0), 1)

                @blk.vector
                def _(e):
                    e.wait_ge(dOD.h, dOD.n)
                    for c in range(NCH):
                        e.wait_ge(cT.h, cT0 + c + 1)
                        ps = PS_A if c % 2 == 0 else PS_B
                        cV.inc(e.tensor_tensor(q_c(c), q_c(c), ps, ALU.mult), 1)
                    if layer == 1:
                        e.wait_ge(dX.h, reload_mark)
                        for c in range(NCH):
                            e.scalar_tensor_tensor(
                                xr_c(c), q_c(c), VC["hb2T"][:, c:c + 1], xr_c(c),
                                ALU.add, ALU.add)

                @blk.tensor
                def _(e):
                    e.wait_ge(cS.h, cS0 + 2)
                    for c in range(NCH):
                        if c >= 2:
                            e.wait_ge(cV.h, cV0 + c - 1)
                        ps = PS_A if c % 2 == 0 else PS_B
                        for j in range(4):
                            li = e.matmul(ps[:, 512 * j:512 * (j + 1)],
                                          SEL16S[:, 128 * c:128 * (c + 1)],
                                          RDEN[:, 512 * j:512 * (j + 1)],
                                          start=True, stop=True)
                        cT.inc(li, 1)

                @blk.sync
                def _(e):
                    e.wait_ge(cV.h, cV0 + NCH)
                    dX.inc(e.dma_start(
                        out=XR.rearrange("p (c t) -> p c t", t=S),
                        in_=xspill.rearrange("(c p) t -> p c t", p=128)), 16)
                    assert dX.n == reload_mark

                @blk.gpsimd
                def _(e):
                    e.wait_ge(dX.h, dX.n)

        def phase_p4_l0(b):
            with nc.Block() as blk:
                cT0, cP0, dP0 = cT.n, cP.n, dP.n

                @blk.gpsimd
                def _(e):
                    cP.inc(e.memset(POOLPAD, 0.0), 1)
                    cP.inc(e.memset(DISHA, 0.0), 1)
                    dP.inc(e.dma_start(out=DISHA[0:64, :],
                                       in_=dram["L0_ffn_disha"]), 16)

                @blk.tensor
                def _(e):
                    for dc in range(NCH):
                        for j in range(4):
                            li = e.matmul(PS_A[0:64, 512 * j:512 * (j + 1)], PSEL,
                                          xn_c(dc)[:, 512 * j:512 * (j + 1)],
                                          start=(dc == 0), stop=(dc == NCH - 1))
                    cT.inc(li, 1)

                @blk.scalar
                def _(e):
                    e.wait_ge(cT.h, cT0 + 1)
                    e.wait_ge(cP.h, cP0 + 1)
                    cS.inc(e.activation(POOLPAD[0:64, :], PS_A[0:64, :],
                                        AF.Copy), 1)

            with nc.Block() as blk:
                cT0, cV0 = cT.n, cV.n

                @blk.tensor
                def _(e):
                    e.wait_ge(dS.h, dS.n)
                    e.wait_ge(dP.h, dP.n)
                    for oc in range(NCH):
                        if oc >= 2:
                            e.wait_ge(cV.h, cV0 + oc - 1)
                        ps = PS_A if oc % 2 == 0 else PS_B
                        for dc in range(NCH):
                            lhsT = WTA[:, 1024 * dc + 128 * oc:
                                       1024 * dc + 128 * (oc + 1)]
                            for j in range(4):
                                e.matmul(ps[:, 512 * j:512 * (j + 1)], lhsT,
                                         xn_c(dc)[:, 512 * j:512 * (j + 1)],
                                         start=(dc == 0), stop=False)
                        lhsT = DISHA[:, 128 * oc:128 * (oc + 1)]
                        for j in range(4):
                            li = e.matmul(ps[:, 512 * j:512 * (j + 1)], lhsT,
                                          POOLPAD[:, 512 * j:512 * (j + 1)],
                                          start=False, stop=True)
                        cT.inc(li, 1)

                @blk.vector
                def _(e):
                    for oc in range(NCH):
                        e.wait_ge(cT.h, cT0 + oc + 1)
                        ps = PS_A if oc % 2 == 0 else PS_B
                        cV.inc(e.tensor_tensor(xr_c(oc), xr_c(oc), ps,
                                               ALU.add), 1)

            with nc.Block() as blk:
                @blk.sync
                def _(e):
                    load_wt(e, "L0_attn_Wo")
                wt_mark = dS.n
                cT0 = cT.n

                @blk.tensor
                def _(e):
                    e.wait_ge(dS.h, wt_mark)
                    if b == 0:
                        for oc in range(NCH):
                            for dc in range(NCH):
                                lhsT = WTA[:, 1024 * dc + 128 * oc:
                                           1024 * dc + 128 * (oc + 1)]
                                li = e.matmul(PS_A[:, 64 * oc:64 * oc + 1], lhsT,
                                              BV16[:, dc:dc + 1],
                                              start=(dc == 0), stop=(dc == NCH - 1))
                        cT.inc(li, 1)

                @blk.vector
                def _(e):
                    if b == 0:
                        e.wait_ge(cT.h, cT0 + 1)
                        ps = bass.AP(PS_A.tensor, PS_A.offset,
                                     [list(PS_A.ap[0]), [64, 8]])
                        e.tensor_tensor(VC["bo1T"], VC["boT"], ps, ALU.add)

            with nc.Block() as blk:
                cT0, cV0 = cT.n, cV.n

                @blk.tensor
                def _(e):
                    for oc in range(NCH):
                        if oc >= 2:
                            e.wait_ge(cV.h, cV0 + oc - 1)
                        ps = PS_A if oc % 2 == 0 else PS_B
                        for dc in range(NCH):
                            lhsT = WTA[:, 1024 * dc + 128 * oc:
                                       1024 * dc + 128 * (oc + 1)]
                            for j in range(4):
                                li = e.matmul(ps[:, 512 * j:512 * (j + 1)], lhsT,
                                              Q16[:, 2048 * dc + 512 * j:
                                                  2048 * dc + 512 * (j + 1)],
                                              start=(dc == 0), stop=(dc == NCH - 1))
                        cT.inc(li, 1)

                @blk.vector
                def _(e):
                    for oc in range(NCH):
                        e.wait_ge(cT.h, cT0 + oc + 1)
                        ps = PS_A if oc % 2 == 0 else PS_B
                        cV.inc(e.scalar_tensor_tensor(
                            xr_c(oc), ps, VC["bo1T"][:, oc:oc + 1], xr_c(oc),
                            ALU.add, ALU.add), 1)

        def phase_p4_l1(b):
            q1 = dram["L1_ffn_fc1_q"]
            s1 = dram["L1_ffn_fc1_scales"]
            q2 = dram["L1_ffn_fc2_q"]
            s2 = dram["L1_ffn_fc2_scales"]
            with nc.Block() as blk:
                cV0, cT0, cS0 = cV.n, cT.n, cS.n
                du_mark = dict(PF.get("du", {}))

                def _vpos_stt(f):
                    sbk, fi = f // 8, f % 8
                    if sbk == 0:
                        return 2 * fi
                    base = {1: 16, 2: 48, 3: 80}[sbk]
                    return base + 2 * fi + (16 if fi >= 2 else 0)

                def stt1_mark(f):
                    return cV0 + _vpos_stt(f) + 1

                def stt2_mark(f):
                    return cV0 + _vpos_stt(f) + 2

                def zadd_mark(sbk, oc, th):
                    zbase = {0: 20, 1: 52, 2: 84, 3: 112}[sbk]
                    return cV0 + zbase + 2 * oc + th + 1

                def ymm_mark(f, th):
                    sbk = f // 8
                    return cT0 + 32 * sbk + 2 * (f % 8) + th + 1

                def zmm_mark(sbk, oc, th):
                    return cT0 + 32 * sbk + 16 + 2 * oc + th + 1

                def silu_mark(f, th):
                    sbk = f // 8
                    return cS0 + 16 * sbk + 2 * (f % 8) + th + 1

                def stt_i_mark(i):
                    return stt1_mark(i // 2) if i % 2 == 0 else stt2_mark(i // 2)

                @blk.sync
                def _(e):
                    for i in range(4, 64, 2):
                        f = i // 2
                        e.wait_ge(cV.h, stt_i_mark(i - 3))
                        src = bass.AP(q1.tensor, q1.offset + 128 * f,
                                      [[4096, 128], [4096 * 128, 8], [1, 128]])
                        DU[i % 3].inc(e.dma_start(out=DQ_INT[i % 3], in_=src), 16)
                        ssrc = bass.AP(s1.tensor, s1.offset + 128 * f,
                                       [[0, 128], [4096, 8], [1, 128]])
                        DU[i % 3].inc(e.dma_start(out=DQ_SCL[i % 3], in_=ssrc), 16)
                        du_mark[i] = DU[i % 3].n

                @blk.gpsimd
                def _(e):
                    for i in [1] + list(range(3, 64, 2)):
                        f = i // 2
                        if i >= 3:
                            e.wait_ge(cV.h, stt_i_mark(i - 3))
                        DV[i % 3].inc(e.dma_start(
                            out=DQ_INT[i % 3], in_=q2[128 * f:128 * (f + 1), :]), 16)
                        ssrc2 = bass.AP(s2.tensor, s2.offset + 1024 * f,
                                        [[0, 128], [1, 1024]])
                        DV[i % 3].inc(e.dma_start(out=DQ_SCL[i % 3], in_=ssrc2), 16)
                        du_mark[i] = DV[i % 3].n

                @blk.vector
                def _(e):
                    def emit_stt_pair(f):
                        sbk = f // 8
                        i1, i2 = 2 * f, 2 * f + 1
                        e.wait_ge(DU[i1 % 3].h, du_mark[i1])
                        if f >= 2:
                            e.wait_ge(cT.h, ymm_mark(f - 2, 1))
                        w1f = WTA[:, 4096 * (f % 2):4096 * (f % 2) + 1024]
                        cV.inc(e.scalar_tensor_tensor(
                            w1f, DQ_INT[i1 % 3], -128.0, DQ_SCL[i1 % 3],
                            ALU.add, ALU.mult), 1)
                        e.wait_ge(DV[i2 % 3].h, du_mark[i2])
                        if sbk >= 2:
                            e.wait_ge(cT.h, zmm_mark(sbk - 2, 7, 1))
                        slot = (sbk % 2) * 8 + (f % 8)
                        w2sb = W2X[:, 1024 * slot:1024 * (slot + 1)]
                        cV.inc(e.scalar_tensor_tensor(
                            w2sb, DQ_INT[i2 % 3], -128.0, DQ_SCL[i2 % 3],
                            ALU.add, ALU.mult), 1)

                    def emit_zadds(sbk):
                        for oc in range(NCH):
                            for th in range(2):
                                e.wait_ge(cT.h, zmm_mark(sbk, oc, th))
                                dst = XR[:, 2048 * oc + 1024 * th:
                                         2048 * oc + 1024 * (th + 1)]
                                cV.inc(e.scalar_tensor_tensor(
                                    dst, PS_B[:, 1024 * th:1024 * (th + 1)],
                                    0.5, dst, ALU.mult, ALU.add), 1)

                    for sbk in range(4):
                        emit_stt_pair(8 * sbk)
                        emit_stt_pair(8 * sbk + 1)
                        if sbk >= 1:
                            emit_zadds(sbk - 1)
                        for fi in range(2, 8):
                            emit_stt_pair(8 * sbk + fi)
                    emit_zadds(3)
                assert cV.n == cV0 + 128

                @blk.tensor
                def _(e):
                    for sbk in range(4):
                        for fi in range(8):
                            f = sbk * 8 + fi
                            e.wait_ge(cV.h, stt1_mark(f))
                            w1f = WTA[:, 4096 * (f % 2):4096 * (f % 2) + 1024]
                            for th in range(2):
                                yi = 2 * f + th
                                if yi >= 2:
                                    e.wait_ge(cS.h, silu_mark((yi - 2) // 2,
                                                              (yi - 2) % 2))
                                ps = PS_A[:, 1024 * (yi % 2):1024 * (yi % 2 + 1)]
                                for hc in range(NCH):
                                    lhsT = w1f[:, 128 * hc:128 * (hc + 1)]
                                    e.matmul(ps[:, 0:512], lhsT,
                                             xn_c(hc)[:, 1024 * th:1024 * th + 512],
                                             start=(hc == 0), stop=(hc == NCH - 1))
                                    li = e.matmul(ps[:, 512:1024], lhsT,
                                                  xn_c(hc)[:, 1024 * th + 512:
                                                           1024 * (th + 1)],
                                                  start=(hc == 0), stop=(hc == NCH - 1))
                                cT.inc(li, 1)
                        for oc in range(NCH):
                            for th in range(2):
                                if oc >= 1:
                                    e.wait_ge(cV.h, zadd_mark(sbk, oc - 1, th))
                                elif sbk >= 1:
                                    e.wait_ge(cV.h, zadd_mark(sbk - 1, 7, th))
                                zs = PS_B[:, 1024 * th:1024 * (th + 1)]
                                for fi in range(8):
                                    f = sbk * 8 + fi
                                    e.wait_ge(cV.h, stt2_mark(f))
                                    e.wait_ge(cS.h, silu_mark(f, 1))
                                    slot = (sbk % 2) * 8 + fi
                                    lhsT = W2X[:, 1024 * slot + 128 * oc:
                                               1024 * slot + 128 * (oc + 1)]
                                    yv = Y8[:, 2048 * fi + 1024 * th:
                                            2048 * fi + 1024 * (th + 1)]
                                    for j in range(2):
                                        li = e.matmul(
                                            zs[:, 512 * j:512 * (j + 1)], lhsT,
                                            yv[:, 512 * j:512 * (j + 1)],
                                            start=(fi == 0), stop=(fi == 7))
                                cT.inc(li, 1)
                assert cT.n == cT0 + 4 * 32

                @blk.scalar
                def _(e):
                    for sbk in range(4):
                        for fi in range(8):
                            f = sbk * 8 + fi
                            for th in range(2):
                                yi = 2 * f + th
                                e.wait_ge(cT.h, ymm_mark(f, th))
                                if sbk >= 1 and th == 0:
                                    e.wait_ge(cT.h, zmm_mark(sbk - 1, 7, 1))
                                ps = PS_A[:, 1024 * (yi % 2):1024 * (yi % 2 + 1)]
                                ydst = Y8[:, 2048 * fi + 1024 * th:
                                          2048 * fi + 1024 * (th + 1)]
                                cS.inc(e.activation(ydst, ps, AF.Silu,
                                                    bias=VC["b1T"][:, f:f + 1],
                                                    scale=1.0), 1)
                assert cS.n == cS0 + 64

            with nc.Block() as blk:
                @blk.sync
                def _(e):
                    dO.inc(e.dma_start(
                        out=xout[b].rearrange("(c p) t -> p c t", p=128),
                        in_=XR.rearrange("p (c t) -> p c t", t=S)), 16)

        def phase_out_l0(b):
            with nc.Block() as blk:
                @blk.sync
                def _(e):
                    dO.inc(e.dma_start(
                        out=xout[b].rearrange("(c p) t -> p c t", p=128),
                        in_=XR.rearrange("p (c t) -> p c t", t=S)), 16)

        for b in range(NBC):
            for li, layer in enumerate(layers):
                phase_p1(b, layer, load_x=(li == 0))
                phase_p2(b, layer)
                if layer == 1:
                    phase_p25(b)
                phase_p3(b, layer)
                phase_p35(b, layer)
                if layer == 0:
                    phase_p4_l0(b)
                    if layers == (0,):
                        phase_out_l0(b)
                else:
                    phase_p4_l1(b)

        with nc.Block() as blk:
            @blk.sync
            def _(e):
                e.wait_ge(dO.h, dO.n)

    nc._megakernel_inputs = list(input_names)
    return nc


_PROG_CACHE = {}
_CONSTS = None


def _get_prog(layers):
    key = tuple(layers)
    if key not in _PROG_CACHE:
        _PROG_CACHE[key] = build_program(key)
    return _PROG_CACHE[key]


def _consts():
    global _CONSTS
    if _CONSTS is None:
        _CONSTS = host_constants()
    return _CONSTS


def _core_maps(prog, inputs, extra_per_core):
    cs, cosT, sinT = _consts()
    names = prog._megakernel_inputs
    base = {}
    for nm in names:
        if nm in inputs:
            base[nm] = np.ascontiguousarray(np.asarray(inputs[nm]))
    base["csmall"] = cs
    if "cosrope" in names:
        base["cosrope"] = cosT
        base["sinrope"] = sinT
    maps = []
    for i in range(8):
        m = dict(base)
        m.update(extra_per_core[i])
        maps.append(m)
    return maps


def kernel(**inputs):
    x = np.asarray(inputs["L0_attn_x"])
    if FUSED:
        prog = _get_prog((0, 1))
        extra = [{"x": np.ascontiguousarray(x[2 * i:2 * i + 2].transpose(0, 2, 1))}
                 for i in range(8)]
        res = run_bass_kernel_spmd(prog, _core_maps(prog, inputs, extra),
                                   core_ids=list(range(8)))
        outs = [res.results[i]["out"].transpose(0, 2, 1) for i in range(8)]
        return np.ascontiguousarray(np.concatenate(outs, axis=0),
                                    dtype=np.float32)
    prog0 = _get_prog((0,))
    extra = [{"x": np.ascontiguousarray(x[2 * i:2 * i + 2].transpose(0, 2, 1))}
             for i in range(8)]
    res0 = run_bass_kernel_spmd(prog0, _core_maps(prog0, inputs, extra),
                                core_ids=list(range(8)))
    prog1 = _get_prog((1,))
    extra1 = [{"x1": np.ascontiguousarray(res0.results[i]["x1"])}
              for i in range(8)]
    res1 = run_bass_kernel_spmd(prog1, _core_maps(prog1, inputs, extra1),
                                core_ids=list(range(8)))
    outs = [res1.results[i]["out"].transpose(0, 2, 1) for i in range(8)]
    return np.ascontiguousarray(np.concatenate(outs, axis=0), dtype=np.float32)
